# version 07
# speedup vs baseline: 1.2060x; 1.0123x over previous
import numpy as np
from contextlib import ExitStack

import concourse.bass as bass
import concourse.mybir as mybir
from concourse.bass_utils import run_bass_kernel_spmd

F32 = mybir.dt.float32
BF16 = mybir.dt.bfloat16
OP = mybir.AluOpType
AF = mybir.ActivationFunctionType
AX = mybir.AxisListType

B, S, DM, DI, H, HD = 2, 8192, 1024, 2048, 32, 64
T = 2048
NSB, SBL, J, L = 4, 512, 64, 8
NCH = NSB * J
KD, KI = DM // 128, DI // 128
NE, ES = 64, 128
XW = T + 4
LN10K = float(np.log(10000.0))

_r = np.arange(DI)
PERM = (_r % H) * HD + (_r // H)
_s = np.arange(T)
_sb, _rr = _s // SBL, _s % SBL
TOKPERM = _sb * SBL + (_rr % J) * L + (_rr // J)


class Sem:
    def __init__(self, ctx, nc, name):
        self.h = ctx.enter_context(nc.semaphore(name))
        self.v = 0

    def inc(self, n=1):
        self.v += n
        return self.v


class KState:
    pass


class DmaChain:

    def __init__(self, ctx, nc, name, n=4):
        self.sems = [Sem(ctx, nc, f"{name}{i}") for i in range(n)]
        self.n = n
        self.i = 0

    def issue(self, e, out, in_):
        j = self.i % self.n
        s = self.sems[j]
        if self.i >= self.n:
            e.wait_ge(s.h, s.v)
        e.dma_start(out=out, in_=in_).then_inc(s.h, 16)
        s.inc(16)
        self.i += 1
        return (s.h, s.v)

    def token(self, k):
        j = k % self.n
        return (self.sems[j].h, 16 * (k // self.n + 1))

    def wait_all(self, e):
        for s in self.sems:
            e.wait_ge(s.h, s.v)


def wait_tokens(e, tokens):
    best = {}
    for sem, val in tokens:
        if sem not in best or val > best[sem]:
            best[sem] = val
    for sem, val in best.items():
        e.wait_ge(sem, val)


def mkdep(e, sem):
    def dep(inst):
        inst.then_inc(sem.h, 1)
        e.wait_ge(sem.h, sem.inc())
    return dep


def bcast_g(tile, off, ng, ncols, row_len):
    return bass.AP(tile, off, [[row_len, 128], [0, ng], [1, ncols]])


def build_program(taps=(), phases=99):
    taps = set(taps)
    nc = bass.Bass("TRN2", target_bir_lowering=False)
    st = KState()
    st.nc = nc
    st.taps = taps
    st.tap_handles = {}

    def din(name, shape, dt=F32):
        return nc.dram_tensor(name, shape, dt, kind="ExternalInput").ap()

    st.x_fm = din("x_fm", [DM, T])
    st.x_halo = din("x_halo", [DM, 4])
    st.wproj = din("wproj", [DM, 4 * DI])
    st.bproj = din("bproj", [128, 64])
    st.wq = din("wq", [DI, 2 * DI])
    st.wsm = din("wsm", [DI, 384])
    st.wout = din("wout", [DI, DM])
    st.wsel = din("wsel", [DM, NE])
    st.wkeys = din("wkeys", [DM, NE * ES])
    st.wvals = din("wvals", [NE * ES, DM])
    st.convw = din("convw", [128, 4 * KI])
    st.convb = din("convb", [128, KI])
    st.d_skip = din("d_skip", [128, KI])
    st.gnw = din("gnw", [128, KI])
    st.gnb = din("gnb", [128, KI])
    st.rmsw = din("rmsw", [128, KD])
    st.bsm = din("bsm", [128, 8])
    st.dtc = din("dtc", [128, 1])
    st.meta = din("meta", [128, 12])

    def scratch(name, shape, dt):
        return nc.dram_tensor(name, shape, dt, kind="Internal").ap()

    st.z_sp = scratch("z_sp", [DI, T], BF16)
    st.k_sp = scratch("k_sp", [2 * DI, T], BF16)
    st.u_sp = scratch("u_sp", [DI, T], BF16)
    st.qh = scratch("qh", [2 * NSB * L * 128, KI * J], BF16)
    st.y0_sp = scratch("y0_sp", [NSB * L * 128, KI * J], BF16)
    st.fre_sp = scratch("fre_sp", [NSB * L * 128, KI * J], BF16)
    st.fim_sp = scratch("fim_sp", [NSB * L * 128, KI * J], BF16)
    st.sm_sp = scratch("sm_sp", [6 * NSB * 128, KI * J], BF16)
    st.yt_sp = scratch("yt_sp", [DI, T], BF16)
    st.cc1_in = scratch("cc1_in", [128, 96], F32)
    st.cc1_out = scratch("cc1_out", [512, 96], F32)
    st.cc2_in = scratch("cc2_in", [32, 2], F32)
    st.cc2_out = scratch("cc2_out", [128, 2], F32)

    st.out_p = nc.dram_tensor("out_p", [DM, T], F32, kind="ExternalOutput").ap()

    def tapout(name, shape, dt):
        if name in taps:
            h = nc.dram_tensor("tap_" + name, shape, dt, kind="ExternalOutput").ap()
            st.tap_handles[name] = h
            return h
        return None

    st.tapout = tapout
    st.groups = [[0, 1, 2, 3], [4, 5, 6, 7]]

    with ExitStack() as ctx:
        st.ctx = ctx

        def sbuf(name, shape, dt):
            return ctx.enter_context(nc.sbuf_tensor(name, shape, dt))
        st.sbuf = sbuf

        def sbufr(c, name, shape, dt):
            return c.enter_context(nc.sbuf_tensor(name, shape, dt, side="right"))
        st.sbufr = sbufr

        st.ones_col = sbuf("ones_col", [128, 1], BF16)
        st.ones_row = sbuf("ones_row", [1, 128], BF16)
        st.ident = sbuf("ident", [128, 128], BF16)
        st.ind32 = sbuf("ind32", [128, 32], BF16)
        st.rope_t = sbuf("rope_t", [128, 1], F32)
        st.spdtc = sbuf("spdtc", [128, 1], F32)
        st.convw_t = sbuf("convw_t", [128, 4 * KI], F32)
        st.convb_t = sbuf("convb_t", [128, KI], F32)
        st.dskip_t = sbuf("dskip_t", [128, KI], F32)
        st.gnw_t = sbuf("gnw_t", [128, KI], F32)
        st.gnb_t = sbuf("gnb_t", [128, KI], F32)
        st.rmsw_t = sbuf("rmsw_t", [128, KD], F32)
        st.bsm_t = sbuf("bsm_t", [128, 8], F32)
        st.dtc_t = sbuf("dtc_t", [128, 1], F32)
        st.meta_t = sbuf("meta_t", [128, 12], F32)
        st.bproj_t = sbuf("bproj_t", [128, 64], F32)
        st.gn_scale = sbuf("gn_scale", [128, 2 * KI], F32)
        st.io1 = sbuf("io1", [128, 128], F32)
        st.io2 = sbuf("io2", [128, 16], F32)
        st.u_ctx = ExitStack()
        for _nm in ["sBre_t", "sBim_t", "sCre_t", "sCim_t", "mbeta_t",
                    "bg_t", "rre_t", "rim_t"]:
            setattr(st, _nm, st.u_ctx.enter_context(
                nc.sbuf_tensor(_nm, [128, T], BF16)))

        st.ps = [ctx.enter_context(nc.psum_tensor("ps%d" % i, [128, 512], F32))
                 for i in range(8)]

        for name in ["tMM", "vC", "sC", "gC", "gD", "bar"]:
            setattr(st, name, Sem(ctx, nc, name))
        st.chIn = DmaChain(ctx, nc, "chIn", 4)
        st.chW = DmaChain(ctx, nc, "chW", 4)
        st.chSp = DmaChain(ctx, nc, "chSp", 4)
        st.chRe = DmaChain(ctx, nc, "chRe", 4)
        st.chOut = DmaChain(ctx, nc, "chOut", 2)
        st.chG = DmaChain(ctx, nc, "chG", 4)
        st.nbar = 0

        p0_setup(st)
        barrier(st)
        if phases >= 1:
            p1_xn(st)
            barrier(st)
            st.p1_ctx.close()
        if phases >= 2:
            p2_inproj(st)
            barrier(st)
            st.p2_ctx.close()
            st.xn_ctx.close()
        if phases >= 3:
            p3_smalls(st)
            barrier(st)
            st.p3_ctx.close()
        if phases >= 4:
            p35_qmm(st)
        if phases >= 5:
            p4_scan(st)
            barrier(st)
            st.p4_ctx.close()
            st.p35_ctx.close()
            st.u_ctx.close()
        elif phases == 4:
            barrier(st)
            st.p35_ctx.close()
        if phases >= 6:
            p5_stitch(st)
            barrier(st)
            st.p5_ctx.close()
        if phases >= 7:
            p6_gn(st)
            p7_outw_ln1(st)
            barrier(st)
        if phases >= 8:
            p8_ffn(st)
        final_waits(st)

    return nc, st


def barrier(st):
    nc = st.nc
    st.nbar += 1
    target = 5 * st.nbar
    with nc.Block() as blk:
        for engname in ["sync", "tensor", "vector", "scalar", "gpsimd"]:
            def body(e, _eng=engname, _t=target):
                if _eng == "sync":
                    for ch in [st.chIn, st.chW, st.chSp, st.chRe, st.chOut]:
                        ch.wait_all(e)
                e.sem_inc(st.bar.h, 1)
                e.wait_ge(st.bar.h, _t)
            getattr(blk, engname)(body)


def p0_setup(st):
    nc = st.nc
    io1, io2 = st.io1, st.io2
    with nc.Block() as blk:
        p0_tokens = []

        @blk.sync
        def _(e):
            for t, src in [(st.convw_t, st.convw), (st.convb_t, st.convb),
                           (st.dskip_t, st.d_skip), (st.gnw_t, st.gnw),
                           (st.gnb_t, st.gnb), (st.rmsw_t, st.rmsw),
                           (st.bsm_t, st.bsm), (st.dtc_t, st.dtc),
                           (st.meta_t, st.meta), (st.bproj_t, st.bproj)]:
                p0_tokens.append(st.chIn.issue(e, t[:, :], src[:, :]))

        @blk.gpsimd
        def _(e):
            e.iota(io1[:, :], [[1, 128]], base=0, channel_multiplier=-1,
                   allow_small_or_imprecise_dtypes=True)
            e.iota(io2[:, 0:1], [[0, 1]], base=0, channel_multiplier=1,
                   allow_small_or_imprecise_dtypes=True).then_inc(st.gC.h, 1)
            st.gC.inc()
        g_done = st.gC.v

        @blk.vector
        def _(e):
            dep = mkdep(e, st.vC)
            e.wait_ge(st.gC.h, g_done)
            acc = io1[:, 64:96]
            dep(e.tensor_scalar(acc, io1[:, 0:32], 0.0, None, OP.is_equal))
            for off in (-32.0, -64.0, -96.0):
                dep(e.tensor_scalar(io1[:, 96:128], io1[:, 0:32], off, None, OP.is_equal))
                dep(e.tensor_add(acc, acc, io1[:, 96:128]))
            dep(e.tensor_copy(st.ind32[:, :], acc))
            e.sem_inc(st.vC.h, 1)
            st.vC.inc()
        ind_done = st.vC.v

        @blk.gpsimd
        def _(e):
            e.wait_ge(st.vC.h, ind_done)
            e.iota(io1[:, :], [[1, 128]], base=0, channel_multiplier=-1,
                   allow_small_or_imprecise_dtypes=True).then_inc(st.gC.h, 1)
            st.gC.inc()
        g2_done = st.gC.v

        @blk.vector
        def _(e):
            dep = mkdep(e, st.vC)
            e.wait_ge(st.gC.h, g2_done)
            dep(e.tensor_scalar(io1[:, :], io1[:, :], 0.0, None, OP.is_equal))
            dep(e.tensor_copy(st.ident[:, :], io1[:, :]))
            e.memset(st.ones_col[:, :], 1.0)
            e.memset(st.ones_row[:, :], 1.0)
            dep(e.tensor_scalar(io2[:, 2:3], io2[:, 0:1], 32.0, None, OP.is_ge))
            dep(e.tensor_scalar(io2[:, 3:4], io2[:, 0:1], 64.0, None, OP.is_ge))
            dep(e.tensor_add(io2[:, 2:3], io2[:, 2:3], io2[:, 3:4]))
            dep(e.tensor_scalar(io2[:, 3:4], io2[:, 0:1], 96.0, None, OP.is_ge))
            dep(e.tensor_add(io2[:, 2:3], io2[:, 2:3], io2[:, 3:4]))
            dep(e.scalar_tensor_tensor(out=io2[:, 1:2], in0=io2[:, 2:3], scalar=-32.0,
                                   in1=io2[:, 0:1], op0=OP.mult, op1=OP.add))
            e.sem_inc(st.vC.h, 1)
            st.vC.inc()
        v_done = st.vC.v

        @blk.scalar
        def _(e):
            dep = mkdep(e, st.sC)
            e.wait_ge(st.vC.h, v_done)
            e.activation(out=st.rope_t[:, :], in_=io2[:, 1:2], func=AF.Exp,
                         scale=-LN10K / 32.0)
            wait_tokens(e, p0_tokens)
            dep(e.activation(out=io2[:, 4:5], in_=st.dtc_t[:, :], func=AF.Abs))
            dep(e.activation(out=io2[:, 4:5], in_=io2[:, 4:5], func=AF.Exp, scale=-1.0))
            dep(e.activation(out=io2[:, 4:5], in_=io2[:, 4:5], func=AF.Ln, bias=1.0))
            e.sem_inc(st.sC.h, 1)
            st.sC.inc()
        s_done = st.sC.v

        @blk.vector
        def _(e):
            dep = mkdep(e, st.vC)
            e.wait_ge(st.sC.h, s_done)
            dep(e.tensor_scalar_max(io2[:, 3:4], st.dtc_t[:, :], 0.0))
            dep(e.tensor_add(st.spdtc[:, :], io2[:, 3:4], io2[:, 4:5]))


def p1_xn(st):
    nc = st.nc
    st.xn_ctx = ExitStack()
    st.xnbf = st.sbufr(st.xn_ctx, "xnbf", [128, KD * XW], BF16)
    p1c = ExitStack()
    xch = st.sbufr(p1c, "xch", [128, KD * XW], F32)
    x2 = st.sbufr(p1c, "x2", [128, 2 * XW], BF16)
    strow = st.sbufr(p1c, "strow", [1, XW], F32)
    invbf = st.sbufr(p1c, "invbf", [1, XW], BF16)
    st.xch = xch
    st.p1_ctx = p1c

    TT = [512, 512, 512, 512, 4]
    TOFF = [0, 512, 1024, 1536, 2048]
    ps = st.ps

    sq_base = st.sC.v
    mm_base = st.tMM.v
    sq_val = lambda k: sq_base + k + 1
    mm_val = lambda k: mm_base + k + 1
    p1_tokens = []

    with nc.Block() as blk:
        @blk.sync
        def _(e):
            for k in range(KD):
                p1_tokens.append(st.chIn.issue(e, xch[:, k * XW:k * XW + T],
                                               st.x_fm[k * 128:(k + 1) * 128, :]))
                p1_tokens.append(st.chIn.issue(e, xch[:, k * XW + T:(k + 1) * XW],
                                               st.x_halo[k * 128:(k + 1) * 128, :]))

        @blk.scalar
        def _(e):
            wait_tokens(e, p1_tokens)
            for k in range(KD):
                if k >= 2:
                    e.wait_ge(st.tMM.h, mm_val(k - 2))
                e.activation(out=x2[:, (k % 2) * XW:(k % 2 + 1) * XW],
                             in_=xch[:, k * XW:(k + 1) * XW],
                             func=AF.Square).then_inc(st.sC.h, 1)
                st.sC.inc()

        @blk.tensor
        def _(e):
            for k in range(KD):
                e.wait_ge(st.sC.h, sq_val(k))
                xoff = (k % 2) * XW
                for ti in range(5):
                    inst = e.matmul(ps[ti][0:1, 0:TT[ti]], st.ones_col[:, :],
                                    x2[:, xoff + TOFF[ti]:xoff + TOFF[ti] + TT[ti]],
                                    start=(k == 0), stop=(k == KD - 1))
                inst.then_inc(st.tMM.h, 1)
                st.tMM.inc()

        @blk.vector
        def _(e):
            dep = mkdep(e, st.vC)
            e.wait_ge(st.tMM.h, mm_val(KD - 1))
            for ti in range(5):
                dep(e.tensor_copy(strow[0:1, TOFF[ti]:TOFF[ti] + TT[ti]],
                              ps[ti][0:1, 0:TT[ti]]))
            e.tensor_scalar(strow[0:1, :], strow[0:1, :], 1.0 / DM, 1e-6,
                            OP.mult, OP.add).then_inc(st.vC.h, 1)
            st.vC.inc()
        v1_evt = st.vC.v

        @blk.scalar
        def _(e):
            e.wait_ge(st.vC.h, v1_evt)
            e.activation(out=strow[0:1, :], in_=strow[0:1, :],
                         func=AF.Sqrt).then_inc(st.sC.h, 1)
            st.sC.inc()
        s2_evt = st.sC.v

        @blk.vector
        def _(e):
            dep = mkdep(e, st.vC)
            e.wait_ge(st.sC.h, s2_evt)
            dep(e.reciprocal(strow[0:1, :], strow[0:1, :]))
            e.tensor_copy(invbf[0:1, :], strow[0:1, :]).then_inc(st.vC.h, 1)
            st.vC.inc()
        inv_evt = st.vC.v

        @blk.tensor
        def _(e):
            e.wait_ge(st.vC.h, inv_evt)
            for ti in range(5):
                inst = e.matmul(ps[ti][:, 0:TT[ti]], st.ones_row[:, :],
                                invbf[0:1, TOFF[ti]:TOFF[ti] + TT[ti]],
                                start=True, stop=True)
            inst.then_inc(st.tMM.h, 1)
            st.tMM.inc()
        bc_evt = st.tMM.v

        @blk.vector
        def _(e):
            e.wait_ge(st.tMM.h, bc_evt)
            for k in range(KD):
                for ti in range(5):
                    e.scalar_tensor_tensor(
                        out=st.xnbf[:, k * XW + TOFF[ti]:k * XW + TOFF[ti] + TT[ti]],
                        in0=xch[:, k * XW + TOFF[ti]:k * XW + TOFF[ti] + TT[ti]],
                        scalar=st.rmsw_t[:, k:k + 1],
                        in1=ps[ti][:, 0:TT[ti]],
                        op0=OP.mult, op1=OP.mult)
            e.sem_inc(st.vC.h, 1)
            st.vC.inc()
    st.p1_xn_done = st.vC.v

    tp = st.tapout("xn", [128, KD * XW], BF16)
    if tp is not None:
        with nc.Block() as blk:
            @blk.sync
            def _(e):
                e.wait_ge(st.vC.h, st.p1_xn_done)
                st.chOut.issue(e, tp[:, :], st.xnbf[:, :])


def p2_inproj(st):
    nc = st.nc
    st.ubuf = st.u_ctx.enter_context(nc.sbuf_tensor("ubuf", [128, KI * T], BF16))
    p2c = ExitStack()
    wbuf = st.sbufr(p2c, "wbufP2", [128, KD * 2048], BF16)
    ebuf = st.sbufr(p2c, "ebufP2", [128, 2 * T], BF16)
    vbuf = st.sbufr(p2c, "vbufP2", [128, 2 * (T + 4)], BF16)
    cv1 = st.sbufr(p2c, "cv1P2", [128, 2 * T], F32)
    st.p2_ctx = p2c
    ps = st.ps

    TT = [512, 512, 512, 512, 4]
    TOFF = [0, 512, 1024, 1536, 2048]
    MQ = 16

    def ntile(m):
        return 5 if m >= 48 else 4

    spill_base = st.chSp.i
    tMM_base = st.tMM.v
    _mm_pref = [0]
    for m in range(64):
        _mm_pref.append(_mm_pref[-1] + ntile(m))
    mm_val = lambda m, ti: tMM_base + _mm_pref[m] + ti + 1
    sC_base = st.sC.v
    def _silus_before_evac(m):
        return max(0, m - 49)

    def ev_val(m, ti):
        return sC_base + _mm_pref[m] + _silus_before_evac(m) + ti + 1

    def ev_m_done(m):
        return sC_base + _mm_pref[m + 1] + _silus_before_evac(m)

    def silu_val(g):
        if g < KI - 1:
            return sC_base + _mm_pref[50 + g] + g + 1
        return sC_base + _mm_pref[64] + KI
    zk_tok = lambda m: st.chSp.token(spill_base + m)
    vC_base = st.vC.v
    cv_val = lambda g: vC_base + 6 * (g + 1)

    with nc.Block() as blk:
        wtok = [[] for _ in range(4)]

        @blk.gpsimd
        def _(e):
            for q in range(4):
                if q >= 1:
                    e.wait_ge(st.tMM.h, mm_val((q - 1) * MQ + MQ - 1, ntile((q - 1) * MQ + MQ - 1) - 1))
                for k in range(KD):
                    wtok[q].append(st.chW.issue(
                        e, wbuf[:, k * 2048:(k + 1) * 2048],
                        st.wproj[k * 128:(k + 1) * 128, q * 2048:(q + 1) * 2048]))

        _g2mt = []
        for m in range(64):
            for ti in range(ntile(m)):
                _g2mt.append((m, ti))
        gidx_of = {mt: i for i, mt in enumerate(_g2mt)}

        @blk.tensor
        def _(e):
            for q in range(4):
                wait_tokens(e, wtok[q])
                woff = 0
                for mi in range(MQ):
                    m = q * MQ + mi
                    for ti in range(ntile(m)):
                        gi = gidx_of[(m, ti)]
                        if gi >= 8:
                            pm, pt = _g2mt[gi - 8]
                            e.wait_ge(st.sC.h, ev_val(pm, pt))
                        for k in range(KD):
                            inst = e.matmul(ps[gi % 8][:, 0:TT[ti]],
                                     wbuf[:, woff + k * 2048 + mi * 128:woff + k * 2048 + mi * 128 + 128],
                                     st.xnbf[:, k * XW + TOFF[ti]:k * XW + TOFF[ti] + TT[ti]],
                                     start=(k == 0), stop=(k == KD - 1))
                        inst.then_inc(st.tMM.h, 1)
                        st.tMM.inc()

        @blk.scalar
        def _(e):
            def emit_silu(g):
                e.wait_ge(st.vC.h, cv_val(g))
                dst = st.ubuf[:, g * T:(g + 1) * T]
                dst = dst.rearrange("p (sb t c) -> p sb c t", sb=NSB, t=L, c=J)
                e.activation(out=dst, in_=cv1[:, (g % 2) * T:(g % 2 + 1) * T],
                             func=AF.Silu, bias=st.convb_t[:, g:g + 1],
                             scale=1.0).then_inc(st.sC.h, 1)
                st.sC.inc()

            for m in range(64):
                is_v = m >= 48
                half = m % 2
                g = m - 48
                if not is_v and m >= 2:
                    sem, val = zk_tok(m - 2)
                    e.wait_ge(sem, val)
                if is_v and g >= 2:
                    e.wait_ge(st.vC.h, cv_val(g - 2))
                for ti in range(ntile(m)):
                    gi = gidx_of[(m, ti)]
                    e.wait_ge(st.tMM.h, mm_val(m, ti))
                    if not is_v:
                        dst = ebuf[:, half * T + TOFF[ti]:half * T + TOFF[ti] + TT[ti]]
                    else:
                        base = half * (T + 4)
                        if ti < 4:
                            dst = vbuf[:, base + 4 + TOFF[ti]:base + 4 + TOFF[ti] + 512]
                            dst = dst.rearrange("p (c t) -> p t c", c=J, t=L)
                        else:
                            dst = vbuf[:, base:base + 4]
                    e.activation(out=dst, in_=ps[gi % 8][:, 0:TT[ti]], func=AF.Identity,
                                 bias=st.bproj_t[:, m:m + 1],
                                 scale=1.0).then_inc(st.sC.h, 1)
                    st.sC.inc()
                if is_v and g >= 1:
                    emit_silu(g - 1)
            emit_silu(KI - 1)

        @blk.sync
        def _(e):
            for m in range(48):
                e.wait_ge(st.sC.h, ev_m_done(m))
                half = m % 2
                if m < 16:
                    dst = st.z_sp[m * 128:(m + 1) * 128, :]
                elif m < 32:
                    dst = st.k_sp[(m - 16) * 128:(m - 15) * 128, :]
                else:
                    dst = st.k_sp[DI + (m - 32) * 128:DI + (m - 31) * 128, :]
                st.chSp.issue(e, dst, ebuf[:, half * T:(half + 1) * T])

        @blk.vector
        def _(e):
            dep = mkdep(e, st.vC)
            for g in range(KI):
                m = 48 + g
                e.wait_ge(st.sC.h, ev_m_done(m))
                if g >= 2:
                    e.wait_ge(st.sC.h, silu_val(g - 2))
                base = (m % 2) * (T + 4)
                dep(e.tensor_scalar(vbuf[:, base + 1:base + 4], vbuf[:, base + 1:base + 4],
                                st.meta_t[:, 1:2], None, OP.mult))
                co = (g % 2) * T
                dep(e.tensor_scalar(cv1[:, co:co + T], vbuf[:, base + 4:base + 4 + T],
                                st.convw_t[:, g * 4 + 3:g * 4 + 4], None, OP.mult))
                for tap, col in [(2, 3), (1, 2), (0, 1)]:
                    dep(e.scalar_tensor_tensor(out=cv1[:, co:co + T],
                                           in0=vbuf[:, base + col:base + col + T],
                                           scalar=st.convw_t[:, g * 4 + tap:g * 4 + tap + 1],
                                           in1=cv1[:, co:co + T], op0=OP.mult, op1=OP.add))
                e.sem_inc(st.vC.h, 1)
                st.vC.inc()

        @blk.sync
        def _(e):
            for g in range(KI):
                e.wait_ge(st.sC.h, silu_val(g))
                st.chSp.issue(e, st.u_sp[g * 128:(g + 1) * 128, :],
                              st.ubuf[:, g * T:(g + 1) * T])
    st.p2_u_all = silu_val(KI - 1)

    tp = st.tapout("u", [DI, T], BF16)
    if tp is not None:
        with nc.Block() as blk:
            @blk.sync
            def _(e):
                e.wait_ge(st.sC.h, st.p2_u_all)
                for g in range(KI):
                    st.chOut.issue(e, tp[g * 128:(g + 1) * 128, :],
                                   st.ubuf[:, g * T:(g + 1) * T])


def p3_smalls(st):
    nc = st.nc
    p3c = ExitStack()
    wsmb = st.sbufr(p3c, "wsmb", [128, KI * 384], BF16)
    sm3 = st.sbufr(p3c, "sm3", [128, 3 * T], F32)
    hq = st.sbufr(p3c, "hq", [128, 2 * T], F32)
    hs = st.sbufr(p3c, "hs", [128, 6 * T], F32)
    posp = st.sbufr(p3c, "posp", [128, T], F32)
    ps = st.ps

    def HQ(i):
        return hq[:, i * T:(i + 1) * T]

    def HS(i):
        return hs[:, i * T:(i + 1) * T]

    QPOS = {"alpha": (0, 0), "om1": (0, 1), "om2": (0, 2), "seldt": (0, 3),
            "beta": (1, 0), "rg": (1, 1), "ug": (1, 2), "sBre": (1, 3),
            "sBim": (2, 0), "sCre": (2, 1), "sCim": (2, 2)}

    wtok = []
    mm_base = st.tMM.v
    mm_val = lambda mc, tt: mm_base + mc * 4 + tt + 1
    v_base = st.vC.v
    ev_val = lambda mc, tt: v_base + mc * 4 + tt + 1

    with nc.Block() as blk:
        @blk.gpsimd
        def _(e):
            for k in range(KI):
                wtok.append(st.chW.issue(
                    e, wsmb[:, k * 384:(k + 1) * 384],
                    st.wsm[k * 128:(k + 1) * 128, :]))
            e.iota(posp[:, :], [[512, 4], [1, 8], [8, 64]], base=0,
                   channel_multiplier=0,
                   allow_small_or_imprecise_dtypes=True).then_inc(st.gC.h, 1)
            st.gC.inc()
        pos_done = st.gC.v

        @blk.tensor
        def _(e):
            wait_tokens(e, wtok)
            for mc in range(3):
                for tt in range(4):
                    if mc >= 1:
                        e.wait_ge(st.vC.h, ev_val(mc - 1, tt))
                    for k in range(KI):
                        inst = e.matmul(
                            ps[tt][:, :],
                            wsmb[:, k * 384 + mc * 128:k * 384 + mc * 128 + 128],
                            st.ubuf[:, k * T + tt * 512:k * T + (tt + 1) * 512],
                            start=(k == 0), stop=(k == KI - 1))
                    inst.then_inc(st.tMM.h, 1)
                    st.tMM.inc()

        @blk.vector
        def _(e):
            for mc in range(3):
                for tt in range(4):
                    e.wait_ge(st.tMM.h, mm_val(mc, tt))
                    e.tensor_copy(sm3[:, mc * T + tt * 512:mc * T + (tt + 1) * 512],
                                  ps[tt][:, :]).then_inc(st.vC.h, 1)
                    st.vC.inc()

        retok = {}

        @blk.sync
        def _(e):
            for name, dtile, dcol in [("alpha", hs, 0), ("om1", hs, 2),
                                      ("om2", hq, 0), ("seldt", hq, 1)]:
                mc, rg_ = QPOS[name]
                e.wait_ge(st.vC.h, ev_val(mc, 3))
                toks = []
                for blk4 in range(4):
                    toks.append(st.chRe.issue(
                        e, dtile[32 * blk4:32 * blk4 + 32, dcol * T:(dcol + 1) * T],
                        sm3[32 * rg_:32 * rg_ + 32, mc * T:(mc + 1) * T]))
                retok[name] = toks

        @blk.gpsimd
        def _(e):
            for name, dstt in [("sBre", st.sBre_t), ("sBim", st.sBim_t),
                               ("sCre", st.sCre_t), ("sCim", st.sCim_t)]:
                mc, rg_ = QPOS[name]
                e.wait_ge(st.vC.h, ev_val(mc, 3))
                toks = []
                for blk4 in range(4):
                    toks.append(st.chW.issue(
                        e, dstt[32 * blk4:32 * blk4 + 32, :],
                        sm3[32 * rg_:32 * rg_ + 32, mc * T:(mc + 1) * T]))
                retok[name] = toks
            st.sel_tok = retok["sBre"] + retok["sBim"] + retok["sCre"] + retok["sCim"]

    with nc.Block() as blk:
        @blk.vector
        def _(e):
            dep = mkdep(e, st.vC)
            for name in ["alpha", "om1", "om2", "seldt"]:
                wait_tokens(e, retok[name])
            e.wait_ge(st.gC.h, pos_done)
            dep(e.tensor_add(st.io2[:, 5:6], st.bsm_t[:, 1:2], st.bsm_t[:, 2:3]))
            dep(e.tensor_add(HS(2), HS(2), HQ(0)))
            dep(e.tensor_scalar(HS(2), HS(2), st.io2[:, 5:6], None, OP.add))
            dep(e.tensor_scalar(HS(3), posp[:, :], st.meta_t[:, 0:1], None, OP.add))
            dep(e.scalar_tensor_tensor(out=HS(2), in0=HS(3), scalar=st.rope_t[:, 0:1],
                                       in1=HS(2), op0=OP.mult, op1=OP.add))
            e.sem_inc(st.vC.h, 1)
            st.vC.inc()
        om_done = st.vC.v

        @blk.scalar
        def _(e):
            deps = mkdep(e, st.sC)
            wait_tokens(e, retok["alpha"])
            deps(e.activation(out=HS(0), in_=HS(0), func=AF.Identity,
                              bias=st.bsm_t[:, 0:1], scale=1.0))
            deps(e.activation(out=HS(1), in_=HS(0), func=AF.Abs))
            deps(e.activation(out=HS(1), in_=HS(1), func=AF.Exp, scale=-1.0))
            deps(e.activation(out=HS(1), in_=HS(1), func=AF.Ln, bias=1.0))
            e.wait_ge(st.vC.h, om_done)
            deps(e.activation(out=HS(3), in_=HS(2), func=AF.Abs))
            e.sem_inc(st.sC.h, 1)
            st.sC.inc()
        s1_done = st.sC.v

        @blk.vector
        def _(e):
            dep = mkdep(e, st.vC)
            e.wait_ge(st.sC.h, s1_done)
            dep(e.tensor_scalar_max(HS(4), HS(0), 0.0))
            dep(e.tensor_add(HS(0), HS(4), HS(1)))
            dep(e.tensor_add(HS(3), HS(3), HS(0)))
            dep(e.tensor_scalar(HS(3), HS(3), 1e-4, None, OP.add))
            dep(e.reciprocal(HS(3), HS(3)))
            dep(e.tensor_scalar(HS(3), HS(3), st.spdtc[:, 0:1], None, OP.mult))
            e.sem_inc(st.vC.h, 1)
            st.vC.inc()
        v2_done = st.vC.v

        @blk.scalar
        def _(e):
            deps = mkdep(e, st.sC)
            e.wait_ge(st.vC.h, v2_done)
            wait_tokens(e, retok["seldt"])
            deps(e.activation(out=HS(1), in_=HQ(1), func=AF.Abs))
            deps(e.activation(out=HS(1), in_=HS(1), func=AF.Exp, scale=-1.0))
            deps(e.activation(out=HS(1), in_=HS(1), func=AF.Ln, bias=1.0))
            e.sem_inc(st.sC.h, 1)
            st.sC.inc()
        s2_done = st.sC.v

        @blk.vector
        def _(e):
            dep = mkdep(e, st.vC)
            e.wait_ge(st.sC.h, s2_done)
            e.wait_ge(st.vC.h, v2_done)
            dep(e.tensor_scalar_max(HS(4), HQ(1), 0.0))
            dep(e.tensor_add(HS(1), HS(4), HS(1)))
            dep(e.tensor_add(HS(1), HS(3), HS(1)))
            dep(e.scalar_tensor_tensor(out=HS(3), in0=HS(1), scalar=0.5,
                                       in1=HS(0), op0=OP.mult, op1=OP.mult))
            dep(e.scalar_tensor_tensor(out=HS(4), in0=HS(1), scalar=0.5,
                                       in1=HS(2), op0=OP.mult, op1=OP.mult))
            e.sem_inc(st.vC.h, 1)
            st.vC.inc()
        aw_done = st.vC.v

        @blk.scalar
        def _(e):
            deps = mkdep(e, st.sC)
            e.wait_ge(st.vC.h, aw_done)
            deps(e.activation(out=HS(5), in_=HS(3), func=AF.Square, bias=1.0))
            deps(e.activation(out=HS(1), in_=HS(4), func=AF.Square))
            e.sem_inc(st.sC.h, 1)
            st.sC.inc()
        s3_done = st.sC.v

        @blk.vector
        def _(e):
            dep = mkdep(e, st.vC)
            e.wait_ge(st.sC.h, s3_done)
            dep(e.tensor_add(HS(5), HS(5), HS(1)))
            dep(e.tensor_scalar(HS(0), HS(3), 2.0, 2.0, OP.mult, OP.add))
            dep(e.tensor_sub(HS(0), HS(0), HS(5)))
            dep(e.scalar_tensor_tensor(out=HS(1), in0=HS(3), scalar=-4.0,
                                       in1=HS(5), op0=OP.mult, op1=OP.add))
            dep(e.reciprocal(HS(5), HS(5)))
            dep(e.tensor_mul(HS(0), HS(0), HS(5)))
            dep(e.tensor_copy(st.rre_t[:, :], HS(0)))
            dep(e.scalar_tensor_tensor(out=HS(4), in0=HS(4), scalar=2.0,
                                       in1=HS(5), op0=OP.mult, op1=OP.mult))
            dep(e.tensor_copy(st.rim_t[:, :], HS(4)))
            dep(e.tensor_mul(HS(1), HS(1), HS(5)))
            e.sem_inc(st.vC.h, 1)
            st.vC.inc()
        v4_done = st.vC.v

        gtok = {}

        @blk.sync
        def _(e):
            e.wait_ge(st.vC.h, v4_done)
            for name, dsti in [("beta", 2), ("rg", 3), ("ug", 5)]:
                mc, rg_ = QPOS[name]
                toks = []
                for blk4 in range(4):
                    toks.append(st.chRe.issue(
                        e, hs[32 * blk4:32 * blk4 + 32, dsti * T:(dsti + 1) * T],
                        sm3[32 * rg_:32 * rg_ + 32, mc * T:(mc + 1) * T]))
                gtok[name] = toks

        @blk.scalar
        def _(e):
            deps = mkdep(e, st.sC)
            e.wait_ge(st.vC.h, v4_done)
            for name in ["beta", "rg", "ug"]:
                wait_tokens(e, gtok[name])
            deps(e.activation(out=HS(2), in_=HS(2), func=AF.Sigmoid,
                              bias=st.bsm_t[:, 3:4], scale=1.0))
            deps(e.activation(out=HS(3), in_=HS(3), func=AF.Sigmoid,
                              bias=st.bsm_t[:, 4:5], scale=1.0))
            deps(e.activation(out=HS(5), in_=HS(5), func=AF.Sigmoid,
                              bias=st.bsm_t[:, 5:6], scale=1.0))
            deps(e.activation(out=HS(1), in_=HS(1), func=AF.Ln))
            e.sem_inc(st.sC.h, 1)
            st.sC.inc()
        s4_done = st.sC.v

        @blk.vector
        def _(e):
            dep = mkdep(e, st.vC)
            e.wait_ge(st.sC.h, s4_done)
            dep(e.tensor_mul(HS(1), HS(1), HS(3)))
            e.sem_inc(st.vC.h, 1)
            st.vC.inc()
        v5_done = st.vC.v

        @blk.scalar
        def _(e):
            deps = mkdep(e, st.sC)
            e.wait_ge(st.vC.h, v5_done)
            deps(e.activation(out=HS(1), in_=HS(1), func=AF.Exp))
            e.sem_inc(st.sC.h, 1)
            st.sC.inc()
        s5_done = st.sC.v

        @blk.vector
        def _(e):
            dep = mkdep(e, st.vC)
            e.wait_ge(st.sC.h, s5_done)
            dep(e.tensor_scalar(HS(1), HS(1), -1.0, 1.0, OP.mult, OP.add))
            dep(e.tensor_scalar_max(HS(1), HS(1), 1e-6))
            e.sem_inc(st.vC.h, 1)
            st.vC.inc()
        v6_done = st.vC.v

        @blk.scalar
        def _(e):
            deps = mkdep(e, st.sC)
            e.wait_ge(st.vC.h, v6_done)
            deps(e.activation(out=HS(1), in_=HS(1), func=AF.Sqrt))
            e.sem_inc(st.sC.h, 1)
            st.sC.inc()
        s6_done = st.sC.v

        @blk.vector
        def _(e):
            dep = mkdep(e, st.vC)
            e.wait_ge(st.sC.h, s6_done)
            dep(e.tensor_mul(HS(1), HS(1), HS(5)))
            dep(e.tensor_mul(st.bg_t[:, :], HS(2), HS(1)))
            dep(e.tensor_scalar(st.mbeta_t[:, :], HS(2), -1.0, None, OP.mult))
            e.sem_inc(st.vC.h, 1)
            st.vC.inc()
    st.p3_done = st.vC.v

    tp = st.tapout("heads", [8 * 128, T], BF16)
    if tp is not None:
        with nc.Block() as blk:
            @blk.sync
            def _(e):
                e.wait_ge(st.vC.h, st.p3_done)
                wait_tokens(e, st.sel_tok)
                for i, t in enumerate([st.rre_t, st.rim_t, st.mbeta_t, st.bg_t,
                                       st.sBre_t, st.sBim_t, st.sCre_t, st.sCim_t]):
                    st.chOut.issue(e, tp[i * 128:(i + 1) * 128, :], t[:, :])
    st.p3_ctx = p3c


def p35_qmm(st):
    nc = st.nc
    p35c = ExitStack()
    wqb = st.sbufr(p35c, "wqb", [128, KI * 512], BF16)
    qe = st.sbufr(p35c, "qe", [128, 2 * 512], BF16)
    ps = st.ps
    qh_r = st.qh.rearrange("(j sb t p) (g c) -> j sb t p g c",
                           j=2, sb=NSB, t=L, p=128, g=KI, c=J)

    mm_base = st.tMM.v
    mm_val = lambda m, tt: mm_base + m * 4 + tt + 1
    v_base = st.vC.v
    ev_val = lambda m, tt: v_base + m * 4 + tt + 1
    sp_base = st.chSp.i
    sp_tok = lambda i: st.chSp.token(sp_base + i)
    st.p35_sp_tok = sp_tok
    wtok = [[] for _ in range(8)]

    with nc.Block() as blk:
        @blk.gpsimd
        def _(e):
            for q in range(8):
                if q >= 1:
                    e.wait_ge(st.tMM.h, mm_val(q * 4 - 1, 3))
                for k in range(KI):
                    wtok[q].append(st.chW.issue(
                        e, wqb[:, k * 512:(k + 1) * 512],
                        st.wq[k * 128:(k + 1) * 128, q * 512:(q + 1) * 512]))

        @blk.tensor
        def _(e):
            for q in range(8):
                wait_tokens(e, wtok[q])
                for mi in range(4):
                    m = q * 4 + mi
                    for tt in range(4):
                        i = m * 4 + tt
                        if i >= 8:
                            e.wait_ge(st.vC.h, ev_val((i - 8) // 4, (i - 8) % 4))
                        for k in range(KI):
                            inst = e.matmul(
                                ps[i % 8][:, :],
                                wqb[:, k * 512 + mi * 128:k * 512 + mi * 128 + 128],
                                st.ubuf[:, k * T + tt * 512:k * T + (tt + 1) * 512],
                                start=(k == 0), stop=(k == KI - 1))
                        inst.then_inc(st.tMM.h, 1)
                        st.tMM.inc()

        @blk.vector
        def _(e):
            wait_tokens(e, st.sel_tok)
            for m in range(32):
                j, g = m // 16, m % 16
                plane = st.sCre_t if j == 0 else st.sCim_t
                for tt in range(4):
                    i = m * 4 + tt
                    e.wait_ge(st.tMM.h, mm_val(m, tt))
                    if i >= 2:
                        sem, val = sp_tok(i - 2)
                        e.wait_ge(sem, val)
                    e.tensor_tensor(qe[:, (i % 2) * 512:(i % 2 + 1) * 512],
                                    ps[i % 8][:, :],
                                    plane[:, tt * 512:(tt + 1) * 512],
                                    OP.mult).then_inc(st.vC.h, 1)
                    st.vC.inc()

        @blk.sync
        def _(e):
            for m in range(32):
                j, g = m // 16, m % 16
                for tt in range(4):
                    i = m * 4 + tt
                    e.wait_ge(st.vC.h, ev_val(m, tt))
                    dst = qh_r[j, tt, :, :, g, :].transpose([1, 0, 2])
                    src = qe[:, (i % 2) * 512:(i % 2 + 1) * 512]
                    src = src.rearrange("p (t c) -> p t c", t=L, c=J)
                    st.chSp.issue(e, dst, src)
    st.p35_ctx = p35c

    tp = st.tapout("qhat", [2 * NSB * L * 128, KI * J], BF16)
    if tp is not None:
        with nc.Block() as blk:
            @blk.sync
            def _(e):
                st.chSp.wait_all(e)
                st.chOut.issue(e, tp[:, :], st.qh[:, :])


def p4_scan(st):
    nc = st.nc
    p4c = ExitStack()
    kapre = st.sbufr(p4c, "kapre", [128, KI * SBL], BF16)
    kapim = st.sbufr(p4c, "kapim", [128, KI * SBL], BF16)
    qsl = st.sbufr(p4c, "qsl", [128, 2 * 1024], BF16)
    trj = {}
    for nm in ["yre", "yim", "p1re", "p1im", "p2re", "p2im"]:
        trj[nm] = st.sbufr(p4c, "tj_" + nm, [128, 2 * 1024], BF16)
    tmp = st.sbufr(p4c, "tmpP4", [128, 12 * 1024], BF16)
    bvt = st.sbufr(p4c, "bvtP4", [128, 1024], BF16)

    RL = KI * SBL

    def kap_sl(plane, t):
        return bass.AP(plane, t * J, [[RL, 128], [SBL, KI], [1, J]])

    def u_sl(sb, t):
        return bass.AP(st.ubuf, sb * SBL + t * J, [[KI * T, 128], [T, KI], [1, J]])

    def bc_sl(plane, sb, t):
        return bass.AP(plane, sb * SBL + t * J, [[T, 128], [0, KI], [1, J]])

    def tj(nm, par):
        return trj[nm][:, par * 1024:(par + 1) * 1024]

    def tmps(i):
        return tmp[:, i * 1024:(i + 1) * 1024]

    qh_rows = lambda j, sb, t: ((j * NSB + sb) * L + t) * 128
    ysp_rows = lambda sb, t: (sb * L + t) * 128

    re_base = st.chRe.i
    sp_base = st.chSp.i
    ktok = {}
    qtok = {}
    sptok = {}
    st.p4_smtok = {}
    for sb in range(NSB):
        ktok[(sb, "re")] = st.chRe.token(re_base + sb * 18)
        ktok[(sb, "im")] = st.chRe.token(re_base + sb * 18 + 1)
        for t in range(L):
            qtok[(sb, t, 0)] = st.chRe.token(re_base + sb * 18 + 2 + t * 2)
            qtok[(sb, t, 1)] = st.chRe.token(re_base + sb * 18 + 2 + t * 2 + 1)
        for t in range(L):
            sptok[(sb, t)] = [st.chSp.token(sp_base + sb * 30 + t * 3 + i)
                              for i in range(3)]
        st.p4_smtok[sb] = [st.chSp.token(sp_base + sb * 30 + 24 + i)
                           for i in range(6)]

    with nc.Block() as blk:
        st.p4_step_end = {}
        st.p4_sb_end = {}

        @blk.vector
        def _(e):
            hw = [0]

            def op(mk, deps=()):
                dv = max(deps) if deps else 0
                if dv > hw[0]:
                    e.wait_ge(st.vC.h, dv)
                    hw[0] = dv
                mk().then_inc(st.vC.h, 1)
                st.vC.inc()
                return st.vC.v

            GRP = [("yre", "yim"), ("p1re", "p1im"), ("p2re", "p2im")]
            T0 = lambda g: tmps(4 * g + 0)
            T1 = lambda g: tmps(4 * g + 1)
            T2 = lambda g: tmps(4 * g + 2)
            T3 = lambda g: tmps(4 * g + 3)
            k_read = 0
            for sb in range(NSB):
                wait_tokens(e, [ktok[(sb, "re")], ktok[(sb, "im")]])
                kv1 = op(lambda: e.tensor_tensor(
                    kapre[:, :], kapre[:, :],
                    bass.AP(st.sBre_t, sb * SBL, [[T, 128], [0, KI], [1, SBL]]),
                    OP.mult))
                kv2 = op(lambda: e.tensor_tensor(
                    kapim[:, :], kapim[:, :],
                    bass.AP(st.sBim_t, sb * SBL, [[T, 128], [0, KI], [1, SBL]]),
                    OP.mult))
                kv = max(kv1, kv2)
                if sb > 0:
                    wait_tokens(e, st.p4_smtok[sb - 1])
                ms_val = {}
                for nm, v0 in [("yre", 0.0), ("yim", 0.0), ("p1re", 1.0),
                               ("p1im", 0.0), ("p2re", 0.0), ("p2im", 1.0)]:
                    ms_val[nm] = op(lambda nm=nm, v0=v0: e.memset(tj(nm, 1), v0),
                                    [kv])
                ge = [max(ms_val[GRP[g][0]], ms_val[GRP[g][1]]) for g in range(3)]
                for t in range(L):
                    cur, prv = t % 2, (t + 1) % 2
                    kre, kim = kap_sl(kapre, t), kap_sl(kapim, t)
                    rre = bc_sl(st.rre_t, sb, t)
                    rim = bc_sl(st.rim_t, sb, t)
                    mb = bc_sl(st.mbeta_t, sb, t)
                    qre = qsl[:, 0:1024]
                    qim = qsl[:, 1024:2048]
                    v = {}
                    GO = [1, 2, 0]
                    for g in GO:
                        v[("A", g)] = op(lambda g=g: e.tensor_tensor(
                            T0(g), tj(GRP[g][0], prv), rre, OP.mult), [ge[g]])
                    bv_v = op(lambda: e.tensor_tensor(
                        bvt[:, :], u_sl(sb, t), bc_sl(st.bg_t, sb, t), OP.mult),
                        [k_read])
                    for g in GO:
                        v[("B", g)] = op(lambda g=g: e.tensor_tensor(
                            T3(g), tj(GRP[g][1], prv), rim, OP.mult), [ge[g]])
                    for g in GO:
                        v[("D", g)] = op(lambda g=g: e.tensor_tensor(
                            T1(g), tj(GRP[g][1], prv), rre, OP.mult), [ge[g]])
                    for g in GO:
                        v[("C", g)] = op(lambda g=g: e.tensor_add(
                            T0(g), T0(g), T3(g)), [v[("A", g)], v[("B", g)]])
                    for g in GO:
                        v[("E", g)] = op(lambda g=g: e.tensor_tensor(
                            T3(g), tj(GRP[g][0], prv), rim, OP.mult),
                            [v[("C", g)]])
                    for g in GO:
                        v[("F", g)] = op(lambda g=g: e.tensor_sub(
                            T1(g), T1(g), T3(g)), [v[("D", g)], v[("E", g)]])
                    if (sb, t - 1) in sptok:
                        wait_tokens(e, sptok[(sb, t - 1)])
                    elif t == 0 and sb > 0:
                        wait_tokens(e, sptok[(sb - 1, L - 1)])
                    for g in GO:
                        v[("G", g)] = op(lambda g=g: e.tensor_tensor(
                            T2(g), kre, T0(g), OP.mult), [v[("C", g)]])
                    for g in GO:
                        v[("H", g)] = op(lambda g=g: e.tensor_tensor(
                            T3(g), kim, T1(g), OP.mult), [v[("F", g)]])
                    for g in GO:
                        v[("I", g)] = op(lambda g=g: e.tensor_add(
                            T2(g), T2(g), T3(g)), [v[("G", g)], v[("H", g)]])
                    for g in GO:
                        v[("J", g)] = op(lambda g=g: e.tensor_tensor(
                            T2(g), T2(g), mb, OP.mult), [v[("I", g)]])
                    v[("K", 0)] = op(lambda: e.tensor_add(
                        T2(0), T2(0), bvt[:, :]), [v[("J", 0)], bv_v])
                    k_read = v[("K", 0)]
                    last = {0: v[("K", 0)], 1: v[("J", 1)], 2: v[("J", 2)]}
                    for g in GO:
                        v[("L", g)] = op(lambda g=g: e.tensor_tensor(
                            T3(g), T2(g), kre, OP.mult), [last[g]])
                    for g in GO:
                        v[("M", g)] = op(lambda g=g: e.tensor_add(
                            tj(GRP[g][0], cur), T0(g), T3(g)),
                            [v[("L", g)], ge[g]])
                    for g in GO:
                        v[("N", g)] = op(lambda g=g: e.tensor_tensor(
                            T3(g), T2(g), kim, OP.mult), [v[("M", g)]])
                    for g in GO:
                        v[("O", g)] = op(lambda g=g: e.tensor_add(
                            tj(GRP[g][1], cur), T1(g), T3(g)),
                            [v[("N", g)], ge[g]])
                    wait_tokens(e, [qtok[(sb, t, 0)], qtok[(sb, t, 1)]])
                    for g in GO:
                        v[("P", g)] = op(lambda g=g: e.tensor_tensor(
                            T0(g), qre, tj(GRP[g][0], cur), OP.mult),
                            [v[("M", g)]])
                    for g in GO:
                        v[("Q", g)] = op(lambda g=g: e.tensor_tensor(
                            T1(g), qim, tj(GRP[g][1], cur), OP.mult),
                            [v[("O", g)]])
                    for g in GO:
                        v[("R", g)] = op(lambda g=g: e.tensor_add(
                            T2(g), T0(g), T1(g)),
                            [v[("P", g)], v[("Q", g)]])
                    ge = [v[("R", g)] for g in range(3)]
                    st.p4_step_end[(sb, t)] = st.vC.v
                st.p4_sb_end[sb] = st.vC.v

        @blk.sync
        def _(e):
            def qload(sb, t):
                par = t % 2
                tk = st.chRe.issue(
                    e, qsl[:, 0:1024],
                    st.qh[qh_rows(0, sb, t):qh_rows(0, sb, t) + 128, :])
                assert tk == qtok[(sb, t, 0)]
                tk = st.chRe.issue(
                    e, qsl[:, 1024:2048],
                    st.qh[qh_rows(1, sb, t):qh_rows(1, sb, t) + 128, :])
                assert tk == qtok[(sb, t, 1)]

            def yspill(sb, t):
                e.wait_ge(st.vC.h, st.p4_step_end[(sb, t)])
                for i, dst in enumerate([st.y0_sp, st.fre_sp, st.fim_sp]):
                    tk = st.chSp.issue(
                        e, dst[ysp_rows(sb, t):ysp_rows(sb, t) + 128, :],
                        tmp[:, (4 * i + 2) * 1024:(4 * i + 3) * 1024])
                    assert tk == sptok[(sb, t)][i]

            for sb in range(NSB):
                if sb > 0:
                    e.wait_ge(st.vC.h, st.p4_sb_end[sb - 1])
                wait_tokens(e, [st.p35_sp_tok(m * 4 + sb) for m in range(32)])
                tk = st.chRe.issue(
                    e, bass.AP(kapre, 0, [[RL, 128], [SBL, KI], [1, SBL]]),
                    bass.AP(st.k_sp.tensor, sb * SBL, [[T, 128], [128 * T, KI], [1, SBL]]))
                assert tk == ktok[(sb, "re")]
                tk = st.chRe.issue(
                    e, bass.AP(kapim, 0, [[RL, 128], [SBL, KI], [1, SBL]]),
                    bass.AP(st.k_sp.tensor, DI * T + sb * SBL, [[T, 128], [128 * T, KI], [1, SBL]]))
                assert tk == ktok[(sb, "im")]
                qload(sb, 0)
                for t in range(L):
                    if t + 1 < L:
                        e.wait_ge(st.vC.h, st.p4_step_end[(sb, t)])
                        yspill(sb, t)
                        qload(sb, t + 1)
                    else:
                        yspill(sb, t)
                e.wait_ge(st.vC.h, st.p4_sb_end[sb])
                for i, src_ in enumerate(["p1re", "p1im", "p2re", "p2im",
                                          "yre", "yim"]):
                    tk = st.chSp.issue(
                        e, st.sm_sp[(i * NSB + sb) * 128:(i * NSB + sb) * 128 + 128, :],
                        tj(src_, (L - 1) % 2))
                    assert tk == st.p4_smtok[sb][i]
    st.p4_ctx = p4c


def p5_stitch(st):
    nc = st.nc
    p5c = ExitStack()
    NAMES = ["a11", "a21", "a12", "a22", "bre", "bim"]
    ssre = st.sbufr(p5c, "ssre", [128, KI * NCH], F32)
    ssim = st.sbufr(p5c, "ssim", [128, KI * NCH], F32)
    p5a = ExitStack()
    ytmp = st.sbufr(p5a, "ytmpP5a", [128, KI * 128], BF16)
    SM = {}
    for nm in NAMES:
        SM[nm] = st.sbufr(p5a, "sm_" + nm, [128, KI * NCH], BF16)
    PONG = {nm: st.sbufr(p5a, "pg_" + nm, [128, KI * NCH], BF16) for nm in NAMES}
    SCR = [st.sbufr(p5a, f"scrP5_{j}", [128, KI * NCH], BF16) for j in range(3)]
    cc1s = st.sbufr(p5a, "cc1s", [128, 96], F32)
    csum = st.sbufr(p5a, "csum", [128, 4 * 96], F32)
    acc = st.sbufr(p5a, "accP5", [128, 2 * 6 * 16], F32)

    smtoks = [tok for sb in range(NSB) for tok in st.p4_smtok[sb]]

    with nc.Block() as blk:
        retoks = []

        @blk.sync
        def _(e):
            wait_tokens(e, smtoks)
            for i, nm in enumerate(NAMES):
                for sb in range(NSB):
                    retoks.append(st.chRe.issue(
                        e,
                        bass.AP(SM[nm], sb * J, [[KI * NCH, 128], [NCH, KI], [1, J]]),
                        st.sm_sp[(i * NSB + sb) * 128:(i * NSB + sb) * 128 + 128, :]))

        @blk.vector
        def _(e):
            hw = [0]

            def op(mk, deps=()):
                dv = max(deps) if deps else 0
                if dv > hw[0]:
                    e.wait_ge(st.vC.h, dv)
                    hw[0] = dv
                mk().then_inc(st.vC.h, 1)
                st.vC.inc()
                return st.vC.v

            wait_tokens(e, retoks)
            AELEM = [("a11", "a11", "a11", "a12", "a21", None),
                     ("a12", "a11", "a12", "a12", "a22", None),
                     ("a21", "a21", "a11", "a22", "a21", None),
                     ("a22", "a21", "a12", "a22", "a22", None),
                     ("bre", "a11", "bre", "a12", "bim", "bre"),
                     ("bim", "a21", "bre", "a22", "bim", "bim")]
            cur, oth = SM, PONG
            lvl_end = 0
            for s in [1, 2, 4, 8, 16, 32, 64, 128]:
                w = NCH - s
                ap3 = lambda t, off, ww: bass.AP(
                    t, off, [[KI * NCH, 128], [NCH, KI], [1, ww]])
                E = {nm: ap3(cur[nm], 0, w) for nm in NAMES}
                O = {nm: ap3(cur[nm], s, w) for nm in NAMES}
                D = {nm: ap3(oth[nm], s, w) for nm in NAMES}
                scr = [ap3(SCR[j], 0, w) for j in range(3)]
                p1 = {}
                p2 = {}
                sv = {}
                for half in range(2):
                    ds = AELEM[half * 3:half * 3 + 3]
                    for j, (dn, l1, r1, l2, r2, badd) in enumerate(ds):
                        p1[dn] = op(lambda dn=dn, l1=l1, r1=r1, j=j: e.tensor_tensor(
                            scr[j], O[l1], E[r1], OP.mult),
                            [lvl_end, sv.get(AELEM[half * 3 + j - 3][0], 0)])
                    for (dn, l1, r1, l2, r2, badd) in ds:
                        p2[dn] = op(lambda dn=dn, l2=l2, r2=r2: e.tensor_tensor(
                            D[dn], O[l2], E[r2], OP.mult), [lvl_end])
                    for j, (dn, l1, r1, l2, r2, badd) in enumerate(ds):
                        sv[dn] = op(lambda dn=dn, j=j: e.tensor_tensor(
                            D[dn], D[dn], scr[j], OP.add),
                            [p1[dn], p2[dn]])
                cpv = []
                for nm in ["a11", "a21", "a12"]:
                    cpv.append(op(lambda nm=nm: e.tensor_copy(
                        ap3(oth[nm], 0, s), ap3(cur[nm], 0, s)), [lvl_end]))
                for dn in ["bre", "bim"]:
                    sv[dn] = op(lambda dn=dn: e.tensor_tensor(
                        D[dn], D[dn], O[dn], OP.add), [sv[dn]])
                for nm in ["a22", "bre", "bim"]:
                    cpv.append(op(lambda nm=nm: e.tensor_copy(
                        ap3(oth[nm], 0, s), ap3(cur[nm], 0, s)), [lvl_end]))
                lvl_end = st.vC.v
                cur, oth = oth, cur
            assert cur is SM
            for i, nm in enumerate(NAMES):
                op(lambda i=i, nm=nm: e.tensor_copy(
                    cc1s[:, i * 16:(i + 1) * 16],
                    bass.AP(SM[nm], NCH - 1,
                            [[KI * NCH, 128], [NCH, KI], [1, 1]])), [lvl_end])
            e.sem_inc(st.vC.h, 1)
            st.vC.inc()
        pack_done = st.vC.v

        @blk.sync
        def _(e):
            e.wait_ge(st.vC.h, pack_done)
            st.chSp.issue(e, st.cc1_in[:, :], cc1s[:, :])

        @blk.gpsimd
        def _(e):
            sem, val = st.chSp.token(st.chSp.i - 1)
            e.wait_ge(sem, val)
            e.collective_compute(
                "AllGather", OP.bypass, replica_groups=st.groups,
                ins=[st.cc1_in[:, :]], outs=[st.cc1_out[:, :]]).then_inc(st.gC.h, 1)
            st.gC.inc()
        ag1_done = st.gC.v

        @blk.sync
        def _(e):
            e.wait_ge(st.gC.h, ag1_done)
            for r in range(4):
                st.chRe.issue(e, csum[:, r * 96:(r + 1) * 96],
                              st.cc1_out[r * 128:(r + 1) * 128, :])
        ld_toks = [st.chRe.token(st.chRe.i - 4 + i) for i in range(4)]

        @blk.vector
        def _(e):
            dep = mkdep(e, st.vC)
            wait_tokens(e, ld_toks)
            A0 = lambda i: acc[:, i * 16:(i + 1) * 16]
            A1 = lambda i: acc[:, 96 + i * 16:96 + (i + 1) * 16]
            for r in range(3):
                mcol = st.meta_t[:, 2 + r:3 + r]
                icol = st.meta_t[:, 6 + r:7 + r]
                src_ = lambda i: csum[:, r * 96 + i * 16:r * 96 + (i + 1) * 16]
                dst = A1 if r > 0 else A0
                for i, nm in enumerate(NAMES):
                    if nm in ("a11", "a22"):
                        dep(e.tensor_scalar(dst(i), src_(i), mcol, icol,
                                            OP.mult, OP.add))
                    else:
                        dep(e.tensor_scalar(dst(i), src_(i), mcol, None, OP.mult))
                if r > 0:
                    t0 = ytmp[:, 0:16]
                    t1 = ytmp[:, 16:32]
                    newa = {}
                    for (dn, l1, r1, l2, r2) in [
                            (0, 0, 0, 1, 2), (1, 0, 1, 1, 3),
                            (2, 2, 0, 3, 2), (3, 2, 1, 3, 3)]:
                        dep(e.tensor_tensor(t0, A1(l1), A0(r1), OP.mult))
                        dep(e.tensor_tensor(t1, A1(l2), A0(r2), OP.mult))
                        dep(e.tensor_add(ytmp[:, 32 + dn * 16:48 + dn * 16], t0, t1))
                    for (dn, l1, l2, badd) in [(4, 0, 1, 4), (5, 2, 3, 5)]:
                        dep(e.tensor_tensor(t0, A1(l1), A0(4), OP.mult))
                        dep(e.tensor_tensor(t1, A1(l2), A0(5), OP.mult))
                        dep(e.tensor_add(t0, t0, t1))
                        dep(e.tensor_add(ytmp[:, 32 + dn * 16:48 + dn * 16], t0, A1(badd)))
                    for i in range(6):
                        dep(e.tensor_copy(A0(i), ytmp[:, 32 + i * 16:48 + i * 16]))
            sre = lambda n: bass.AP(ssre, n, [[KI * NCH, 128], [NCH, KI], [1, 1]])
            sim_ = lambda n: bass.AP(ssim, n, [[KI * NCH, 128], [NCH, KI], [1, 1]])
            W = NCH - 1
            pfx = lambda nm: bass.AP(SM[nm], 0, [[KI * NCH, 128], [NCH, KI], [1, W]])
            sso = lambda t: bass.AP(t, 1, [[KI * NCH, 128], [NCH, KI], [1, W]])
            s0re = bass.AP(acc, 4 * 16, [[2 * 6 * 16, 128], [1, 16], [0, W]])
            s0im = bass.AP(acc, 5 * 16, [[2 * 6 * 16, 128], [1, 16], [0, W]])
            scw = [bass.AP(SCR[j], 0, [[KI * NCH, 128], [NCH, KI], [1, W]])
                   for j in range(2)]
            dep(e.tensor_tensor(scw[0], pfx("a11"), s0re, OP.mult))
            dep(e.tensor_tensor(scw[1], pfx("a12"), s0im, OP.mult))
            dep(e.tensor_add(sso(ssre), scw[0], scw[1]))
            dep(e.tensor_tensor(sso(ssre), sso(ssre), pfx("bre"), OP.add))
            dep(e.tensor_copy(sre(0), A0(4)))
            dep(e.tensor_tensor(scw[0], pfx("a21"), s0re, OP.mult))
            dep(e.tensor_tensor(scw[1], pfx("a22"), s0im, OP.mult))
            dep(e.tensor_add(sso(ssim), scw[0], scw[1]))
            dep(e.tensor_tensor(sso(ssim), sso(ssim), pfx("bim"), OP.add))
            dep(e.tensor_copy(sim_(0), A0(5)))
            e.sem_inc(st.vC.h, 1)
            st.vC.inc()
        chain_done = st.vC.v

    p5a.close()
    ytmp2 = st.sbufr(p5c, "ytmpP5b", [128, KI * SBL], BF16)
    ycor = {nm: st.sbufr(p5c, "yc_" + nm, [128, KI * SBL], BF16)
            for nm in ["y0", "fre", "fim"]}
    st.p5_sp_tok_pred = {sb: [st.chSp.token(st.chSp.i + sb * KI + i) for i in range(KI)] for sb in range(NSB)}
    st.p5_cor_end = {sb: chain_done + 4 * (sb + 1) for sb in range(NSB)}
    st.p5_sp_tok = {}
    with nc.Block() as blk:
        ytoks = {}

        @blk.sync
        def _(e):
            def reload(sb):
                for nm, dram in [("y0", st.y0_sp), ("fre", st.fre_sp),
                                 ("fim", st.fim_sp)]:
                    ytoks[(sb, nm)] = st.chRe.issue(
                        e,
                        bass.AP(ycor[nm], 0, [[KI * SBL, 128], [1024, L], [1, 1024]]),
                        bass.AP(dram.tensor, sb * L * 128 * 1024,
                                [[1024, 128], [128 * 1024, L], [1, 1024]]))

            e.wait_ge(st.vC.h, chain_done)
            reload(0)
            for sb in range(NSB):
                e.wait_ge(st.vC.h, st.p5_cor_end[sb])
                toks = []
                for g in range(KI):
                    dst = st.yt_sp[g * 128:(g + 1) * 128, sb * SBL:(sb + 1) * SBL]
                    srcp = bass.AP(ycor["fim"], g * J,
                                   [[KI * SBL, 128], [1024, L], [1, J]])
                    toks.append(st.chSp.issue(e, dst, srcp))
                st.p5_sp_tok[sb] = toks
                assert toks == st.p5_sp_tok_pred[sb]
                if sb + 1 < NSB:
                    wait_tokens(e, toks)
                    reload(sb + 1)

        @blk.vector
        def _(e):
            dep = mkdep(e, st.vC)
            e.wait_ge(st.vC.h, chain_done)
            for sb in range(NSB):
                wait_tokens(e, [ytoks[(sb, nm)] for nm in ["y0", "fre", "fim"]])
                if sb > 0:
                    wait_tokens(e, st.p5_sp_tok_pred[sb - 1])

                def yap(nm):
                    return bass.AP(ycor[nm], 0,
                                   [[KI * SBL, 128], [1024, L], [64, KI], [1, J]])
                tmap = bass.AP(ytmp2, 0, [[KI * SBL, 128], [1024, L], [64, KI], [1, J]])
                ss_re = bass.AP(ssre, sb * J, [[KI * NCH, 128], [0, L], [NCH, KI], [1, J]])
                ss_im = bass.AP(ssim, sb * J, [[KI * NCH, 128], [0, L], [NCH, KI], [1, J]])
                dep(e.tensor_tensor(tmap, yap("fre"), ss_re, OP.mult))
                dep(e.tensor_tensor(tmap, tmap, yap("y0"), OP.add))
                dep(e.tensor_tensor(yap("fim"), yap("fim"), ss_im, OP.mult))
                dep(e.tensor_tensor(yap("fim"), tmap, yap("fim"), OP.add))
                assert st.p5_cor_end[sb] == st.vC.v


    st.p5_done = st.vC.v
    st.p5_ctx = p5c

    tp = st.tapout("schain", [256, KI * NCH], F32)
    if tp is not None:
        with nc.Block() as blk:
            @blk.sync
            def _(e):
                e.wait_ge(st.vC.h, st.p5_done)
                st.chOut.issue(e, tp[0:128, :], ssre[:, :])
                st.chOut.issue(e, tp[128:256, :], ssim[:, :])
    tp = st.tapout("ytrue", [DI, T], BF16)
    if tp is not None:
        with nc.Block() as blk:
            @blk.sync
            def _(e):
                for sb in range(NSB):
                    wait_tokens(e, st.p5_sp_tok[sb])
                st.chOut.issue(e, tp[:, :], st.yt_sp[:, :])


def p6_gn(st):
    nc = st.nc
    st.p67_ctx = ExitStack()
    st.xfbuf = st.sbufr(st.p67_ctx, "xfbuf", [128, KD * T], BF16)
    p6c = ExitStack()
    ybuf = st.sbufr(p6c, "ybufP6", [128, 2 * T], BF16)
    y2b = st.sbufr(p6c, "y2bP6", [128, T], BF16)
    stt = st.sbufr(p6c, "sttP6", [128, 64], F32)
    ps = st.ps

    ytoks = {}
    v_base = st.vC.v
    s_base = st.sC.v
    mm_base6 = st.tMM.v
    st.p6_mm2 = {g: mm_base6 + 2 * (g + 1) for g in range(KI)}

    with nc.Block() as blk:
        @blk.sync
        def _(e):
            for sb in range(NSB):
                wait_tokens(e, st.p5_sp_tok[sb])
            for g in range(KI):
                if g >= 2:
                    e.wait_ge(st.tMM.h, st.p6_mm2[g - 2])
                ytoks[g] = st.chRe.issue(e, ybuf[:, (g % 2) * T:(g % 2 + 1) * T],
                                         st.yt_sp[g * 128:(g + 1) * 128, :])

        @blk.scalar
        def _(e):
            for g in range(KI):
                wait_tokens(e, [ytoks[g]])
                if g >= 1:
                    e.wait_ge(st.tMM.h, st.p6_mm2[g - 1])
                e.activation(out=y2b[:, :], in_=ybuf[:, (g % 2) * T:(g % 2 + 1) * T],
                             func=AF.Square).then_inc(st.sC.h, 1)
                st.sC.inc()

        @blk.tensor
        def _(e):
            for g in range(KI):
                wait_tokens(e, [ytoks[g]])
                for tt in range(4):
                    inst = e.matmul(ps[6][0:32, :], st.ind32[:, :],
                                    ybuf[:, (g % 2) * T + tt * 512:(g % 2) * T + (tt + 1) * 512],
                                    start=(g == 0 and tt == 0),
                                    stop=(g == KI - 1 and tt == 3))
                inst.then_inc(st.tMM.h, 1)
                st.tMM.inc()
                e.wait_ge(st.sC.h, s_base + g + 1)
                for tt in range(4):
                    inst = e.matmul(ps[7][0:32, :], st.ind32[:, :],
                                    y2b[:, tt * 512:(tt + 1) * 512],
                                    start=(g == 0 and tt == 0),
                                    stop=(g == KI - 1 and tt == 3))
                inst.then_inc(st.tMM.h, 1)
                assert st.tMM.inc() == st.p6_mm2[g]
        mm_all = st.tMM.v

        @blk.vector
        def _(e):
            dep = mkdep(e, st.vC)
            e.wait_ge(st.tMM.h, mm_all)
            dep(e.tensor_reduce(stt[0:32, 0:1], ps[6][0:32, :], AX.X, OP.add))
            dep(e.tensor_reduce(stt[0:32, 1:2], ps[7][0:32, :], AX.X, OP.add))
            e.sem_inc(st.vC.h, 1)
            st.vC.inc()
        red_done = st.vC.v

        @blk.sync
        def _(e):
            e.wait_ge(st.vC.h, red_done)
            st.chSp.issue(e, st.cc2_in[:, :], stt[0:32, 0:2])
        sp_tok = st.chSp.token(st.chSp.i - 1)

        @blk.gpsimd
        def _(e):
            e.wait_ge(sp_tok[0], sp_tok[1])
            e.collective_compute(
                "AllGather", OP.bypass, replica_groups=st.groups,
                ins=[st.cc2_in[:, :]], outs=[st.cc2_out[:, :]]).then_inc(st.gC.h, 1)
            st.gC.inc()
        ag_done = st.gC.v

        @blk.sync
        def _(e):
            e.wait_ge(st.gC.h, ag_done)
            t1 = st.chRe.issue(e, stt[:, 2:4], st.cc2_out[:, :])
            sem, val = t1
            e.wait_ge(sem, val)
            toks = [st.chRe.issue(e, stt[0:32, 4 + 2 * r:6 + 2 * r],
                                  stt[32 * r:32 * r + 32, 2:4]) for r in [1, 2, 3]]
            st.p6_altoks = toks

        @blk.vector
        def _(e):
            dep = mkdep(e, st.vC)
            wait_tokens(e, st.p6_altoks)
            dep(e.tensor_add(stt[0:32, 2:4], stt[0:32, 2:4], stt[0:32, 6:8]))
            dep(e.tensor_add(stt[0:32, 2:4], stt[0:32, 2:4], stt[0:32, 8:10]))
            dep(e.tensor_add(stt[0:32, 2:4], stt[0:32, 2:4], stt[0:32, 10:12]))
            NT = float(HD * S)
            dep(e.tensor_scalar(stt[0:32, 10:12], stt[0:32, 2:4], 1.0 / NT, None, OP.mult))
            dep(e.tensor_mul(stt[0:32, 12:13], stt[0:32, 10:11], stt[0:32, 10:11]))
            dep(e.tensor_sub(stt[0:32, 13:14], stt[0:32, 11:12], stt[0:32, 12:13]))
            dep(e.tensor_scalar(stt[0:32, 13:14], stt[0:32, 13:14], 1e-5, None, OP.add))
            e.sem_inc(st.vC.h, 1)
            st.vC.inc()
        var_done = st.vC.v

        @blk.scalar
        def _(e):
            e.wait_ge(st.vC.h, var_done)
            e.activation(out=stt[0:32, 14:15], in_=stt[0:32, 13:14],
                         func=AF.Sqrt).then_inc(st.sC.h, 1)
            st.sC.inc()
        sq_done = st.sC.v

        @blk.vector
        def _(e):
            dep = mkdep(e, st.vC)
            e.wait_ge(st.sC.h, sq_done)
            dep(e.reciprocal(stt[0:32, 14:15], stt[0:32, 14:15]))
            e.sem_inc(st.vC.h, 1)
            st.vC.inc()
        rstd_done = st.vC.v

        @blk.sync
        def _(e):
            e.wait_ge(st.vC.h, rstd_done)
            toks = []
            for r in range(4):
                toks.append(st.chRe.issue(e, stt[32 * r:32 * r + 32, 16:17],
                                          stt[0:32, 14:15]))
                toks.append(st.chRe.issue(e, stt[32 * r:32 * r + 32, 17:18],
                                          stt[0:32, 10:11]))
            st.p6_tile_toks = toks

        @blk.vector
        def _(e):
            dep = mkdep(e, st.vC)
            wait_tokens(e, st.p6_tile_toks)
            dep(e.tensor_scalar(st.gn_scale[:, 0:KI], st.gnw_t[:, :],
                                stt[:, 16:17], None, OP.mult))
            dep(e.tensor_scalar(st.gn_scale[:, KI:2 * KI], st.gn_scale[:, 0:KI],
                                stt[:, 17:18], None, OP.mult))
            dep(e.tensor_sub(st.gn_scale[:, KI:2 * KI], st.gnb_t[:, :],
                             st.gn_scale[:, KI:2 * KI]))
            e.sem_inc(st.vC.h, 1)
            st.vC.inc()
    st.p6_done = st.vC.v
    st.p6_ctx = p6c


def p7_outw_ln1(st):
    nc = st.nc
    st.p6_ctx.close()
    p7c = ExitStack()
    yfin = st.sbufr(p7c, "yfinP7", [128, KI * T], BF16)
    woutb = st.sbufr(p7c, "woutbP7", [128, KI * 1024], BF16)
    ybuf = st.sbufr(p7c, "ybufP7", [128, 2 * T], BF16)
    zbuf = st.sbufr(p7c, "zbufP7", [128, T], BF16)
    ubuf2 = st.sbufr(p7c, "ubufP7", [128, T], BF16)
    szb = st.sbufr(p7c, "szbP7", [128, T], BF16)
    xrl = st.sbufr(p7c, "xrlP7", [128, T], F32)
    strow = st.sbufr(p7c, "strowP7", [1, T], F32)
    invbf = st.sbufr(p7c, "invbfP7", [1, T], BF16)
    ps = st.ps

    rtoks = {}
    wtoks = []
    v_base = st.vC.v
    s_base = st.sC.v
    vyf_base = st.vC.v
    st.p7_yf_done = {g: vyf_base + 3 * (g + 1) for g in range(KI)}
    st.p7_szused = {g: vyf_base + 3 * g + 2 for g in range(KI)}

    with nc.Block() as blk:
        @blk.gpsimd
        def _(e):
            for k in range(KI):
                wtoks.append(st.chW.issue(
                    e, woutb[:, k * 1024:(k + 1) * 1024],
                    st.wout[k * 128:(k + 1) * 128, :]))

        @blk.sync
        def _(e):
            e.wait_ge(st.vC.h, st.p6_done)
            for g in range(KI):
                if g >= 2:
                    e.wait_ge(st.vC.h, st.p7_yf_done[g - 2])
                rtoks[(g, "y")] = st.chRe.issue(
                    e, ybuf[:, (g % 2) * T:(g % 2 + 1) * T],
                    st.yt_sp[g * 128:(g + 1) * 128, :])
                if g >= 1:
                    e.wait_ge(st.vC.h, st.p7_yf_done[g - 1])
                rtoks[(g, "z")] = st.chRe.issue(
                    e, zbuf[:, :], st.z_sp[g * 128:(g + 1) * 128, :])
                rtoks[(g, "u")] = st.chRe.issue(
                    e, ubuf2[:, :], st.u_sp[g * 128:(g + 1) * 128, :])

        @blk.scalar
        def _(e):
            for g in range(KI):
                wait_tokens(e, [rtoks[(g, "z")]])
                if g >= 1:
                    e.wait_ge(st.vC.h, st.p7_szused[g - 1])
                e.activation(out=szb[:, :], in_=zbuf[:, :],
                             func=AF.Silu).then_inc(st.sC.h, 1)
                st.sC.inc()

        @blk.vector
        def _(e):
            dep = mkdep(e, st.vC)
            for g in range(KI):
                wait_tokens(e, [rtoks[(g, "y")], rtoks[(g, "u")]])
                e.wait_ge(st.sC.h, s_base + g + 1)
                dst = yfin[:, g * T:(g + 1) * T]
                dep(e.tensor_scalar(dst, ybuf[:, (g % 2) * T:(g % 2 + 1) * T],
                                    st.gn_scale[:, g:g + 1],
                                    st.gn_scale[:, KI + g:KI + g + 1],
                                    OP.mult, OP.add))
                dep(e.tensor_mul(dst, dst, szb[:, :]))
                assert st.p7_szused[g] == st.vC.v
                dep(e.scalar_tensor_tensor(out=dst,
                                           in0=ubuf2[:, :],
                                           scalar=st.dskip_t[:, g:g + 1],
                                           in1=dst, op0=OP.mult, op1=OP.add))
                assert st.p7_yf_done[g] == st.vC.v
    mm_base = st.tMM.v
    mm_val = lambda mo, tt: mm_base + mo * 4 + tt + 1
    v7_base = st.vC.v
    st.p7_ev = {(mo, tt): v7_base + mo * 4 + tt + 1
                for mo in range(KD) for tt in range(4)}

    with nc.Block() as blk:
        xtoks = {}

        @blk.sync
        def _(e):
            for mo in range(KD):
                if mo >= 1:
                    e.wait_ge(st.vC.h, st.p7_ev[(mo - 1, 3)])
                xtoks[mo] = st.chIn.issue(e, xrl[:, :],
                                          st.x_fm[mo * 128:(mo + 1) * 128, :])

        @blk.tensor
        def _(e):
            wait_tokens(e, wtoks)
            for g in range(KI):
                e.wait_ge(st.vC.h, st.p7_yf_done[g])
            for mo in range(KD):
                for tt in range(4):
                    if mo * 4 + tt >= 8:
                        pi = (mo * 4 + tt - 8)
                        e.wait_ge(st.vC.h, st.p7_ev[(pi // 4, pi % 4)])
                    for k in range(KI):
                        inst = e.matmul(
                            ps[(mo * 4 + tt) % 8][:, :],
                            woutb[:, k * 1024 + mo * 128:k * 1024 + mo * 128 + 128],
                            yfin[:, k * T + tt * 512:k * T + (tt + 1) * 512],
                            start=(k == 0), stop=(k == KI - 1))
                    inst.then_inc(st.tMM.h, 1)
                    st.tMM.inc()

        @blk.vector
        def _(e):
            dep = mkdep(e, st.vC)
            for mo in range(KD):
                wait_tokens(e, [xtoks[mo]])
                for tt in range(4):
                    e.wait_ge(st.tMM.h, mm_val(mo, tt))
                    dep(e.scalar_tensor_tensor(
                        out=st.xfbuf[:, mo * T + tt * 512:mo * T + (tt + 1) * 512],
                        in0=xrl[:, tt * 512:(tt + 1) * 512],
                        scalar=2.0, in1=ps[(mo * 4 + tt) % 8][:, :],
                        op0=OP.mult, op1=OP.add))
                    assert st.p7_ev[(mo, tt)] == st.vC.v
    _ln_norm(st, st.xfbuf, ybuf, strow, invbf)
    st.p7_ctx = p7c


def _ln_norm(st, vb, scratch, strow, invbf, out_dram=None, obuf=None):
    nc = st.nc
    ps = st.ps

    with nc.Block() as blk:
        mm_base = st.tMM.v
        prev_v = st.vC.v
        prev_s = st.sC.v

        @blk.tensor
        def _(e):
            e.wait_ge(st.vC.h, prev_v)
            for tt in range(4):
                for k in range(KD):
                    inst = e.matmul(ps[2 + tt][0:1, :], st.ones_col[:, :],
                                    vb[:, k * T + tt * 512:k * T + (tt + 1) * 512],
                                    start=(k == 0), stop=(k == KD - 1))
                inst.then_inc(st.tMM.h, 1)
                st.tMM.inc()
        sums_done = st.tMM.v

        @blk.vector
        def _(e):
            dep = mkdep(e, st.vC)
            e.wait_ge(st.tMM.h, sums_done)
            for tt in range(4):
                dep(e.tensor_scalar(strow[0:1, tt * 512:(tt + 1) * 512],
                                    ps[2 + tt][0:1, :], 1.0 / DM, None, OP.mult))
            dep(e.tensor_copy(invbf[0:1, :], strow[0:1, :]))
            e.sem_inc(st.vC.h, 1)
            st.vC.inc()
        mu_done = st.vC.v

        @blk.tensor
        def _(e):
            e.wait_ge(st.vC.h, mu_done)
            for tt in range(4):
                inst = e.matmul(ps[2 + tt][:, :], st.ones_row[:, :],
                                invbf[0:1, tt * 512:(tt + 1) * 512],
                                start=True, stop=True)
            inst.then_inc(st.tMM.h, 1)
            st.tMM.inc()
        bc_done = st.tMM.v

        @blk.vector
        def _(e):
            dep = mkdep(e, st.vC)
            e.wait_ge(st.tMM.h, bc_done)
            for k in range(KD):
                for tt in range(4):
                    sl = vb[:, k * T + tt * 512:k * T + (tt + 1) * 512]
                    dep(e.tensor_sub(sl, sl, ps[2 + tt][:, :]))
            e.sem_inc(st.vC.h, 1)
            st.vC.inc()
        d_done = st.vC.v

        ln_mm2 = lambda k: bc_done + k + 1

        @blk.scalar
        def _(e):
            for k in range(KD):
                e.wait_ge(st.vC.h, d_done)
                if k >= 2:
                    e.wait_ge(st.tMM.h, ln_mm2(k - 2))
                e.activation(out=scratch[:, (k % 2) * T:(k % 2 + 1) * T],
                             in_=vb[:, k * T:(k + 1) * T],
                             func=AF.Square).then_inc(st.sC.h, 1)
                st.sC.inc()

        @blk.tensor
        def _(e):
            for k in range(KD):
                e.wait_ge(st.sC.h, prev_s + k + 1)
                for tt in range(4):
                    inst = e.matmul(ps[2 + tt][0:1, :],
                                    st.ones_col[:, :],
                                    scratch[:, (k % 2) * T + tt * 512:(k % 2) * T + (tt + 1) * 512],
                                    start=(k == 0), stop=(k == KD - 1))
                inst.then_inc(st.tMM.h, 1)
                assert st.tMM.inc() == ln_mm2(k)
        sq_done = st.tMM.v

        @blk.vector
        def _(e):
            dep = mkdep(e, st.vC)
            e.wait_ge(st.tMM.h, sq_done)
            for tt in range(4):
                dep(e.tensor_scalar(strow[0:1, tt * 512:(tt + 1) * 512],
                                    ps[2 + tt][0:1, :], 1.0 / DM, 1e-5,
                                    OP.mult, OP.add))
            e.sem_inc(st.vC.h, 1)
            st.vC.inc()
        var_done = st.vC.v

        @blk.scalar
        def _(e):
            e.wait_ge(st.vC.h, var_done)
            e.activation(out=strow[0:1, :], in_=strow[0:1, :],
                         func=AF.Sqrt).then_inc(st.sC.h, 1)
            st.sC.inc()
        sq2_done = st.sC.v

        @blk.vector
        def _(e):
            dep = mkdep(e, st.vC)
            e.wait_ge(st.sC.h, sq2_done)
            dep(e.reciprocal(strow[0:1, :], strow[0:1, :]))
            dep(e.tensor_copy(invbf[0:1, :], strow[0:1, :]))
            e.sem_inc(st.vC.h, 1)
            st.vC.inc()
        rs_done = st.vC.v

        @blk.tensor
        def _(e):
            e.wait_ge(st.vC.h, rs_done)
            for tt in range(4):
                inst = e.matmul(ps[2 + tt][:, :], st.ones_row[:, :],
                                invbf[0:1, tt * 512:(tt + 1) * 512],
                                start=True, stop=True)
            inst.then_inc(st.tMM.h, 1)
            st.tMM.inc()
        bc2_done = st.tMM.v

        st.ln_out_done = {}
        if out_dram is not None:
            st.ln_otok = {(k, hf): st.chOut.token(st.chOut.i + 2 * k + hf)
                          for k in range(KD) for hf in range(2)}

        @blk.vector
        def _(e):
            dep = mkdep(e, st.vC)
            e.wait_ge(st.tMM.h, bc2_done)
            for k in range(KD):
                for tt in range(4):
                    sl = vb[:, k * T + tt * 512:k * T + (tt + 1) * 512]
                    if out_dram is None:
                        dep(e.tensor_mul(sl, sl, ps[2 + tt][:, :]))
                    else:
                        hf = tt // 2
                        if tt % 2 == 0 and (k, hf) != (0, 0):
                            pk, phf = (k, 0) if hf == 1 else (k - 1, 1)
                            sem, val = st.ln_otok[(pk, phf)]
                            e.wait_ge(sem, val)
                        dep(e.tensor_tensor(
                            obuf[:, (tt % 2) * 512:(tt % 2 + 1) * 512],
                            sl, ps[2 + tt][:, :], OP.mult))
                        if tt % 2 == 1:
                            st.ln_out_done[(k, hf)] = st.vC.v

        if out_dram is not None:
            @blk.sync
            def _(e):
                for k in range(KD):
                    for hf in range(2):
                        e.wait_ge(st.vC.h, st.ln_out_done[(k, hf)])
                        tk = st.chOut.issue(
                            e, out_dram[k * 128:(k + 1) * 128,
                                        hf * 1024:(hf + 1) * 1024],
                            obuf[:, :])
                        assert tk == st.ln_otok[(k, hf)]


def p8_ffn(st):
    nc = st.nc
    st.p7_ctx.close()
    p8c = ExitStack()
    wselb = st.sbufr(p8c, "wselbP8", [128, KD * NE], BF16)
    lg = st.sbufr(p8c, "lgP8", [128, 64], F32)
    selb = st.sbufr(p8c, "selbP8", [128, 64], BF16)
    t8 = st.sbufr(p8c, "t8P8", [128, 8], F32)
    thr = st.sbufr(p8c, "thrP8", [128, 1], F32)
    msk = st.sbufr(p8c, "mskP8", [128, 64], F32)
    gtile = st.sbufr(p8c, "gtileP8", [128, 64], F32)
    gatesT = st.sbufr(p8c, "gatesTP8", [64, T], BF16)
    wkb = st.sbufr(p8c, "wkbP8", [128, 2 * KD * 2048], BF16)
    wvb = st.sbufr(p8c, "wvbP8", [128, 16 * 1024], BF16)
    scob = st.sbufr(p8c, "scobP8", [128, 16 * 512], BF16)
    gp = st.sbufr(p8c, "gpP8", [128, 6 * 512], BF16)
    v2acc = st.sbufr(p8c, "v2accP8", [128, KD * T], BF16)
    obuf = st.sbufr(p8c, "obufP8", [128, 1024], F32)
    strw = st.sbufr(p8c, "strwP8", [1, T], F32)
    invb = st.sbufr(p8c, "invbP8", [1, T], BF16)
    ps = st.ps

    xf_ready = st.vC.v
    v8 = st.vC.v
    s8 = st.sC.v
    st.p8_lg_ev = {i: v8 + 6 * i + 1 for i in range(16)}
    st.p8_gt_ev = {i: v8 + 6 * i + 5 for i in range(16)}
    with nc.Block() as blk:
        seltok = []

        @blk.gpsimd
        def _(e):
            for k in range(KD):
                seltok.append(st.chW.issue(
                    e, wselb[:, k * NE:(k + 1) * NE],
                    st.wsel[k * 128:(k + 1) * 128, :]))

        gmm = {}
        gtr = {}

        @blk.tensor
        def _(e):
            wait_tokens(e, seltok)
            e.wait_ge(st.vC.h, xf_ready)
            for i in range(16):
                if i >= 1:
                    e.wait_ge(st.vC.h, st.p8_lg_ev[i - 1])
                for k in range(KD):
                    inst = e.matmul(ps[0][:, 0:NE],
                                    st.xfbuf[:, k * T + i * 128:k * T + i * 128 + 128],
                                    wselb[:, k * NE:(k + 1) * NE],
                                    start=(k == 0), stop=(k == KD - 1))
                inst.then_inc(st.tMM.h, 1)
                gmm[i] = st.tMM.inc()
                e.wait_ge(st.vC.h, st.p8_gt_ev[i])
                inst = e.transpose(ps[1][0:64, 0:128], gtile[:, :], st.io1[:, :])
                inst.then_inc(st.tMM.h, 1)
                gtr[i] = st.tMM.inc()

        @blk.vector
        def _(e):
            dep = mkdep(e, st.vC)
            for i in range(16):
                e.wait_ge(st.tMM.h, gmm[i])
                dep(e.tensor_copy(lg[:, :], ps[0][:, 0:NE]))
                assert st.vC.v == st.p8_lg_ev[i]
                dep(e.max(out=t8[:, :], in_=lg[:, :]))
                dep(e.tensor_copy(thr[:, :], t8[:, 3:4]))
                dep(e.tensor_scalar(msk[:, :], lg[:, :], thr[:, 0:1], None, OP.is_ge))
                e.wait_ge(st.sC.h, s8 + i + 1)
                dep(e.tensor_mul(gtile[:, :], msk[:, :], selb[:, :]))
                assert st.vC.v == st.p8_gt_ev[i]
                e.wait_ge(st.tMM.h, gtr[i])
                dep(e.tensor_copy(gatesT[:, i * 128:(i + 1) * 128],
                                  ps[1][0:64, 0:128]))

        @blk.scalar
        def _(e):
            for i in range(16):
                e.wait_ge(st.vC.h, st.p8_lg_ev[i])
                e.activation(out=selb[:, :], in_=lg[:, :],
                             func=AF.Sigmoid).then_inc(st.sC.h, 1)
                st.sC.inc()
    gates_done = st.vC.v
    st.p8_gtcp = {i: v8 + 6 * i + 6 for i in range(16)}

    t8_base = st.tMM.v
    PER_TILE = 16 + 8

    def hmm_val(mblk, tile, mi):
        return t8_base + (mblk * 4 + tile) * PER_TILE + mi + 1

    def omm_val(mblk, tile, v):
        return t8_base + (mblk * 4 + tile) * PER_TILE + 16 + v + 1
    st.p8_blk_mm = {mblk: t8_base + (mblk * 4 + 4) * PER_TILE
                    for mblk in range(4)}
    v8b = st.vC.v

    def relu_ev(mblk, tile, mi):
        p = mi + 1 if mi < 6 else 8 + 2 * (mi - 6)
        return v8b + (mblk * 4 + tile) * 40 + p

    def sco_ev(mblk, tile, mi):
        p = 7 + 2 * mi if mi <= 9 else 27 + (mi - 10)
        return v8b + (mblk * 4 + tile) * 40 + p

    def acc_ev(mblk, tile, v):
        return v8b + (mblk * 4 + tile) * 40 + 32 + v + 1
    gre_base = st.chRe.i
    gg_base = st.chG.i

    def gp_in(mblk, tile, mi):
        m = mblk * 16 + mi
        return bass.AP(gatesT, m * T + tile * 512, [[T, 1], [0, 128], [1, 512]])

    def gp_tok(mblk, tile, mi):
        i = (mblk * 4 + tile) * 16 + mi
        if i % 2 == 0:
            return st.chRe.token(gre_base + i // 2)
        return st.chG.token(gg_base + i // 2)

    with nc.Block() as blk:
        wktok = {}
        wvtok = {}

        @blk.sync
        def _(e):
            for mblk in range(4):
                for tile in range(4):
                    for mi in range(16):
                        i = (mblk * 4 + tile) * 16 + mi
                        if i % 2 != 0:
                            continue
                        if mi == 0:
                            e.wait_ge(st.vC.h, st.p8_gtcp[tile * 4 + 3])
                        if i >= 6:
                            pi = i - 6
                            e.wait_ge(st.vC.h,
                                      sco_ev(pi // 64, (pi // 16) % 4, pi % 16))
                        tk = st.chRe.issue(
                            e, gp[:, (i % 6) * 512:(i % 6 + 1) * 512],
                            gp_in(mblk, tile, mi))
                        assert tk == gp_tok(mblk, tile, mi)

        @blk.gpsimd
        def _(e):
            for mblk in range(4):
                if mblk >= 2:
                    e.wait_ge(st.tMM.h, hmm_val(mblk - 2, 3, 15))
                h = (mblk % 2) * KD * 2048
                toks = []
                for k in range(KD):
                    toks.append(st.chW.issue(
                        e, wkb[:, h + k * 2048:h + (k + 1) * 2048],
                        st.wkeys[k * 128:(k + 1) * 128,
                                 mblk * 2048:(mblk + 1) * 2048]))
                wktok[mblk] = toks
                if mblk >= 1:
                    e.wait_ge(st.tMM.h, st.p8_blk_mm[mblk - 1])
                toks = []
                for mi in range(16):
                    m = mblk * 16 + mi
                    toks.append(st.chW.issue(
                        e, wvb[:, mi * 1024:(mi + 1) * 1024],
                        st.wvals[m * 128:(m + 1) * 128, :]))
                wvtok[mblk] = toks

        @blk.tensor
        def _(e):
            for mblk in range(4):
                h = (mblk % 2) * KD * 2048
                h2 = 0
                wait_tokens(e, wktok[mblk])
                for tile in range(4):
                    for mi in range(16):
                        i = (mblk * 4 + tile) * 16 + mi
                        if i >= 6:
                            pi = i - 6
                            e.wait_ge(st.vC.h,
                                      relu_ev(pi // 64, (pi // 16) % 4, pi % 16))
                        for k in range(KD):
                            inst = e.matmul(
                                ps[i % 6][:, :],
                                wkb[:, h + k * 2048 + mi * 128:h + k * 2048 + (mi + 1) * 128],
                                st.xfbuf[:, k * T + tile * 512:k * T + (tile + 1) * 512],
                                start=(k == 0), stop=(k == KD - 1))
                        inst.then_inc(st.tMM.h, 1)
                        st.tMM.inc()
                    if tile == 0:
                        wait_tokens(e, wvtok[mblk])
                    for v in range(8):
                        e.wait_ge(st.vC.h, sco_ev(mblk, tile, 15))
                        j = (mblk * 4 + tile) * 8 + v
                        if j >= 2:
                            pj = j - 2
                            e.wait_ge(st.vC.h,
                                      acc_ev(pj // 32, (pj // 8) % 4, pj % 8))
                        for mi in range(16):
                            inst = e.matmul(
                                ps[6 + v % 2][:, :],
                                wvb[:, h2 + mi * 1024 + v * 128:h2 + mi * 1024 + (v + 1) * 128],
                                scob[:, mi * 512:(mi + 1) * 512],
                                start=(mi == 0), stop=(mi == 15))
                        inst.then_inc(st.tMM.h, 1)
                        st.tMM.inc()

        @blk.scalar
        def _(e):
            for i in range(1, 256, 2):
                mb, tl, m_ = i // 64, (i // 16) % 4, i % 16
                if m_ == 1:
                    e.wait_ge(st.vC.h, st.p8_gtcp[tl * 4 + 3])
                if i >= 6:
                    pi = i - 6
                    e.wait_ge(st.vC.h,
                              sco_ev(pi // 64, (pi // 16) % 4, pi % 16))
                tk = st.chG.issue(e, gp[:, (i % 6) * 512:(i % 6 + 1) * 512],
                                  gp_in(mb, tl, m_))
                assert tk == gp_tok(mb, tl, m_)

        @blk.vector
        def _(e):
            hw = [st.vC.v]

            def op(mk, deps=()):
                dv = max(deps) if deps else 0
                if dv > hw[0]:
                    e.wait_ge(st.vC.h, dv)
                    hw[0] = dv
                mk().then_inc(st.vC.h, 1)
                st.vC.inc()
                return st.vC.v

            for mblk in range(4):
                for tile in range(4):
                    def relu(mi):
                        i3 = (mblk * 4 + tile) * 16 + mi
                        e.wait_ge(st.tMM.h, hmm_val(mblk, tile, mi))
                        op(lambda: e.tensor_scalar(
                            scob[:, mi * 512:(mi + 1) * 512],
                            ps[i3 % 6][:, :], 0.0, None, OP.max))
                        assert st.vC.v == relu_ev(mblk, tile, mi)

                    def sco(mi):
                        i3 = (mblk * 4 + tile) * 16 + mi
                        wait_tokens(e, [gp_tok(mblk, tile, mi)])
                        op(lambda: e.tensor_tensor(
                            scob[:, mi * 512:(mi + 1) * 512],
                            scob[:, mi * 512:(mi + 1) * 512],
                            gp[:, (i3 % 6) * 512:(i3 % 6 + 1) * 512],
                            OP.mult), [relu_ev(mblk, tile, mi)])
                        assert st.vC.v == sco_ev(mblk, tile, mi)

                    for mi in range(6):
                        relu(mi)
                    for j in range(10):
                        sco(j)
                        relu(j + 6)
                    for j in range(10, 16):
                        sco(j)
                    for v in range(8):
                        e.wait_ge(st.tMM.h, omm_val(mblk, tile, v))
                        dst = v2acc[:, v * T + tile * 512:v * T + (tile + 1) * 512]
                        if mblk == 0:
                            op(lambda dst=dst, v=v: e.tensor_copy(
                                dst, ps[6 + v % 2][:, :]))
                        else:
                            op(lambda dst=dst, v=v: e.tensor_tensor(
                                dst, dst, ps[6 + v % 2][:, :], OP.add),
                                [acc_ev(mblk - 1, tile, v)])
                        assert st.vC.v == acc_ev(mblk, tile, v)

    with nc.Block() as blk:
        @blk.vector
        def _(e):
            dep = mkdep(e, st.vC)
            for k in range(KD):
                for tt in range(4):
                    sl = v2acc[:, k * T + tt * 512:k * T + (tt + 1) * 512]
                    dep(e.tensor_add(sl, sl,
                                     st.xfbuf[:, k * T + tt * 512:k * T + (tt + 1) * 512]))
            e.sem_inc(st.vC.h, 1)
            st.vC.inc()
    _ln_norm(st, v2acc, st.xfbuf, strw, invb, out_dram=st.out_p, obuf=obuf)
    st.p8_ctx = p8c


def final_waits(st):
    nc = st.nc
    with nc.Block() as blk:
        @blk.sync
        def _(e):
            for ch in [st.chOut, st.chSp, st.chRe, st.chIn, st.chW]:
                ch.wait_all(e)


def _prep_inputs(inp):
    f = lambda k: np.ascontiguousarray(np.asarray(inp[k], np.float32))
    W = f("L0_attn_in_proj_W")
    bW = f("L0_attn_in_proj_b")
    rows = np.concatenate([PERM, DI + 2 * PERM, DI + 2 * PERM + 1, 3 * DI + PERM])
    wproj = np.ascontiguousarray(W[rows].T)
    bproj = np.ascontiguousarray(bW[rows].reshape(64, 128).T)
    bproj = np.ascontiguousarray(bW[rows].reshape(64, 128).T)

    Qw = f("L0_attn_Q_W")
    qrows = np.concatenate([2 * PERM, 2 * PERM + 1])
    wq = np.ascontiguousarray(Qw[np.ix_(qrows, PERM)].T)

    dyn = f("L0_attn_dyn_W")
    sm_rows = [dyn[0:32], dyn[32:64], dyn[64:96], f("L0_attn_seldt_W"),
               f("L0_attn_beta_W"), f("L0_attn_rg_W"), f("L0_attn_ug_W"),
               f("L0_attn_selB_W")[0::2], f("L0_attn_selB_W")[1::2],
               f("L0_attn_selC_W")[0::2], f("L0_attn_selC_W")[1::2],
               np.zeros((32, DI), np.float32)]
    wsm = np.concatenate(sm_rows, 0)[:, PERM]
    wsm = np.ascontiguousarray(wsm.T)

    wout = np.ascontiguousarray(f("L0_attn_out_W")[:, PERM].T)
    wsel = np.ascontiguousarray(f("L0_ffn_expert_sel").T)
    wkeys = np.ascontiguousarray(f("L0_ffn_keys").transpose(1, 0, 2).reshape(DM, NE * ES))
    wvals = np.ascontiguousarray(f("L0_ffn_values").reshape(NE * ES, DM))

    def tile_di(v):
        return np.ascontiguousarray(v[PERM].reshape(KI, 128).T)

    convw = f("L0_attn_conv_w")[PERM]
    convw_dev = np.ascontiguousarray(
        convw.reshape(KI, 128, 4).transpose(1, 0, 2).reshape(128, KI * 4))
    convb_dev = tile_di(f("L0_attn_conv_b"))
    d_dev = tile_di(f("L0_attn_D"))
    gnw_dev = tile_di(f("L0_attn_gn_w"))
    gnb_dev = tile_di(f("L0_attn_gn_b"))
    rmsw_dev = np.ascontiguousarray(f("L0_attn_rms_w").reshape(KD, 128).T)
    dynb = f("L0_attn_dyn_b")
    bsm = np.stack([dynb[0:32], dynb[32:64], dynb[64:96],
                    f("L0_attn_beta_b"), f("L0_attn_rg_b"), f("L0_attn_ug_b"),
                    np.zeros(32, np.float32), np.zeros(32, np.float32)], 1)
    bsm_dev = np.ascontiguousarray(np.tile(bsm, (4, 1)))
    dtc_dev = np.ascontiguousarray(np.tile(f("L0_attn_dt_c"), 4)[:, None])

    return dict(wproj=wproj, bproj=bproj, wq=wq, wsm=wsm, wout=wout,
                wsel=wsel, wkeys=wkeys, wvals=wvals, convw=convw_dev,
                convb=convb_dev, d_skip=d_dev, gnw=gnw_dev, gnb=gnb_dev,
                rmsw=rmsw_dev, bsm=bsm_dev, dtc=dtc_dev)


def _core_inputs(inp, shared, cid):
    x = np.asarray(inp["L0_attn_x"], np.float32)
    b, q = cid // 4, cid % 4
    t0 = q * T
    xs = x[b, t0 + TOKPERM, :]
    x_fm = np.ascontiguousarray(xs.T)
    halo = np.zeros((DM, 4), np.float32)
    if q > 0:
        halo[:, 1:4] = x[b, t0 - 3:t0, :].T
    meta = np.zeros((128, 12), np.float32)
    meta[:, 0] = float(t0)
    meta[:, 1] = 1.0 if q > 0 else 0.0
    for j in range(4):
        meta[:, 2 + j] = 1.0 if j < q else 0.0
        meta[:, 6 + j] = 0.0 if j < q else 1.0
    d = dict(shared)
    d["x_fm"] = x_fm
    d["x_halo"] = np.ascontiguousarray(halo)
    d["meta"] = np.ascontiguousarray(meta)
    return d


def kernel(**inputs):
    nc, st = build_program()
    shared = _prep_inputs(inputs)
    in_maps = [_core_inputs(inputs, shared, cid) for cid in range(8)]
    res = run_bass_kernel_spmd(nc, in_maps, core_ids=list(range(8)))
    out = np.zeros((B, S, DM), np.float32)
    for cid in range(8):
        b, q = cid // 4, cid % 4
        out[b, q * T + TOKPERM, :] = res.results[cid]["out_p"].T
    return out


# version 09
# speedup vs baseline: 1.2066x; 1.0004x over previous
import numpy as np
from contextlib import ExitStack

import concourse.bass as bass
import concourse.mybir as mybir
from concourse.bass_utils import run_bass_kernel_spmd

F32 = mybir.dt.float32
BF16 = mybir.dt.bfloat16
OP = mybir.AluOpType
AF = mybir.ActivationFunctionType
AX = mybir.AxisListType

B, S, DM, DI, H, HD = 2, 8192, 1024, 2048, 32, 64
T = 2048
NSB, SBL, J, L = 4, 512, 64, 8
NCH = NSB * J
KD, KI = DM // 128, DI // 128
NE, ES = 64, 128
XW = T + 4
LN10K = float(np.log(10000.0))

_r = np.arange(DI)
PERM = (_r % H) * HD + (_r // H)
_s = np.arange(T)
_sb, _rr = _s // SBL, _s % SBL
TOKPERM = _sb * SBL + (_rr % J) * L + (_rr // J)


class Sem:
    def __init__(self, ctx, nc, name):
        self.h = ctx.enter_context(nc.semaphore(name))
        self.v = 0

    def inc(self, n=1):
        self.v += n
        return self.v


class KState:
    pass


class DmaChain:

    def __init__(self, ctx, nc, name, n=4):
        self.sems = [Sem(ctx, nc, f"{name}{i}") for i in range(n)]
        self.n = n
        self.i = 0

    def issue(self, e, out, in_):
        j = self.i % self.n
        s = self.sems[j]
        if self.i >= self.n:
            e.wait_ge(s.h, s.v)
        e.dma_start(out=out, in_=in_).then_inc(s.h, 16)
        s.inc(16)
        self.i += 1
        return (s.h, s.v)

    def token(self, k):
        j = k % self.n
        return (self.sems[j].h, 16 * (k // self.n + 1))

    def wait_all(self, e):
        for s in self.sems:
            e.wait_ge(s.h, s.v)


def wait_tokens(e, tokens):
    best = {}
    for sem, val in tokens:
        if sem not in best or val > best[sem]:
            best[sem] = val
    for sem, val in best.items():
        e.wait_ge(sem, val)


def mkdep(e, sem):
    def dep(inst):
        inst.then_inc(sem.h, 1)
        e.wait_ge(sem.h, sem.inc())
    return dep


def bcast_g(tile, off, ng, ncols, row_len):
    return bass.AP(tile, off, [[row_len, 128], [0, ng], [1, ncols]])


def build_program(taps=(), phases=99):
    taps = set(taps)
    nc = bass.Bass("TRN2", target_bir_lowering=False)
    st = KState()
    st.nc = nc
    st.taps = taps
    st.tap_handles = {}

    def din(name, shape, dt=F32):
        return nc.dram_tensor(name, shape, dt, kind="ExternalInput").ap()

    st.x_fm = din("x_fm", [DM, T])
    st.x_halo = din("x_halo", [DM, 4])
    st.wproj = din("wproj", [DM, 4 * DI])
    st.bproj = din("bproj", [128, 64])
    st.wq = din("wq", [DI, 2 * DI])
    st.wsm = din("wsm", [DI, 384])
    st.wout = din("wout", [DI, DM])
    st.wsel = din("wsel", [DM, NE])
    st.wkeys = din("wkeys", [DM, NE * ES])
    st.wvals = din("wvals", [NE * ES, DM])
    st.convw = din("convw", [128, 4 * KI])
    st.convb = din("convb", [128, KI])
    st.d_skip = din("d_skip", [128, KI])
    st.gnw = din("gnw", [128, KI])
    st.gnb = din("gnb", [128, KI])
    st.rmsw = din("rmsw", [128, KD])
    st.bsm = din("bsm", [128, 8])
    st.dtc = din("dtc", [128, 1])
    st.meta = din("meta", [128, 12])

    def scratch(name, shape, dt):
        return nc.dram_tensor(name, shape, dt, kind="Internal").ap()

    st.z_sp = scratch("z_sp", [DI, T], BF16)
    st.k_sp = scratch("k_sp", [2 * DI, T], BF16)
    st.u_sp = scratch("u_sp", [DI, T], BF16)
    st.qh = scratch("qh", [2 * NSB * L * 128, KI * J], BF16)
    st.y0_sp = scratch("y0_sp", [NSB * L * 128, KI * J], BF16)
    st.fre_sp = scratch("fre_sp", [NSB * L * 128, KI * J], BF16)
    st.fim_sp = scratch("fim_sp", [NSB * L * 128, KI * J], BF16)
    st.sm_sp = scratch("sm_sp", [6 * NSB * 128, KI * J], BF16)
    st.yt_sp = scratch("yt_sp", [DI, T], BF16)
    st.cc1_in = scratch("cc1_in", [128, 96], F32)
    st.cc1_out = scratch("cc1_out", [512, 96], F32)
    st.cc2_in = scratch("cc2_in", [32, 2], F32)
    st.cc2_out = scratch("cc2_out", [128, 2], F32)

    st.out_p = nc.dram_tensor("out_p", [DM, T], F32, kind="ExternalOutput").ap()

    def tapout(name, shape, dt):
        if name in taps:
            h = nc.dram_tensor("tap_" + name, shape, dt, kind="ExternalOutput").ap()
            st.tap_handles[name] = h
            return h
        return None

    st.tapout = tapout
    st.groups = [[0, 1, 2, 3], [4, 5, 6, 7]]

    with ExitStack() as ctx:
        st.ctx = ctx

        def sbuf(name, shape, dt):
            return ctx.enter_context(nc.sbuf_tensor(name, shape, dt))
        st.sbuf = sbuf

        def sbufr(c, name, shape, dt):
            return c.enter_context(nc.sbuf_tensor(name, shape, dt, side="right"))
        st.sbufr = sbufr

        st.ones_col = sbuf("ones_col", [128, 1], BF16)
        st.ones_row = sbuf("ones_row", [1, 128], BF16)
        st.ident = sbuf("ident", [128, 128], BF16)
        st.ind32 = sbuf("ind32", [128, 32], BF16)
        st.rope_t = sbuf("rope_t", [128, 1], F32)
        st.spdtc = sbuf("spdtc", [128, 1], F32)
        st.convw_t = sbuf("convw_t", [128, 4 * KI], F32)
        st.convb_t = sbuf("convb_t", [128, KI], F32)
        st.dskip_t = sbuf("dskip_t", [128, KI], F32)
        st.gnw_t = sbuf("gnw_t", [128, KI], F32)
        st.gnb_t = sbuf("gnb_t", [128, KI], F32)
        st.rmsw_t = sbuf("rmsw_t", [128, KD], F32)
        st.bsm_t = sbuf("bsm_t", [128, 8], F32)
        st.dtc_t = sbuf("dtc_t", [128, 1], F32)
        st.meta_t = sbuf("meta_t", [128, 12], F32)
        st.bproj_t = sbuf("bproj_t", [128, 64], F32)
        st.gn_scale = sbuf("gn_scale", [128, 2 * KI], F32)
        st.io1 = sbuf("io1", [128, 128], F32)
        st.io2 = sbuf("io2", [128, 16], F32)
        st.u_ctx = ExitStack()
        for _nm in ["sBre_t", "sBim_t", "sCre_t", "sCim_t", "mbeta_t",
                    "bg_t", "rre_t", "rim_t"]:
            setattr(st, _nm, st.u_ctx.enter_context(
                nc.sbuf_tensor(_nm, [128, T], BF16)))

        st.ps = [ctx.enter_context(nc.psum_tensor("ps%d" % i, [128, 512], F32))
                 for i in range(8)]

        for name in ["tMM", "vC", "sC", "gC", "gD", "bar"]:
            setattr(st, name, Sem(ctx, nc, name))
        st.chIn = DmaChain(ctx, nc, "chIn", 4)
        st.chW = DmaChain(ctx, nc, "chW", 4)
        st.chSp = DmaChain(ctx, nc, "chSp", 4)
        st.chRe = DmaChain(ctx, nc, "chRe", 4)
        st.chOut = DmaChain(ctx, nc, "chOut", 2)
        st.chG = DmaChain(ctx, nc, "chG", 4)
        st.nbar = 0

        p0_setup(st)
        barrier(st)
        if phases >= 1:
            p1_xn(st)
            barrier(st)
            st.p1_ctx.close()
        if phases >= 2:
            p2_inproj(st)
            barrier(st)
            st.p2_ctx.close()
            st.xn_ctx.close()
        if phases >= 3:
            p3_smalls(st)
            barrier(st)
            st.p3_ctx.close()
        if phases >= 4:
            p35_qmm(st)
        if phases >= 5:
            p4_scan(st)
            barrier(st)
            st.p4_ctx.close()
            st.p35_ctx.close()
            st.u_ctx.close()
        elif phases == 4:
            barrier(st)
            st.p35_ctx.close()
        if phases >= 6:
            p5_stitch(st)
            barrier(st)
            st.p5_ctx.close()
        if phases >= 7:
            p6_gn(st)
            p7_outw_ln1(st)
            barrier(st)
        if phases >= 8:
            p8_ffn(st)
        final_waits(st)

    return nc, st


def barrier(st):
    nc = st.nc
    st.nbar += 1
    target = 5 * st.nbar
    with nc.Block() as blk:
        for engname in ["sync", "tensor", "vector", "scalar", "gpsimd"]:
            def body(e, _eng=engname, _t=target):
                if _eng == "sync":
                    for ch in [st.chIn, st.chW, st.chSp, st.chRe, st.chOut]:
                        ch.wait_all(e)
                e.sem_inc(st.bar.h, 1)
                e.wait_ge(st.bar.h, _t)
            getattr(blk, engname)(body)


def p0_setup(st):
    nc = st.nc
    io1, io2 = st.io1, st.io2
    with nc.Block() as blk:
        p0_tokens = []

        @blk.sync
        def _(e):
            for t, src in [(st.convw_t, st.convw), (st.convb_t, st.convb),
                           (st.dskip_t, st.d_skip), (st.gnw_t, st.gnw),
                           (st.gnb_t, st.gnb), (st.rmsw_t, st.rmsw),
                           (st.bsm_t, st.bsm), (st.dtc_t, st.dtc),
                           (st.meta_t, st.meta), (st.bproj_t, st.bproj)]:
                p0_tokens.append(st.chIn.issue(e, t[:, :], src[:, :]))

        @blk.gpsimd
        def _(e):
            e.iota(io1[:, :], [[1, 128]], base=0, channel_multiplier=-1,
                   allow_small_or_imprecise_dtypes=True)
            e.iota(io2[:, 0:1], [[0, 1]], base=0, channel_multiplier=1,
                   allow_small_or_imprecise_dtypes=True).then_inc(st.gC.h, 1)
            st.gC.inc()
        g_done = st.gC.v

        @blk.vector
        def _(e):
            dep = mkdep(e, st.vC)
            e.wait_ge(st.gC.h, g_done)
            acc = io1[:, 64:96]
            dep(e.tensor_scalar(acc, io1[:, 0:32], 0.0, None, OP.is_equal))
            for off in (-32.0, -64.0, -96.0):
                dep(e.tensor_scalar(io1[:, 96:128], io1[:, 0:32], off, None, OP.is_equal))
                dep(e.tensor_add(acc, acc, io1[:, 96:128]))
            dep(e.tensor_copy(st.ind32[:, :], acc))
            e.sem_inc(st.vC.h, 1)
            st.vC.inc()
        ind_done = st.vC.v

        @blk.gpsimd
        def _(e):
            e.wait_ge(st.vC.h, ind_done)
            e.iota(io1[:, :], [[1, 128]], base=0, channel_multiplier=-1,
                   allow_small_or_imprecise_dtypes=True).then_inc(st.gC.h, 1)
            st.gC.inc()
        g2_done = st.gC.v

        @blk.vector
        def _(e):
            dep = mkdep(e, st.vC)
            e.wait_ge(st.gC.h, g2_done)
            dep(e.tensor_scalar(io1[:, :], io1[:, :], 0.0, None, OP.is_equal))
            dep(e.tensor_copy(st.ident[:, :], io1[:, :]))
            e.memset(st.ones_col[:, :], 1.0)
            e.memset(st.ones_row[:, :], 1.0)
            dep(e.tensor_scalar(io2[:, 2:3], io2[:, 0:1], 32.0, None, OP.is_ge))
            dep(e.tensor_scalar(io2[:, 3:4], io2[:, 0:1], 64.0, None, OP.is_ge))
            dep(e.tensor_add(io2[:, 2:3], io2[:, 2:3], io2[:, 3:4]))
            dep(e.tensor_scalar(io2[:, 3:4], io2[:, 0:1], 96.0, None, OP.is_ge))
            dep(e.tensor_add(io2[:, 2:3], io2[:, 2:3], io2[:, 3:4]))
            dep(e.scalar_tensor_tensor(out=io2[:, 1:2], in0=io2[:, 2:3], scalar=-32.0,
                                   in1=io2[:, 0:1], op0=OP.mult, op1=OP.add))
            e.sem_inc(st.vC.h, 1)
            st.vC.inc()
        v_done = st.vC.v

        @blk.scalar
        def _(e):
            dep = mkdep(e, st.sC)
            e.wait_ge(st.vC.h, v_done)
            e.activation(out=st.rope_t[:, :], in_=io2[:, 1:2], func=AF.Exp,
                         scale=-LN10K / 32.0)
            wait_tokens(e, p0_tokens)
            dep(e.activation(out=io2[:, 4:5], in_=st.dtc_t[:, :], func=AF.Abs))
            dep(e.activation(out=io2[:, 4:5], in_=io2[:, 4:5], func=AF.Exp, scale=-1.0))
            dep(e.activation(out=io2[:, 4:5], in_=io2[:, 4:5], func=AF.Ln, bias=1.0))
            e.sem_inc(st.sC.h, 1)
            st.sC.inc()
        s_done = st.sC.v

        @blk.vector
        def _(e):
            dep = mkdep(e, st.vC)
            e.wait_ge(st.sC.h, s_done)
            dep(e.tensor_scalar_max(io2[:, 3:4], st.dtc_t[:, :], 0.0))
            dep(e.tensor_add(st.spdtc[:, :], io2[:, 3:4], io2[:, 4:5]))


def p1_xn(st):
    nc = st.nc
    st.xn_ctx = ExitStack()
    st.xnbf = st.sbufr(st.xn_ctx, "xnbf", [128, KD * XW], BF16)
    p1c = ExitStack()
    xch = st.sbufr(p1c, "xch", [128, KD * XW], F32)
    x2 = st.sbufr(p1c, "x2", [128, 2 * XW], BF16)
    strow = st.sbufr(p1c, "strow", [1, XW], F32)
    invbf = st.sbufr(p1c, "invbf", [1, XW], BF16)
    st.xch = xch
    st.p1_ctx = p1c

    TT = [512, 512, 512, 512, 4]
    TOFF = [0, 512, 1024, 1536, 2048]
    ps = st.ps

    sq_base = st.sC.v
    mm_base = st.tMM.v
    sq_val = lambda k: sq_base + k + 1
    mm_val = lambda k: mm_base + k + 1
    p1_tokens = []

    with nc.Block() as blk:
        @blk.sync
        def _(e):
            for k in range(KD):
                p1_tokens.append(st.chIn.issue(e, xch[:, k * XW:k * XW + T],
                                               st.x_fm[k * 128:(k + 1) * 128, :]))
                p1_tokens.append(st.chIn.issue(e, xch[:, k * XW + T:(k + 1) * XW],
                                               st.x_halo[k * 128:(k + 1) * 128, :]))

        @blk.scalar
        def _(e):
            wait_tokens(e, p1_tokens)
            for k in range(KD):
                if k >= 2:
                    e.wait_ge(st.tMM.h, mm_val(k - 2))
                e.activation(out=x2[:, (k % 2) * XW:(k % 2 + 1) * XW],
                             in_=xch[:, k * XW:(k + 1) * XW],
                             func=AF.Square).then_inc(st.sC.h, 1)
                st.sC.inc()

        @blk.tensor
        def _(e):
            for k in range(KD):
                e.wait_ge(st.sC.h, sq_val(k))
                xoff = (k % 2) * XW
                for ti in range(5):
                    inst = e.matmul(ps[ti][0:1, 0:TT[ti]], st.ones_col[:, :],
                                    x2[:, xoff + TOFF[ti]:xoff + TOFF[ti] + TT[ti]],
                                    start=(k == 0), stop=(k == KD - 1))
                inst.then_inc(st.tMM.h, 1)
                st.tMM.inc()

        @blk.vector
        def _(e):
            dep = mkdep(e, st.vC)
            e.wait_ge(st.tMM.h, mm_val(KD - 1))
            for ti in range(5):
                dep(e.tensor_copy(strow[0:1, TOFF[ti]:TOFF[ti] + TT[ti]],
                              ps[ti][0:1, 0:TT[ti]]))
            e.tensor_scalar(strow[0:1, :], strow[0:1, :], 1.0 / DM, 1e-6,
                            OP.mult, OP.add).then_inc(st.vC.h, 1)
            st.vC.inc()
        v1_evt = st.vC.v

        @blk.scalar
        def _(e):
            e.wait_ge(st.vC.h, v1_evt)
            e.activation(out=strow[0:1, :], in_=strow[0:1, :],
                         func=AF.Sqrt).then_inc(st.sC.h, 1)
            st.sC.inc()
        s2_evt = st.sC.v

        @blk.vector
        def _(e):
            dep = mkdep(e, st.vC)
            e.wait_ge(st.sC.h, s2_evt)
            dep(e.reciprocal(strow[0:1, :], strow[0:1, :]))
            e.tensor_copy(invbf[0:1, :], strow[0:1, :]).then_inc(st.vC.h, 1)
            st.vC.inc()
        inv_evt = st.vC.v

        @blk.tensor
        def _(e):
            e.wait_ge(st.vC.h, inv_evt)
            for ti in range(5):
                inst = e.matmul(ps[ti][:, 0:TT[ti]], st.ones_row[:, :],
                                invbf[0:1, TOFF[ti]:TOFF[ti] + TT[ti]],
                                start=True, stop=True)
            inst.then_inc(st.tMM.h, 1)
            st.tMM.inc()
        bc_evt = st.tMM.v

        @blk.vector
        def _(e):
            e.wait_ge(st.tMM.h, bc_evt)
            for k in range(KD):
                for ti in range(5):
                    e.scalar_tensor_tensor(
                        out=st.xnbf[:, k * XW + TOFF[ti]:k * XW + TOFF[ti] + TT[ti]],
                        in0=xch[:, k * XW + TOFF[ti]:k * XW + TOFF[ti] + TT[ti]],
                        scalar=st.rmsw_t[:, k:k + 1],
                        in1=ps[ti][:, 0:TT[ti]],
                        op0=OP.mult, op1=OP.mult)
            e.sem_inc(st.vC.h, 1)
            st.vC.inc()
    st.p1_xn_done = st.vC.v

    tp = st.tapout("xn", [128, KD * XW], BF16)
    if tp is not None:
        with nc.Block() as blk:
            @blk.sync
            def _(e):
                e.wait_ge(st.vC.h, st.p1_xn_done)
                st.chOut.issue(e, tp[:, :], st.xnbf[:, :])


def p2_inproj(st):
    nc = st.nc
    st.ubuf = st.u_ctx.enter_context(nc.sbuf_tensor("ubuf", [128, KI * T], BF16))
    p2c = ExitStack()
    wbuf = st.sbufr(p2c, "wbufP2", [128, KD * 2048], BF16)
    ebuf = st.sbufr(p2c, "ebufP2", [128, 2 * T], BF16)
    vbuf = st.sbufr(p2c, "vbufP2", [128, 2 * (T + 4)], BF16)
    cv1 = st.sbufr(p2c, "cv1P2", [128, 2 * T], F32)
    st.p2_ctx = p2c
    ps = st.ps

    TT = [512, 512, 512, 512, 4]
    TOFF = [0, 512, 1024, 1536, 2048]
    MQ = 16

    def ntile(m):
        return 5 if m >= 48 else 4

    spill_base = st.chSp.i
    tMM_base = st.tMM.v
    _mm_pref = [0]
    for m in range(64):
        _mm_pref.append(_mm_pref[-1] + ntile(m))
    mm_val = lambda m, ti: tMM_base + _mm_pref[m] + ti + 1
    sC_base = st.sC.v
    def _silus_before_evac(m):
        return max(0, m - 49)

    def ev_val(m, ti):
        return sC_base + _mm_pref[m] + _silus_before_evac(m) + ti + 1

    def ev_m_done(m):
        return sC_base + _mm_pref[m + 1] + _silus_before_evac(m)

    def silu_val(g):
        if g < KI - 1:
            return sC_base + _mm_pref[50 + g] + g + 1
        return sC_base + _mm_pref[64] + KI
    zk_tok = lambda m: st.chSp.token(spill_base + m)
    vC_base = st.vC.v
    cv_val = lambda g: vC_base + 6 * (g + 1)

    with nc.Block() as blk:
        wtok = [[] for _ in range(4)]

        @blk.gpsimd
        def _(e):
            for q in range(4):
                if q >= 1:
                    e.wait_ge(st.tMM.h, mm_val((q - 1) * MQ + MQ - 1, ntile((q - 1) * MQ + MQ - 1) - 1))
                for k in range(KD):
                    wtok[q].append(st.chW.issue(
                        e, wbuf[:, k * 2048:(k + 1) * 2048],
                        st.wproj[k * 128:(k + 1) * 128, q * 2048:(q + 1) * 2048]))

        _g2mt = []
        for m in range(64):
            for ti in range(ntile(m)):
                _g2mt.append((m, ti))
        gidx_of = {mt: i for i, mt in enumerate(_g2mt)}

        @blk.tensor
        def _(e):
            for q in range(4):
                wait_tokens(e, wtok[q])
                woff = 0
                for mi in range(MQ):
                    m = q * MQ + mi
                    for ti in range(ntile(m)):
                        gi = gidx_of[(m, ti)]
                        if gi >= 8:
                            pm, pt = _g2mt[gi - 8]
                            e.wait_ge(st.sC.h, ev_val(pm, pt))
                        for k in range(KD):
                            inst = e.matmul(ps[gi % 8][:, 0:TT[ti]],
                                     wbuf[:, woff + k * 2048 + mi * 128:woff + k * 2048 + mi * 128 + 128],
                                     st.xnbf[:, k * XW + TOFF[ti]:k * XW + TOFF[ti] + TT[ti]],
                                     start=(k == 0), stop=(k == KD - 1))
                        inst.then_inc(st.tMM.h, 1)
                        st.tMM.inc()

        @blk.scalar
        def _(e):
            def emit_silu(g):
                e.wait_ge(st.vC.h, cv_val(g))
                dst = st.ubuf[:, g * T:(g + 1) * T]
                dst = dst.rearrange("p (sb t c) -> p sb c t", sb=NSB, t=L, c=J)
                e.activation(out=dst, in_=cv1[:, (g % 2) * T:(g % 2 + 1) * T],
                             func=AF.Silu, bias=st.convb_t[:, g:g + 1],
                             scale=1.0).then_inc(st.sC.h, 1)
                st.sC.inc()

            for m in range(64):
                is_v = m >= 48
                half = m % 2
                g = m - 48
                if not is_v and m >= 2:
                    sem, val = zk_tok(m - 2)
                    e.wait_ge(sem, val)
                if is_v and g >= 2:
                    e.wait_ge(st.vC.h, cv_val(g - 2))
                for ti in range(ntile(m)):
                    gi = gidx_of[(m, ti)]
                    e.wait_ge(st.tMM.h, mm_val(m, ti))
                    if not is_v:
                        dst = ebuf[:, half * T + TOFF[ti]:half * T + TOFF[ti] + TT[ti]]
                    else:
                        base = half * (T + 4)
                        if ti < 4:
                            dst = vbuf[:, base + 4 + TOFF[ti]:base + 4 + TOFF[ti] + 512]
                            dst = dst.rearrange("p (c t) -> p t c", c=J, t=L)
                        else:
                            dst = vbuf[:, base:base + 4]
                    e.activation(out=dst, in_=ps[gi % 8][:, 0:TT[ti]], func=AF.Identity,
                                 bias=st.bproj_t[:, m:m + 1],
                                 scale=1.0).then_inc(st.sC.h, 1)
                    st.sC.inc()
                if is_v and g >= 1:
                    emit_silu(g - 1)
            emit_silu(KI - 1)

        @blk.sync
        def _(e):
            for m in range(48):
                e.wait_ge(st.sC.h, ev_m_done(m))
                half = m % 2
                if m < 16:
                    dst = st.z_sp[m * 128:(m + 1) * 128, :]
                elif m < 32:
                    dst = st.k_sp[(m - 16) * 128:(m - 15) * 128, :]
                else:
                    dst = st.k_sp[DI + (m - 32) * 128:DI + (m - 31) * 128, :]
                st.chSp.issue(e, dst, ebuf[:, half * T:(half + 1) * T])

        @blk.vector
        def _(e):
            dep = mkdep(e, st.vC)
            for g in range(KI):
                m = 48 + g
                e.wait_ge(st.sC.h, ev_m_done(m))
                if g >= 2:
                    e.wait_ge(st.sC.h, silu_val(g - 2))
                base = (m % 2) * (T + 4)
                dep(e.tensor_scalar(vbuf[:, base + 1:base + 4], vbuf[:, base + 1:base + 4],
                                st.meta_t[:, 1:2], None, OP.mult))
                co = (g % 2) * T
                dep(e.tensor_scalar(cv1[:, co:co + T], vbuf[:, base + 4:base + 4 + T],
                                st.convw_t[:, g * 4 + 3:g * 4 + 4], None, OP.mult))
                for tap, col in [(2, 3), (1, 2), (0, 1)]:
                    dep(e.scalar_tensor_tensor(out=cv1[:, co:co + T],
                                           in0=vbuf[:, base + col:base + col + T],
                                           scalar=st.convw_t[:, g * 4 + tap:g * 4 + tap + 1],
                                           in1=cv1[:, co:co + T], op0=OP.mult, op1=OP.add))
                e.sem_inc(st.vC.h, 1)
                st.vC.inc()

        @blk.sync
        def _(e):
            for g in range(KI):
                e.wait_ge(st.sC.h, silu_val(g))
                st.chSp.issue(e, st.u_sp[g * 128:(g + 1) * 128, :],
                              st.ubuf[:, g * T:(g + 1) * T])
    st.p2_u_all = silu_val(KI - 1)

    tp = st.tapout("u", [DI, T], BF16)
    if tp is not None:
        with nc.Block() as blk:
            @blk.sync
            def _(e):
                e.wait_ge(st.sC.h, st.p2_u_all)
                for g in range(KI):
                    st.chOut.issue(e, tp[g * 128:(g + 1) * 128, :],
                                   st.ubuf[:, g * T:(g + 1) * T])


def p3_smalls(st):
    nc = st.nc
    p3c = ExitStack()
    wsmb = st.sbufr(p3c, "wsmb", [128, KI * 384], BF16)
    sm3 = st.sbufr(p3c, "sm3", [128, 3 * T], F32)
    hq = st.sbufr(p3c, "hq", [128, 2 * T], F32)
    hs = st.sbufr(p3c, "hs", [128, 6 * T], F32)
    posp = st.sbufr(p3c, "posp", [128, T], F32)
    ps = st.ps

    def HQ(i):
        return hq[:, i * T:(i + 1) * T]

    def HS(i):
        return hs[:, i * T:(i + 1) * T]

    QPOS = {"alpha": (0, 0), "om1": (0, 1), "om2": (0, 2), "seldt": (0, 3),
            "beta": (1, 0), "rg": (1, 1), "ug": (1, 2), "sBre": (1, 3),
            "sBim": (2, 0), "sCre": (2, 1), "sCim": (2, 2)}

    wtok = []
    mm_base = st.tMM.v
    mm_val = lambda mc, tt: mm_base + mc * 4 + tt + 1
    v_base = st.vC.v
    ev_val = lambda mc, tt: v_base + mc * 4 + tt + 1

    with nc.Block() as blk:
        @blk.gpsimd
        def _(e):
            for k in range(KI):
                wtok.append(st.chW.issue(
                    e, wsmb[:, k * 384:(k + 1) * 384],
                    st.wsm[k * 128:(k + 1) * 128, :]))
            e.iota(posp[:, :], [[512, 4], [1, 8], [8, 64]], base=0,
                   channel_multiplier=0,
                   allow_small_or_imprecise_dtypes=True).then_inc(st.gC.h, 1)
            st.gC.inc()
        pos_done = st.gC.v

        @blk.tensor
        def _(e):
            wait_tokens(e, wtok)
            for mc in range(3):
                for tt in range(4):
                    if mc >= 1:
                        e.wait_ge(st.vC.h, ev_val(mc - 1, tt))
                    for k in range(KI):
                        inst = e.matmul(
                            ps[tt][:, :],
                            wsmb[:, k * 384 + mc * 128:k * 384 + mc * 128 + 128],
                            st.ubuf[:, k * T + tt * 512:k * T + (tt + 1) * 512],
                            start=(k == 0), stop=(k == KI - 1))
                    inst.then_inc(st.tMM.h, 1)
                    st.tMM.inc()

        @blk.vector
        def _(e):
            for mc in range(3):
                for tt in range(4):
                    e.wait_ge(st.tMM.h, mm_val(mc, tt))
                    e.tensor_copy(sm3[:, mc * T + tt * 512:mc * T + (tt + 1) * 512],
                                  ps[tt][:, :]).then_inc(st.vC.h, 1)
                    st.vC.inc()

        retok = {}

        @blk.sync
        def _(e):
            for name, dtile, dcol in [("alpha", hs, 0), ("om1", hs, 2),
                                      ("om2", hq, 0), ("seldt", hq, 1)]:
                mc, rg_ = QPOS[name]
                e.wait_ge(st.vC.h, ev_val(mc, 3))
                toks = []
                for blk4 in range(4):
                    toks.append(st.chRe.issue(
                        e, dtile[32 * blk4:32 * blk4 + 32, dcol * T:(dcol + 1) * T],
                        sm3[32 * rg_:32 * rg_ + 32, mc * T:(mc + 1) * T]))
                retok[name] = toks

        @blk.gpsimd
        def _(e):
            for name, dstt in [("sBre", st.sBre_t), ("sBim", st.sBim_t),
                               ("sCre", st.sCre_t), ("sCim", st.sCim_t)]:
                mc, rg_ = QPOS[name]
                e.wait_ge(st.vC.h, ev_val(mc, 3))
                toks = []
                for blk4 in range(4):
                    toks.append(st.chW.issue(
                        e, dstt[32 * blk4:32 * blk4 + 32, :],
                        sm3[32 * rg_:32 * rg_ + 32, mc * T:(mc + 1) * T]))
                retok[name] = toks
            st.sel_tok = retok["sBre"] + retok["sBim"] + retok["sCre"] + retok["sCim"]

    with nc.Block() as blk:
        @blk.vector
        def _(e):
            dep = mkdep(e, st.vC)
            for name in ["alpha", "om1", "om2", "seldt"]:
                wait_tokens(e, retok[name])
            e.wait_ge(st.gC.h, pos_done)
            dep(e.tensor_add(st.io2[:, 5:6], st.bsm_t[:, 1:2], st.bsm_t[:, 2:3]))
            dep(e.tensor_add(HS(2), HS(2), HQ(0)))
            dep(e.tensor_scalar(HS(2), HS(2), st.io2[:, 5:6], None, OP.add))
            dep(e.tensor_scalar(HS(3), posp[:, :], st.meta_t[:, 0:1], None, OP.add))
            dep(e.scalar_tensor_tensor(out=HS(2), in0=HS(3), scalar=st.rope_t[:, 0:1],
                                       in1=HS(2), op0=OP.mult, op1=OP.add))
            e.sem_inc(st.vC.h, 1)
            st.vC.inc()
        om_done = st.vC.v

        @blk.scalar
        def _(e):
            deps = mkdep(e, st.sC)
            wait_tokens(e, retok["alpha"])
            deps(e.activation(out=HS(0), in_=HS(0), func=AF.Identity,
                              bias=st.bsm_t[:, 0:1], scale=1.0))
            deps(e.activation(out=HS(1), in_=HS(0), func=AF.Abs))
            deps(e.activation(out=HS(1), in_=HS(1), func=AF.Exp, scale=-1.0))
            deps(e.activation(out=HS(1), in_=HS(1), func=AF.Ln, bias=1.0))
            e.wait_ge(st.vC.h, om_done)
            deps(e.activation(out=HS(3), in_=HS(2), func=AF.Abs))
            e.sem_inc(st.sC.h, 1)
            st.sC.inc()
        s1_done = st.sC.v

        @blk.vector
        def _(e):
            dep = mkdep(e, st.vC)
            e.wait_ge(st.sC.h, s1_done)
            dep(e.tensor_scalar_max(HS(4), HS(0), 0.0))
            dep(e.tensor_add(HS(0), HS(4), HS(1)))
            dep(e.tensor_add(HS(3), HS(3), HS(0)))
            dep(e.tensor_scalar(HS(3), HS(3), 1e-4, None, OP.add))
            dep(e.reciprocal(HS(3), HS(3)))
            dep(e.tensor_scalar(HS(3), HS(3), st.spdtc[:, 0:1], None, OP.mult))
            e.sem_inc(st.vC.h, 1)
            st.vC.inc()
        v2_done = st.vC.v

        @blk.scalar
        def _(e):
            deps = mkdep(e, st.sC)
            e.wait_ge(st.vC.h, v2_done)
            wait_tokens(e, retok["seldt"])
            deps(e.activation(out=HS(1), in_=HQ(1), func=AF.Abs))
            deps(e.activation(out=HS(1), in_=HS(1), func=AF.Exp, scale=-1.0))
            deps(e.activation(out=HS(1), in_=HS(1), func=AF.Ln, bias=1.0))
            e.sem_inc(st.sC.h, 1)
            st.sC.inc()
        s2_done = st.sC.v

        @blk.vector
        def _(e):
            dep = mkdep(e, st.vC)
            e.wait_ge(st.sC.h, s2_done)
            e.wait_ge(st.vC.h, v2_done)
            dep(e.tensor_scalar_max(HS(4), HQ(1), 0.0))
            dep(e.tensor_add(HS(1), HS(4), HS(1)))
            dep(e.tensor_add(HS(1), HS(3), HS(1)))
            dep(e.scalar_tensor_tensor(out=HS(3), in0=HS(1), scalar=0.5,
                                       in1=HS(0), op0=OP.mult, op1=OP.mult))
            dep(e.scalar_tensor_tensor(out=HS(4), in0=HS(1), scalar=0.5,
                                       in1=HS(2), op0=OP.mult, op1=OP.mult))
            e.sem_inc(st.vC.h, 1)
            st.vC.inc()
        aw_done = st.vC.v

        @blk.scalar
        def _(e):
            deps = mkdep(e, st.sC)
            e.wait_ge(st.vC.h, aw_done)
            deps(e.activation(out=HS(5), in_=HS(3), func=AF.Square, bias=1.0))
            deps(e.activation(out=HS(1), in_=HS(4), func=AF.Square))
            e.sem_inc(st.sC.h, 1)
            st.sC.inc()
        s3_done = st.sC.v

        @blk.vector
        def _(e):
            dep = mkdep(e, st.vC)
            e.wait_ge(st.sC.h, s3_done)
            dep(e.tensor_add(HS(5), HS(5), HS(1)))
            dep(e.tensor_scalar(HS(0), HS(3), 2.0, 2.0, OP.mult, OP.add))
            dep(e.tensor_sub(HS(0), HS(0), HS(5)))
            dep(e.scalar_tensor_tensor(out=HS(1), in0=HS(3), scalar=-4.0,
                                       in1=HS(5), op0=OP.mult, op1=OP.add))
            dep(e.reciprocal(HS(5), HS(5)))
            dep(e.tensor_mul(HS(0), HS(0), HS(5)))
            dep(e.tensor_copy(st.rre_t[:, :], HS(0)))
            dep(e.scalar_tensor_tensor(out=HS(4), in0=HS(4), scalar=2.0,
                                       in1=HS(5), op0=OP.mult, op1=OP.mult))
            dep(e.tensor_copy(st.rim_t[:, :], HS(4)))
            dep(e.tensor_mul(HS(1), HS(1), HS(5)))
            e.sem_inc(st.vC.h, 1)
            st.vC.inc()
        v4_done = st.vC.v

        gtok = {}

        @blk.sync
        def _(e):
            e.wait_ge(st.vC.h, v4_done)
            for name, dsti in [("beta", 2), ("rg", 3), ("ug", 5)]:
                mc, rg_ = QPOS[name]
                toks = []
                for blk4 in range(4):
                    toks.append(st.chRe.issue(
                        e, hs[32 * blk4:32 * blk4 + 32, dsti * T:(dsti + 1) * T],
                        sm3[32 * rg_:32 * rg_ + 32, mc * T:(mc + 1) * T]))
                gtok[name] = toks

        @blk.scalar
        def _(e):
            deps = mkdep(e, st.sC)
            e.wait_ge(st.vC.h, v4_done)
            for name in ["beta", "rg", "ug"]:
                wait_tokens(e, gtok[name])
            deps(e.activation(out=HS(2), in_=HS(2), func=AF.Sigmoid,
                              bias=st.bsm_t[:, 3:4], scale=1.0))
            deps(e.activation(out=HS(3), in_=HS(3), func=AF.Sigmoid,
                              bias=st.bsm_t[:, 4:5], scale=1.0))
            deps(e.activation(out=HS(5), in_=HS(5), func=AF.Sigmoid,
                              bias=st.bsm_t[:, 5:6], scale=1.0))
            deps(e.activation(out=HS(1), in_=HS(1), func=AF.Ln))
            e.sem_inc(st.sC.h, 1)
            st.sC.inc()
        s4_done = st.sC.v

        @blk.vector
        def _(e):
            dep = mkdep(e, st.vC)
            e.wait_ge(st.sC.h, s4_done)
            dep(e.tensor_mul(HS(1), HS(1), HS(3)))
            e.sem_inc(st.vC.h, 1)
            st.vC.inc()
        v5_done = st.vC.v

        @blk.scalar
        def _(e):
            deps = mkdep(e, st.sC)
            e.wait_ge(st.vC.h, v5_done)
            deps(e.activation(out=HS(1), in_=HS(1), func=AF.Exp))
            e.sem_inc(st.sC.h, 1)
            st.sC.inc()
        s5_done = st.sC.v

        @blk.vector
        def _(e):
            dep = mkdep(e, st.vC)
            e.wait_ge(st.sC.h, s5_done)
            dep(e.tensor_scalar(HS(1), HS(1), -1.0, 1.0, OP.mult, OP.add))
            dep(e.tensor_scalar_max(HS(1), HS(1), 1e-6))
            e.sem_inc(st.vC.h, 1)
            st.vC.inc()
        v6_done = st.vC.v

        @blk.scalar
        def _(e):
            deps = mkdep(e, st.sC)
            e.wait_ge(st.vC.h, v6_done)
            deps(e.activation(out=HS(1), in_=HS(1), func=AF.Sqrt))
            e.sem_inc(st.sC.h, 1)
            st.sC.inc()
        s6_done = st.sC.v

        @blk.vector
        def _(e):
            dep = mkdep(e, st.vC)
            e.wait_ge(st.sC.h, s6_done)
            dep(e.tensor_mul(HS(1), HS(1), HS(5)))
            dep(e.tensor_mul(st.bg_t[:, :], HS(2), HS(1)))
            dep(e.tensor_scalar(st.mbeta_t[:, :], HS(2), -1.0, None, OP.mult))
            e.sem_inc(st.vC.h, 1)
            st.vC.inc()
    st.p3_done = st.vC.v

    tp = st.tapout("heads", [8 * 128, T], BF16)
    if tp is not None:
        with nc.Block() as blk:
            @blk.sync
            def _(e):
                e.wait_ge(st.vC.h, st.p3_done)
                wait_tokens(e, st.sel_tok)
                for i, t in enumerate([st.rre_t, st.rim_t, st.mbeta_t, st.bg_t,
                                       st.sBre_t, st.sBim_t, st.sCre_t, st.sCim_t]):
                    st.chOut.issue(e, tp[i * 128:(i + 1) * 128, :], t[:, :])
    st.p3_ctx = p3c


def p35_qmm(st):
    nc = st.nc
    p35c = ExitStack()
    wqb = st.sbufr(p35c, "wqb", [128, KI * 512], BF16)
    qe = st.sbufr(p35c, "qe", [128, 2 * 512], BF16)
    ps = st.ps
    qh_r = st.qh.rearrange("(j sb t p) (g c) -> j sb t p g c",
                           j=2, sb=NSB, t=L, p=128, g=KI, c=J)

    mm_base = st.tMM.v
    mm_val = lambda m, tt: mm_base + m * 4 + tt + 1
    v_base = st.vC.v
    ev_val = lambda m, tt: v_base + m * 4 + tt + 1
    sp_base = st.chSp.i
    sp_tok = lambda i: st.chSp.token(sp_base + i)
    st.p35_sp_tok = sp_tok
    wtok = [[] for _ in range(8)]

    with nc.Block() as blk:
        @blk.gpsimd
        def _(e):
            for q in range(8):
                if q >= 1:
                    e.wait_ge(st.tMM.h, mm_val(q * 4 - 1, 3))
                for k in range(KI):
                    wtok[q].append(st.chW.issue(
                        e, wqb[:, k * 512:(k + 1) * 512],
                        st.wq[k * 128:(k + 1) * 128, q * 512:(q + 1) * 512]))

        @blk.tensor
        def _(e):
            for q in range(8):
                wait_tokens(e, wtok[q])
                for mi in range(4):
                    m = q * 4 + mi
                    for tt in range(4):
                        i = m * 4 + tt
                        if i >= 8:
                            e.wait_ge(st.vC.h, ev_val((i - 8) // 4, (i - 8) % 4))
                        for k in range(KI):
                            inst = e.matmul(
                                ps[i % 8][:, :],
                                wqb[:, k * 512 + mi * 128:k * 512 + mi * 128 + 128],
                                st.ubuf[:, k * T + tt * 512:k * T + (tt + 1) * 512],
                                start=(k == 0), stop=(k == KI - 1))
                        inst.then_inc(st.tMM.h, 1)
                        st.tMM.inc()

        @blk.vector
        def _(e):
            wait_tokens(e, st.sel_tok)
            for m in range(32):
                j, g = m // 16, m % 16
                plane = st.sCre_t if j == 0 else st.sCim_t
                for tt in range(4):
                    i = m * 4 + tt
                    e.wait_ge(st.tMM.h, mm_val(m, tt))
                    if i >= 2:
                        sem, val = sp_tok(i - 2)
                        e.wait_ge(sem, val)
                    e.tensor_tensor(qe[:, (i % 2) * 512:(i % 2 + 1) * 512],
                                    ps[i % 8][:, :],
                                    plane[:, tt * 512:(tt + 1) * 512],
                                    OP.mult).then_inc(st.vC.h, 1)
                    st.vC.inc()

        @blk.sync
        def _(e):
            for m in range(32):
                j, g = m // 16, m % 16
                for tt in range(4):
                    i = m * 4 + tt
                    e.wait_ge(st.vC.h, ev_val(m, tt))
                    dst = qh_r[j, tt, :, :, g, :].transpose([1, 0, 2])
                    src = qe[:, (i % 2) * 512:(i % 2 + 1) * 512]
                    src = src.rearrange("p (t c) -> p t c", t=L, c=J)
                    st.chSp.issue(e, dst, src)
    st.p35_ctx = p35c

    tp = st.tapout("qhat", [2 * NSB * L * 128, KI * J], BF16)
    if tp is not None:
        with nc.Block() as blk:
            @blk.sync
            def _(e):
                st.chSp.wait_all(e)
                st.chOut.issue(e, tp[:, :], st.qh[:, :])


def p4_scan(st):
    nc = st.nc
    p4c = ExitStack()
    kapre = st.sbufr(p4c, "kapre", [128, KI * SBL], BF16)
    kapim = st.sbufr(p4c, "kapim", [128, KI * SBL], BF16)
    qsl = st.sbufr(p4c, "qsl", [128, 2 * 1024], BF16)
    trj = {}
    for nm in ["yre", "yim", "p1re", "p1im", "p2re", "p2im"]:
        trj[nm] = st.sbufr(p4c, "tj_" + nm, [128, 2 * 1024], BF16)
    tmp = st.sbufr(p4c, "tmpP4", [128, 12 * 1024], BF16)
    bvt = st.sbufr(p4c, "bvtP4", [128, 1024], BF16)

    RL = KI * SBL

    def kap_sl(plane, t):
        return bass.AP(plane, t * J, [[RL, 128], [SBL, KI], [1, J]])

    def u_sl(sb, t):
        return bass.AP(st.ubuf, sb * SBL + t * J, [[KI * T, 128], [T, KI], [1, J]])

    def bc_sl(plane, sb, t):
        return bass.AP(plane, sb * SBL + t * J, [[T, 128], [0, KI], [1, J]])

    def tj(nm, par):
        return trj[nm][:, par * 1024:(par + 1) * 1024]

    def tmps(i):
        return tmp[:, i * 1024:(i + 1) * 1024]

    qh_rows = lambda j, sb, t: ((j * NSB + sb) * L + t) * 128
    ysp_rows = lambda sb, t: (sb * L + t) * 128

    re_base = st.chRe.i
    sp_base = st.chSp.i
    ktok = {}
    qtok = {}
    sptok = {}
    st.p4_smtok = {}
    for sb in range(NSB):
        ktok[(sb, "re")] = st.chRe.token(re_base + sb * 18)
        ktok[(sb, "im")] = st.chRe.token(re_base + sb * 18 + 1)
        for t in range(L):
            qtok[(sb, t, 0)] = st.chRe.token(re_base + sb * 18 + 2 + t * 2)
            qtok[(sb, t, 1)] = st.chRe.token(re_base + sb * 18 + 2 + t * 2 + 1)
        for t in range(L):
            sptok[(sb, t)] = [st.chSp.token(sp_base + sb * 30 + t * 3 + i)
                              for i in range(3)]
        st.p4_smtok[sb] = [st.chSp.token(sp_base + sb * 30 + 24 + i)
                           for i in range(6)]

    with nc.Block() as blk:
        st.p4_step_end = {}
        st.p4_sb_end = {}

        @blk.vector
        def _(e):
            hw = [0]

            def op(mk, deps=()):
                dv = max(deps) if deps else 0
                if dv > hw[0]:
                    e.wait_ge(st.vC.h, dv)
                    hw[0] = dv
                mk().then_inc(st.vC.h, 1)
                st.vC.inc()
                return st.vC.v

            GRP = [("yre", "yim"), ("p1re", "p1im"), ("p2re", "p2im")]
            T0 = lambda g: tmps(4 * g + 0)
            T1 = lambda g: tmps(4 * g + 1)
            T2 = lambda g: tmps(4 * g + 2)
            T3 = lambda g: tmps(4 * g + 3)
            k_read = 0
            for sb in range(NSB):
                wait_tokens(e, [ktok[(sb, "re")], ktok[(sb, "im")]])
                kv1 = op(lambda: e.tensor_tensor(
                    kapre[:, :], kapre[:, :],
                    bass.AP(st.sBre_t, sb * SBL, [[T, 128], [0, KI], [1, SBL]]),
                    OP.mult))
                kv2 = op(lambda: e.tensor_tensor(
                    kapim[:, :], kapim[:, :],
                    bass.AP(st.sBim_t, sb * SBL, [[T, 128], [0, KI], [1, SBL]]),
                    OP.mult))
                kv = max(kv1, kv2)
                if sb > 0:
                    wait_tokens(e, st.p4_smtok[sb - 1])
                ms_val = {}
                for nm, v0 in [("yre", 0.0), ("yim", 0.0), ("p1re", 1.0),
                               ("p1im", 0.0), ("p2re", 0.0), ("p2im", 1.0)]:
                    ms_val[nm] = op(lambda nm=nm, v0=v0: e.memset(tj(nm, 1), v0),
                                    [kv])
                ge = [max(ms_val[GRP[g][0]], ms_val[GRP[g][1]]) for g in range(3)]
                for t in range(L):
                    cur, prv = t % 2, (t + 1) % 2
                    kre, kim = kap_sl(kapre, t), kap_sl(kapim, t)
                    rre = bc_sl(st.rre_t, sb, t)
                    rim = bc_sl(st.rim_t, sb, t)
                    mb = bc_sl(st.mbeta_t, sb, t)
                    qre = qsl[:, 0:1024]
                    qim = qsl[:, 1024:2048]
                    v = {}
                    GO = [1, 2, 0]
                    for g in GO:
                        v[("A", g)] = op(lambda g=g: e.tensor_tensor(
                            T0(g), tj(GRP[g][0], prv), rre, OP.mult), [ge[g]])
                    bv_v = op(lambda: e.tensor_tensor(
                        bvt[:, :], u_sl(sb, t), bc_sl(st.bg_t, sb, t), OP.mult),
                        [k_read])
                    for g in GO:
                        v[("B", g)] = op(lambda g=g: e.tensor_tensor(
                            T3(g), tj(GRP[g][1], prv), rim, OP.mult), [ge[g]])
                    for g in GO:
                        v[("D", g)] = op(lambda g=g: e.tensor_tensor(
                            T1(g), tj(GRP[g][1], prv), rre, OP.mult), [ge[g]])
                    for g in GO:
                        v[("C", g)] = op(lambda g=g: e.tensor_add(
                            T0(g), T0(g), T3(g)), [v[("A", g)], v[("B", g)]])
                    for g in GO:
                        v[("E", g)] = op(lambda g=g: e.tensor_tensor(
                            T3(g), tj(GRP[g][0], prv), rim, OP.mult),
                            [v[("C", g)]])
                    for g in GO:
                        v[("F", g)] = op(lambda g=g: e.tensor_sub(
                            T1(g), T1(g), T3(g)), [v[("D", g)], v[("E", g)]])
                    if (sb, t - 1) in sptok:
                        wait_tokens(e, sptok[(sb, t - 1)])
                    elif t == 0 and sb > 0:
                        wait_tokens(e, sptok[(sb - 1, L - 1)])
                    for g in GO:
                        v[("G", g)] = op(lambda g=g: e.tensor_tensor(
                            T2(g), kre, T0(g), OP.mult), [v[("C", g)]])
                    for g in GO:
                        v[("H", g)] = op(lambda g=g: e.tensor_tensor(
                            T3(g), kim, T1(g), OP.mult), [v[("F", g)]])
                    for g in GO:
                        v[("I", g)] = op(lambda g=g: e.tensor_add(
                            T2(g), T2(g), T3(g)), [v[("G", g)], v[("H", g)]])
                    for g in GO:
                        v[("J", g)] = op(lambda g=g: e.tensor_tensor(
                            T2(g), T2(g), mb, OP.mult), [v[("I", g)]])
                    v[("K", 0)] = op(lambda: e.tensor_add(
                        T2(0), T2(0), bvt[:, :]), [v[("J", 0)], bv_v])
                    k_read = v[("K", 0)]
                    last = {0: v[("K", 0)], 1: v[("J", 1)], 2: v[("J", 2)]}
                    for g in GO:
                        v[("L", g)] = op(lambda g=g: e.tensor_tensor(
                            T3(g), T2(g), kre, OP.mult), [last[g]])
                    for g in GO:
                        v[("M", g)] = op(lambda g=g: e.tensor_add(
                            tj(GRP[g][0], cur), T0(g), T3(g)),
                            [v[("L", g)], ge[g]])
                    for g in GO:
                        v[("N", g)] = op(lambda g=g: e.tensor_tensor(
                            T3(g), T2(g), kim, OP.mult), [v[("M", g)]])
                    for g in GO:
                        v[("O", g)] = op(lambda g=g: e.tensor_add(
                            tj(GRP[g][1], cur), T1(g), T3(g)),
                            [v[("N", g)], ge[g]])
                    wait_tokens(e, [qtok[(sb, t, 0)], qtok[(sb, t, 1)]])
                    for g in GO:
                        v[("P", g)] = op(lambda g=g: e.tensor_tensor(
                            T0(g), qre, tj(GRP[g][0], cur), OP.mult),
                            [v[("M", g)]])
                    for g in GO:
                        v[("Q", g)] = op(lambda g=g: e.tensor_tensor(
                            T1(g), qim, tj(GRP[g][1], cur), OP.mult),
                            [v[("O", g)]])
                    for g in GO:
                        v[("R", g)] = op(lambda g=g: e.tensor_add(
                            T2(g), T0(g), T1(g)),
                            [v[("P", g)], v[("Q", g)]])
                    ge = [v[("R", g)] for g in range(3)]
                    st.p4_step_end[(sb, t)] = st.vC.v
                st.p4_sb_end[sb] = st.vC.v

        @blk.sync
        def _(e):
            def qload(sb, t):
                par = t % 2
                tk = st.chRe.issue(
                    e, qsl[:, 0:1024],
                    st.qh[qh_rows(0, sb, t):qh_rows(0, sb, t) + 128, :])
                assert tk == qtok[(sb, t, 0)]
                tk = st.chRe.issue(
                    e, qsl[:, 1024:2048],
                    st.qh[qh_rows(1, sb, t):qh_rows(1, sb, t) + 128, :])
                assert tk == qtok[(sb, t, 1)]

            def yspill(sb, t):
                e.wait_ge(st.vC.h, st.p4_step_end[(sb, t)])
                for i, dst in enumerate([st.y0_sp, st.fre_sp, st.fim_sp]):
                    tk = st.chSp.issue(
                        e, dst[ysp_rows(sb, t):ysp_rows(sb, t) + 128, :],
                        tmp[:, (4 * i + 2) * 1024:(4 * i + 3) * 1024])
                    assert tk == sptok[(sb, t)][i]

            for sb in range(NSB):
                if sb > 0:
                    e.wait_ge(st.vC.h, st.p4_sb_end[sb - 1])
                wait_tokens(e, [st.p35_sp_tok(m * 4 + sb) for m in range(32)])
                tk = st.chRe.issue(
                    e, bass.AP(kapre, 0, [[RL, 128], [SBL, KI], [1, SBL]]),
                    bass.AP(st.k_sp.tensor, sb * SBL, [[T, 128], [128 * T, KI], [1, SBL]]))
                assert tk == ktok[(sb, "re")]
                tk = st.chRe.issue(
                    e, bass.AP(kapim, 0, [[RL, 128], [SBL, KI], [1, SBL]]),
                    bass.AP(st.k_sp.tensor, DI * T + sb * SBL, [[T, 128], [128 * T, KI], [1, SBL]]))
                assert tk == ktok[(sb, "im")]
                qload(sb, 0)
                for t in range(L):
                    if t + 1 < L:
                        e.wait_ge(st.vC.h, st.p4_step_end[(sb, t)])
                        yspill(sb, t)
                        qload(sb, t + 1)
                    else:
                        yspill(sb, t)
                e.wait_ge(st.vC.h, st.p4_sb_end[sb])
                for i, src_ in enumerate(["p1re", "p1im", "p2re", "p2im",
                                          "yre", "yim"]):
                    tk = st.chSp.issue(
                        e, st.sm_sp[(i * NSB + sb) * 128:(i * NSB + sb) * 128 + 128, :],
                        tj(src_, (L - 1) % 2))
                    assert tk == st.p4_smtok[sb][i]
    st.p4_ctx = p4c


def p5_stitch(st):
    nc = st.nc
    p5c = ExitStack()
    NAMES = ["a11", "a21", "a12", "a22", "bre", "bim"]
    ssre = st.sbufr(p5c, "ssre", [128, KI * NCH], F32)
    ssim = st.sbufr(p5c, "ssim", [128, KI * NCH], F32)
    p5a = ExitStack()
    ytmp = st.sbufr(p5a, "ytmpP5a", [128, KI * 128], BF16)
    SM = {}
    for nm in NAMES:
        SM[nm] = st.sbufr(p5a, "sm_" + nm, [128, KI * NCH], BF16)
    PONG = {nm: st.sbufr(p5a, "pg_" + nm, [128, KI * NCH], BF16) for nm in NAMES}
    SCR = [st.sbufr(p5a, f"scrP5_{j}", [128, KI * NCH], BF16) for j in range(3)]
    cc1s = st.sbufr(p5a, "cc1s", [128, 96], F32)
    csum = st.sbufr(p5a, "csum", [128, 4 * 96], F32)
    acc = st.sbufr(p5a, "accP5", [128, 2 * 6 * 16], F32)

    smtoks = [tok for sb in range(NSB) for tok in st.p4_smtok[sb]]

    with nc.Block() as blk:
        retoks = []

        @blk.sync
        def _(e):
            wait_tokens(e, smtoks)
            for i, nm in enumerate(NAMES):
                for sb in range(NSB):
                    retoks.append(st.chRe.issue(
                        e,
                        bass.AP(SM[nm], sb * J, [[KI * NCH, 128], [NCH, KI], [1, J]]),
                        st.sm_sp[(i * NSB + sb) * 128:(i * NSB + sb) * 128 + 128, :]))

        @blk.vector
        def _(e):
            hw = [0]

            def op(mk, deps=()):
                dv = max(deps) if deps else 0
                if dv > hw[0]:
                    e.wait_ge(st.vC.h, dv)
                    hw[0] = dv
                mk().then_inc(st.vC.h, 1)
                st.vC.inc()
                return st.vC.v

            wait_tokens(e, retoks)
            AELEM = [("a11", "a11", "a11", "a12", "a21", None),
                     ("a12", "a11", "a12", "a12", "a22", None),
                     ("a21", "a21", "a11", "a22", "a21", None),
                     ("a22", "a21", "a12", "a22", "a22", None),
                     ("bre", "a11", "bre", "a12", "bim", "bre"),
                     ("bim", "a21", "bre", "a22", "bim", "bim")]
            cur, oth = SM, PONG
            lvl_end = 0
            for s in [1, 2, 4, 8, 16, 32, 64, 128]:
                w = NCH - s
                ap3 = lambda t, off, ww: bass.AP(
                    t, off, [[KI * NCH, 128], [NCH, KI], [1, ww]])
                E = {nm: ap3(cur[nm], 0, w) for nm in NAMES}
                O = {nm: ap3(cur[nm], s, w) for nm in NAMES}
                D = {nm: ap3(oth[nm], s, w) for nm in NAMES}
                scr = [ap3(SCR[j], 0, w) for j in range(3)]
                p1 = {}
                p2 = {}
                sv = {}
                for half in range(2):
                    ds = AELEM[half * 3:half * 3 + 3]
                    for j, (dn, l1, r1, l2, r2, badd) in enumerate(ds):
                        p1[dn] = op(lambda dn=dn, l1=l1, r1=r1, j=j: e.tensor_tensor(
                            scr[j], O[l1], E[r1], OP.mult),
                            [lvl_end, sv.get(AELEM[half * 3 + j - 3][0], 0)])
                    for (dn, l1, r1, l2, r2, badd) in ds:
                        p2[dn] = op(lambda dn=dn, l2=l2, r2=r2: e.tensor_tensor(
                            D[dn], O[l2], E[r2], OP.mult), [lvl_end])
                    for j, (dn, l1, r1, l2, r2, badd) in enumerate(ds):
                        sv[dn] = op(lambda dn=dn, j=j: e.tensor_tensor(
                            D[dn], D[dn], scr[j], OP.add),
                            [p1[dn], p2[dn]])
                cpv = []
                for nm in ["a11", "a21", "a12"]:
                    cpv.append(op(lambda nm=nm: e.tensor_copy(
                        ap3(oth[nm], 0, s), ap3(cur[nm], 0, s)), [lvl_end]))
                for dn in ["bre", "bim"]:
                    sv[dn] = op(lambda dn=dn: e.tensor_tensor(
                        D[dn], D[dn], O[dn], OP.add), [sv[dn]])
                for nm in ["a22", "bre", "bim"]:
                    cpv.append(op(lambda nm=nm: e.tensor_copy(
                        ap3(oth[nm], 0, s), ap3(cur[nm], 0, s)), [lvl_end]))
                lvl_end = st.vC.v
                cur, oth = oth, cur
            assert cur is SM
            for i, nm in enumerate(NAMES):
                op(lambda i=i, nm=nm: e.tensor_copy(
                    cc1s[:, i * 16:(i + 1) * 16],
                    bass.AP(SM[nm], NCH - 1,
                            [[KI * NCH, 128], [NCH, KI], [1, 1]])), [lvl_end])
            e.sem_inc(st.vC.h, 1)
            st.vC.inc()
        pack_done = st.vC.v

        @blk.sync
        def _(e):
            e.wait_ge(st.vC.h, pack_done)
            st.chSp.issue(e, st.cc1_in[:, :], cc1s[:, :])

        @blk.gpsimd
        def _(e):
            sem, val = st.chSp.token(st.chSp.i - 1)
            e.wait_ge(sem, val)
            e.collective_compute(
                "AllGather", OP.bypass, replica_groups=st.groups,
                ins=[st.cc1_in[:, :]], outs=[st.cc1_out[:, :]]).then_inc(st.gC.h, 1)
            st.gC.inc()
        ag1_done = st.gC.v

        @blk.sync
        def _(e):
            e.wait_ge(st.gC.h, ag1_done)
            for r in range(4):
                st.chRe.issue(e, csum[:, r * 96:(r + 1) * 96],
                              st.cc1_out[r * 128:(r + 1) * 128, :])
        ld_toks = [st.chRe.token(st.chRe.i - 4 + i) for i in range(4)]

        @blk.vector
        def _(e):
            dep = mkdep(e, st.vC)
            wait_tokens(e, ld_toks)
            A0 = lambda i: acc[:, i * 16:(i + 1) * 16]
            A1 = lambda i: acc[:, 96 + i * 16:96 + (i + 1) * 16]
            for r in range(3):
                mcol = st.meta_t[:, 2 + r:3 + r]
                icol = st.meta_t[:, 6 + r:7 + r]
                src_ = lambda i: csum[:, r * 96 + i * 16:r * 96 + (i + 1) * 16]
                dst = A1 if r > 0 else A0
                for i, nm in enumerate(NAMES):
                    if nm in ("a11", "a22"):
                        dep(e.tensor_scalar(dst(i), src_(i), mcol, icol,
                                            OP.mult, OP.add))
                    else:
                        dep(e.tensor_scalar(dst(i), src_(i), mcol, None, OP.mult))
                if r > 0:
                    t0 = ytmp[:, 0:16]
                    t1 = ytmp[:, 16:32]
                    newa = {}
                    for (dn, l1, r1, l2, r2) in [
                            (0, 0, 0, 1, 2), (1, 0, 1, 1, 3),
                            (2, 2, 0, 3, 2), (3, 2, 1, 3, 3)]:
                        dep(e.tensor_tensor(t0, A1(l1), A0(r1), OP.mult))
                        dep(e.tensor_tensor(t1, A1(l2), A0(r2), OP.mult))
                        dep(e.tensor_add(ytmp[:, 32 + dn * 16:48 + dn * 16], t0, t1))
                    for (dn, l1, l2, badd) in [(4, 0, 1, 4), (5, 2, 3, 5)]:
                        dep(e.tensor_tensor(t0, A1(l1), A0(4), OP.mult))
                        dep(e.tensor_tensor(t1, A1(l2), A0(5), OP.mult))
                        dep(e.tensor_add(t0, t0, t1))
                        dep(e.tensor_add(ytmp[:, 32 + dn * 16:48 + dn * 16], t0, A1(badd)))
                    for i in range(6):
                        dep(e.tensor_copy(A0(i), ytmp[:, 32 + i * 16:48 + i * 16]))
            sre = lambda n: bass.AP(ssre, n, [[KI * NCH, 128], [NCH, KI], [1, 1]])
            sim_ = lambda n: bass.AP(ssim, n, [[KI * NCH, 128], [NCH, KI], [1, 1]])
            W = NCH - 1
            pfx = lambda nm: bass.AP(SM[nm], 0, [[KI * NCH, 128], [NCH, KI], [1, W]])
            sso = lambda t: bass.AP(t, 1, [[KI * NCH, 128], [NCH, KI], [1, W]])
            s0re = bass.AP(acc, 4 * 16, [[2 * 6 * 16, 128], [1, 16], [0, W]])
            s0im = bass.AP(acc, 5 * 16, [[2 * 6 * 16, 128], [1, 16], [0, W]])
            scw = [bass.AP(SCR[j], 0, [[KI * NCH, 128], [NCH, KI], [1, W]])
                   for j in range(2)]
            dep(e.tensor_tensor(scw[0], pfx("a11"), s0re, OP.mult))
            dep(e.tensor_tensor(scw[1], pfx("a12"), s0im, OP.mult))
            dep(e.tensor_add(sso(ssre), scw[0], scw[1]))
            dep(e.tensor_tensor(sso(ssre), sso(ssre), pfx("bre"), OP.add))
            dep(e.tensor_copy(sre(0), A0(4)))
            dep(e.tensor_tensor(scw[0], pfx("a21"), s0re, OP.mult))
            dep(e.tensor_tensor(scw[1], pfx("a22"), s0im, OP.mult))
            dep(e.tensor_add(sso(ssim), scw[0], scw[1]))
            dep(e.tensor_tensor(sso(ssim), sso(ssim), pfx("bim"), OP.add))
            dep(e.tensor_copy(sim_(0), A0(5)))
            e.sem_inc(st.vC.h, 1)
            st.vC.inc()
        chain_done = st.vC.v

    p5a.close()
    ytmp2 = st.sbufr(p5c, "ytmpP5b", [128, KI * SBL], BF16)
    ycor = {nm: st.sbufr(p5c, "yc_" + nm, [128, KI * SBL], BF16)
            for nm in ["y0", "fre", "fim"]}
    st.p5_sp_tok_pred = {sb: [st.chSp.token(st.chSp.i + sb * KI + i) for i in range(KI)] for sb in range(NSB)}
    st.p5_cor_end = {sb: chain_done + 4 * (sb + 1) for sb in range(NSB)}
    st.p5_sp_tok = {}
    with nc.Block() as blk:
        ytoks = {}

        @blk.sync
        def _(e):
            def reload(sb):
                for nm, dram in [("y0", st.y0_sp), ("fre", st.fre_sp),
                                 ("fim", st.fim_sp)]:
                    ytoks[(sb, nm)] = st.chRe.issue(
                        e,
                        bass.AP(ycor[nm], 0, [[KI * SBL, 128], [1024, L], [1, 1024]]),
                        bass.AP(dram.tensor, sb * L * 128 * 1024,
                                [[1024, 128], [128 * 1024, L], [1, 1024]]))

            e.wait_ge(st.vC.h, chain_done)
            reload(0)
            for sb in range(NSB):
                e.wait_ge(st.vC.h, st.p5_cor_end[sb])
                toks = []
                for g in range(KI):
                    dst = st.yt_sp[g * 128:(g + 1) * 128, sb * SBL:(sb + 1) * SBL]
                    srcp = bass.AP(ycor["fim"], g * J,
                                   [[KI * SBL, 128], [1024, L], [1, J]])
                    toks.append(st.chSp.issue(e, dst, srcp))
                st.p5_sp_tok[sb] = toks
                assert toks == st.p5_sp_tok_pred[sb]
                if sb + 1 < NSB:
                    wait_tokens(e, toks)
                    reload(sb + 1)

        @blk.vector
        def _(e):
            dep = mkdep(e, st.vC)
            e.wait_ge(st.vC.h, chain_done)
            for sb in range(NSB):
                wait_tokens(e, [ytoks[(sb, nm)] for nm in ["y0", "fre", "fim"]])
                if sb > 0:
                    wait_tokens(e, st.p5_sp_tok_pred[sb - 1])

                def yap(nm):
                    return bass.AP(ycor[nm], 0,
                                   [[KI * SBL, 128], [1024, L], [64, KI], [1, J]])
                tmap = bass.AP(ytmp2, 0, [[KI * SBL, 128], [1024, L], [64, KI], [1, J]])
                ss_re = bass.AP(ssre, sb * J, [[KI * NCH, 128], [0, L], [NCH, KI], [1, J]])
                ss_im = bass.AP(ssim, sb * J, [[KI * NCH, 128], [0, L], [NCH, KI], [1, J]])
                dep(e.tensor_tensor(tmap, yap("fre"), ss_re, OP.mult))
                dep(e.tensor_tensor(tmap, tmap, yap("y0"), OP.add))
                dep(e.tensor_tensor(yap("fim"), yap("fim"), ss_im, OP.mult))
                dep(e.tensor_tensor(yap("fim"), tmap, yap("fim"), OP.add))
                assert st.p5_cor_end[sb] == st.vC.v


    st.p5_done = st.vC.v
    st.p5_ctx = p5c

    tp = st.tapout("schain", [256, KI * NCH], F32)
    if tp is not None:
        with nc.Block() as blk:
            @blk.sync
            def _(e):
                e.wait_ge(st.vC.h, st.p5_done)
                st.chOut.issue(e, tp[0:128, :], ssre[:, :])
                st.chOut.issue(e, tp[128:256, :], ssim[:, :])
    tp = st.tapout("ytrue", [DI, T], BF16)
    if tp is not None:
        with nc.Block() as blk:
            @blk.sync
            def _(e):
                for sb in range(NSB):
                    wait_tokens(e, st.p5_sp_tok[sb])
                st.chOut.issue(e, tp[:, :], st.yt_sp[:, :])


def p6_gn(st):
    nc = st.nc
    st.p67_ctx = ExitStack()
    st.xfbuf = st.sbufr(st.p67_ctx, "xfbuf", [128, KD * T], BF16)
    p6c = ExitStack()
    ybuf = st.sbufr(p6c, "ybufP6", [128, 2 * T], BF16)
    y2b = st.sbufr(p6c, "y2bP6", [128, T], BF16)
    stt = st.sbufr(p6c, "sttP6", [128, 64], F32)
    ps = st.ps

    ytoks = {}
    v_base = st.vC.v
    s_base = st.sC.v
    mm_base6 = st.tMM.v
    st.p6_mm2 = {g: mm_base6 + 2 * (g + 1) for g in range(KI)}

    with nc.Block() as blk:
        @blk.sync
        def _(e):
            for sb in range(NSB):
                wait_tokens(e, st.p5_sp_tok[sb])
            for g in range(KI):
                if g >= 2:
                    e.wait_ge(st.tMM.h, st.p6_mm2[g - 2])
                ytoks[g] = st.chRe.issue(e, ybuf[:, (g % 2) * T:(g % 2 + 1) * T],
                                         st.yt_sp[g * 128:(g + 1) * 128, :])

        @blk.scalar
        def _(e):
            for g in range(KI):
                wait_tokens(e, [ytoks[g]])
                if g >= 1:
                    e.wait_ge(st.tMM.h, st.p6_mm2[g - 1])
                e.activation(out=y2b[:, :], in_=ybuf[:, (g % 2) * T:(g % 2 + 1) * T],
                             func=AF.Square).then_inc(st.sC.h, 1)
                st.sC.inc()

        @blk.tensor
        def _(e):
            for g in range(KI):
                wait_tokens(e, [ytoks[g]])
                for tt in range(4):
                    inst = e.matmul(ps[6][0:32, :], st.ind32[:, :],
                                    ybuf[:, (g % 2) * T + tt * 512:(g % 2) * T + (tt + 1) * 512],
                                    start=(g == 0 and tt == 0),
                                    stop=(g == KI - 1 and tt == 3))
                inst.then_inc(st.tMM.h, 1)
                st.tMM.inc()
                e.wait_ge(st.sC.h, s_base + g + 1)
                for tt in range(4):
                    inst = e.matmul(ps[7][0:32, :], st.ind32[:, :],
                                    y2b[:, tt * 512:(tt + 1) * 512],
                                    start=(g == 0 and tt == 0),
                                    stop=(g == KI - 1 and tt == 3))
                inst.then_inc(st.tMM.h, 1)
                assert st.tMM.inc() == st.p6_mm2[g]
        mm_all = st.tMM.v

        @blk.vector
        def _(e):
            dep = mkdep(e, st.vC)
            e.wait_ge(st.tMM.h, mm_all)
            dep(e.tensor_reduce(stt[0:32, 0:1], ps[6][0:32, :], AX.X, OP.add))
            dep(e.tensor_reduce(stt[0:32, 1:2], ps[7][0:32, :], AX.X, OP.add))
            e.sem_inc(st.vC.h, 1)
            st.vC.inc()
        red_done = st.vC.v

        @blk.sync
        def _(e):
            e.wait_ge(st.vC.h, red_done)
            st.chSp.issue(e, st.cc2_in[:, :], stt[0:32, 0:2])
        sp_tok = st.chSp.token(st.chSp.i - 1)

        @blk.gpsimd
        def _(e):
            e.wait_ge(sp_tok[0], sp_tok[1])
            e.collective_compute(
                "AllGather", OP.bypass, replica_groups=st.groups,
                ins=[st.cc2_in[:, :]], outs=[st.cc2_out[:, :]]).then_inc(st.gC.h, 1)
            st.gC.inc()
        ag_done = st.gC.v

        @blk.sync
        def _(e):
            e.wait_ge(st.gC.h, ag_done)
            t1 = st.chRe.issue(e, stt[:, 2:4], st.cc2_out[:, :])
            sem, val = t1
            e.wait_ge(sem, val)
            toks = [st.chRe.issue(e, stt[0:32, 4 + 2 * r:6 + 2 * r],
                                  stt[32 * r:32 * r + 32, 2:4]) for r in [1, 2, 3]]
            st.p6_altoks = toks

        @blk.vector
        def _(e):
            dep = mkdep(e, st.vC)
            wait_tokens(e, st.p6_altoks)
            dep(e.tensor_add(stt[0:32, 2:4], stt[0:32, 2:4], stt[0:32, 6:8]))
            dep(e.tensor_add(stt[0:32, 2:4], stt[0:32, 2:4], stt[0:32, 8:10]))
            dep(e.tensor_add(stt[0:32, 2:4], stt[0:32, 2:4], stt[0:32, 10:12]))
            NT = float(HD * S)
            dep(e.tensor_scalar(stt[0:32, 10:12], stt[0:32, 2:4], 1.0 / NT, None, OP.mult))
            dep(e.tensor_mul(stt[0:32, 12:13], stt[0:32, 10:11], stt[0:32, 10:11]))
            dep(e.tensor_sub(stt[0:32, 13:14], stt[0:32, 11:12], stt[0:32, 12:13]))
            dep(e.tensor_scalar(stt[0:32, 13:14], stt[0:32, 13:14], 1e-5, None, OP.add))
            e.sem_inc(st.vC.h, 1)
            st.vC.inc()
        var_done = st.vC.v

        @blk.scalar
        def _(e):
            e.wait_ge(st.vC.h, var_done)
            e.activation(out=stt[0:32, 14:15], in_=stt[0:32, 13:14],
                         func=AF.Sqrt).then_inc(st.sC.h, 1)
            st.sC.inc()
        sq_done = st.sC.v

        @blk.vector
        def _(e):
            dep = mkdep(e, st.vC)
            e.wait_ge(st.sC.h, sq_done)
            dep(e.reciprocal(stt[0:32, 14:15], stt[0:32, 14:15]))
            e.sem_inc(st.vC.h, 1)
            st.vC.inc()
        rstd_done = st.vC.v

        @blk.sync
        def _(e):
            e.wait_ge(st.vC.h, rstd_done)
            toks = []
            for r in range(4):
                toks.append(st.chRe.issue(e, stt[32 * r:32 * r + 32, 16:17],
                                          stt[0:32, 14:15]))
                toks.append(st.chRe.issue(e, stt[32 * r:32 * r + 32, 17:18],
                                          stt[0:32, 10:11]))
            st.p6_tile_toks = toks

        @blk.vector
        def _(e):
            dep = mkdep(e, st.vC)
            wait_tokens(e, st.p6_tile_toks)
            dep(e.tensor_scalar(st.gn_scale[:, 0:KI], st.gnw_t[:, :],
                                stt[:, 16:17], None, OP.mult))
            dep(e.tensor_scalar(st.gn_scale[:, KI:2 * KI], st.gn_scale[:, 0:KI],
                                stt[:, 17:18], None, OP.mult))
            dep(e.tensor_sub(st.gn_scale[:, KI:2 * KI], st.gnb_t[:, :],
                             st.gn_scale[:, KI:2 * KI]))
            e.sem_inc(st.vC.h, 1)
            st.vC.inc()
    st.p6_done = st.vC.v
    st.p6_ctx = p6c


def p7_outw_ln1(st):
    nc = st.nc
    st.p6_ctx.close()
    p7c = ExitStack()
    yfin = st.sbufr(p7c, "yfinP7", [128, KI * T], BF16)
    woutb = st.sbufr(p7c, "woutbP7", [128, KI * 1024], BF16)
    ybuf = st.sbufr(p7c, "ybufP7", [128, 2 * T], BF16)
    zbuf = st.sbufr(p7c, "zbufP7", [128, T], BF16)
    ubuf2 = st.sbufr(p7c, "ubufP7", [128, T], BF16)
    szb = st.sbufr(p7c, "szbP7", [128, T], BF16)
    xrl = st.sbufr(p7c, "xrlP7", [128, T], F32)
    strow = st.sbufr(p7c, "strowP7", [1, T], F32)
    invbf = st.sbufr(p7c, "invbfP7", [1, T], BF16)
    ps = st.ps

    rtoks = {}
    wtoks = []
    v_base = st.vC.v
    s_base = st.sC.v
    vyf_base = st.vC.v
    st.p7_yf_done = {g: vyf_base + 3 * (g + 1) for g in range(KI)}
    st.p7_szused = {g: vyf_base + 3 * g + 2 for g in range(KI)}

    with nc.Block() as blk:
        @blk.gpsimd
        def _(e):
            for k in range(KI):
                wtoks.append(st.chW.issue(
                    e, woutb[:, k * 1024:(k + 1) * 1024],
                    st.wout[k * 128:(k + 1) * 128, :]))

        @blk.sync
        def _(e):
            e.wait_ge(st.vC.h, st.p6_done)
            for g in range(KI):
                if g >= 2:
                    e.wait_ge(st.vC.h, st.p7_yf_done[g - 2])
                rtoks[(g, "y")] = st.chRe.issue(
                    e, ybuf[:, (g % 2) * T:(g % 2 + 1) * T],
                    st.yt_sp[g * 128:(g + 1) * 128, :])
                if g >= 1:
                    e.wait_ge(st.vC.h, st.p7_yf_done[g - 1])
                rtoks[(g, "z")] = st.chRe.issue(
                    e, zbuf[:, :], st.z_sp[g * 128:(g + 1) * 128, :])
                rtoks[(g, "u")] = st.chRe.issue(
                    e, ubuf2[:, :], st.u_sp[g * 128:(g + 1) * 128, :])

        @blk.scalar
        def _(e):
            for g in range(KI):
                wait_tokens(e, [rtoks[(g, "z")]])
                if g >= 1:
                    e.wait_ge(st.vC.h, st.p7_szused[g - 1])
                e.activation(out=szb[:, :], in_=zbuf[:, :],
                             func=AF.Silu).then_inc(st.sC.h, 1)
                st.sC.inc()

        @blk.vector
        def _(e):
            dep = mkdep(e, st.vC)
            for g in range(KI):
                wait_tokens(e, [rtoks[(g, "y")], rtoks[(g, "u")]])
                e.wait_ge(st.sC.h, s_base + g + 1)
                dst = yfin[:, g * T:(g + 1) * T]
                dep(e.tensor_scalar(dst, ybuf[:, (g % 2) * T:(g % 2 + 1) * T],
                                    st.gn_scale[:, g:g + 1],
                                    st.gn_scale[:, KI + g:KI + g + 1],
                                    OP.mult, OP.add))
                dep(e.tensor_mul(dst, dst, szb[:, :]))
                assert st.p7_szused[g] == st.vC.v
                dep(e.scalar_tensor_tensor(out=dst,
                                           in0=ubuf2[:, :],
                                           scalar=st.dskip_t[:, g:g + 1],
                                           in1=dst, op0=OP.mult, op1=OP.add))
                assert st.p7_yf_done[g] == st.vC.v
    mm_base = st.tMM.v
    mm_val = lambda mo, tt: mm_base + mo * 4 + tt + 1
    v7_base = st.vC.v
    st.p7_ev = {(mo, tt): v7_base + mo * 4 + tt + 1
                for mo in range(KD) for tt in range(4)}

    with nc.Block() as blk:
        xtoks = {}

        @blk.sync
        def _(e):
            for mo in range(KD):
                if mo >= 1:
                    e.wait_ge(st.vC.h, st.p7_ev[(mo - 1, 3)])
                xtoks[mo] = st.chIn.issue(e, xrl[:, :],
                                          st.x_fm[mo * 128:(mo + 1) * 128, :])

        @blk.tensor
        def _(e):
            wait_tokens(e, wtoks)
            for g in range(KI):
                e.wait_ge(st.vC.h, st.p7_yf_done[g])
            for mo in range(KD):
                for tt in range(4):
                    if mo * 4 + tt >= 8:
                        pi = (mo * 4 + tt - 8)
                        e.wait_ge(st.vC.h, st.p7_ev[(pi // 4, pi % 4)])
                    for k in range(KI):
                        inst = e.matmul(
                            ps[(mo * 4 + tt) % 8][:, :],
                            woutb[:, k * 1024 + mo * 128:k * 1024 + mo * 128 + 128],
                            yfin[:, k * T + tt * 512:k * T + (tt + 1) * 512],
                            start=(k == 0), stop=(k == KI - 1))
                    inst.then_inc(st.tMM.h, 1)
                    st.tMM.inc()

        @blk.vector
        def _(e):
            dep = mkdep(e, st.vC)
            for mo in range(KD):
                wait_tokens(e, [xtoks[mo]])
                for tt in range(4):
                    e.wait_ge(st.tMM.h, mm_val(mo, tt))
                    dep(e.scalar_tensor_tensor(
                        out=st.xfbuf[:, mo * T + tt * 512:mo * T + (tt + 1) * 512],
                        in0=xrl[:, tt * 512:(tt + 1) * 512],
                        scalar=2.0, in1=ps[(mo * 4 + tt) % 8][:, :],
                        op0=OP.mult, op1=OP.add))
                    assert st.p7_ev[(mo, tt)] == st.vC.v
    _ln_norm(st, st.xfbuf, ybuf, strow, invbf)
    st.p7_ctx = p7c


def _ln_norm(st, vb, scratch, strow, invbf, out_dram=None, obuf=None):
    nc = st.nc
    ps = st.ps

    with nc.Block() as blk:
        mm_base = st.tMM.v
        prev_v = st.vC.v
        prev_s = st.sC.v

        @blk.tensor
        def _(e):
            e.wait_ge(st.vC.h, prev_v)
            for tt in range(4):
                for k in range(KD):
                    inst = e.matmul(ps[2 + tt][0:1, :], st.ones_col[:, :],
                                    vb[:, k * T + tt * 512:k * T + (tt + 1) * 512],
                                    start=(k == 0), stop=(k == KD - 1))
                inst.then_inc(st.tMM.h, 1)
                st.tMM.inc()
        sums_done = st.tMM.v

        @blk.vector
        def _(e):
            dep = mkdep(e, st.vC)
            e.wait_ge(st.tMM.h, sums_done)
            for tt in range(4):
                dep(e.tensor_scalar(strow[0:1, tt * 512:(tt + 1) * 512],
                                    ps[2 + tt][0:1, :], 1.0 / DM, None, OP.mult))
            dep(e.tensor_copy(invbf[0:1, :], strow[0:1, :]))
            e.sem_inc(st.vC.h, 1)
            st.vC.inc()
        mu_done = st.vC.v

        @blk.tensor
        def _(e):
            e.wait_ge(st.vC.h, mu_done)
            for tt in range(4):
                inst = e.matmul(ps[2 + tt][:, :], st.ones_row[:, :],
                                invbf[0:1, tt * 512:(tt + 1) * 512],
                                start=True, stop=True)
            inst.then_inc(st.tMM.h, 1)
            st.tMM.inc()
        bc_done = st.tMM.v

        @blk.vector
        def _(e):
            dep = mkdep(e, st.vC)
            e.wait_ge(st.tMM.h, bc_done)
            for k in range(KD):
                for tt in range(4):
                    sl = vb[:, k * T + tt * 512:k * T + (tt + 1) * 512]
                    dep(e.tensor_sub(sl, sl, ps[2 + tt][:, :]))
            e.sem_inc(st.vC.h, 1)
            st.vC.inc()
        d_done = st.vC.v

        ln_mm2 = lambda k: bc_done + k + 1

        @blk.scalar
        def _(e):
            for k in range(KD):
                e.wait_ge(st.vC.h, d_done)
                if k >= 2:
                    e.wait_ge(st.tMM.h, ln_mm2(k - 2))
                e.activation(out=scratch[:, (k % 2) * T:(k % 2 + 1) * T],
                             in_=vb[:, k * T:(k + 1) * T],
                             func=AF.Square).then_inc(st.sC.h, 1)
                st.sC.inc()

        @blk.tensor
        def _(e):
            for k in range(KD):
                e.wait_ge(st.sC.h, prev_s + k + 1)
                for tt in range(4):
                    inst = e.matmul(ps[2 + tt][0:1, :],
                                    st.ones_col[:, :],
                                    scratch[:, (k % 2) * T + tt * 512:(k % 2) * T + (tt + 1) * 512],
                                    start=(k == 0), stop=(k == KD - 1))
                inst.then_inc(st.tMM.h, 1)
                assert st.tMM.inc() == ln_mm2(k)
        sq_done = st.tMM.v

        @blk.vector
        def _(e):
            dep = mkdep(e, st.vC)
            e.wait_ge(st.tMM.h, sq_done)
            for tt in range(4):
                dep(e.tensor_scalar(strow[0:1, tt * 512:(tt + 1) * 512],
                                    ps[2 + tt][0:1, :], 1.0 / DM, 1e-5,
                                    OP.mult, OP.add))
            e.sem_inc(st.vC.h, 1)
            st.vC.inc()
        var_done = st.vC.v

        @blk.scalar
        def _(e):
            e.wait_ge(st.vC.h, var_done)
            e.activation(out=strow[0:1, :], in_=strow[0:1, :],
                         func=AF.Sqrt).then_inc(st.sC.h, 1)
            st.sC.inc()
        sq2_done = st.sC.v

        @blk.vector
        def _(e):
            dep = mkdep(e, st.vC)
            e.wait_ge(st.sC.h, sq2_done)
            dep(e.reciprocal(strow[0:1, :], strow[0:1, :]))
            dep(e.tensor_copy(invbf[0:1, :], strow[0:1, :]))
            e.sem_inc(st.vC.h, 1)
            st.vC.inc()
        rs_done = st.vC.v

        @blk.tensor
        def _(e):
            e.wait_ge(st.vC.h, rs_done)
            for tt in range(4):
                inst = e.matmul(ps[2 + tt][:, :], st.ones_row[:, :],
                                invbf[0:1, tt * 512:(tt + 1) * 512],
                                start=True, stop=True)
            inst.then_inc(st.tMM.h, 1)
            st.tMM.inc()
        bc2_done = st.tMM.v

        st.ln_out_done = {}
        if out_dram is not None:
            st.ln_otok = {(k, hf): st.chOut.token(st.chOut.i + 2 * k + hf)
                          for k in range(KD) for hf in range(2)}

        @blk.vector
        def _(e):
            dep = mkdep(e, st.vC)
            e.wait_ge(st.tMM.h, bc2_done)
            for k in range(KD):
                for tt in range(4):
                    sl = vb[:, k * T + tt * 512:k * T + (tt + 1) * 512]
                    if out_dram is None:
                        dep(e.tensor_mul(sl, sl, ps[2 + tt][:, :]))
                    else:
                        hf = tt // 2
                        if tt % 2 == 0 and (k, hf) != (0, 0):
                            pk, phf = (k, 0) if hf == 1 else (k - 1, 1)
                            sem, val = st.ln_otok[(pk, phf)]
                            e.wait_ge(sem, val)
                        dep(e.tensor_tensor(
                            obuf[:, (tt % 2) * 512:(tt % 2 + 1) * 512],
                            sl, ps[2 + tt][:, :], OP.mult))
                        if tt % 2 == 1:
                            st.ln_out_done[(k, hf)] = st.vC.v

        if out_dram is not None:
            @blk.sync
            def _(e):
                for k in range(KD):
                    for hf in range(2):
                        e.wait_ge(st.vC.h, st.ln_out_done[(k, hf)])
                        tk = st.chOut.issue(
                            e, out_dram[k * 128:(k + 1) * 128,
                                        hf * 1024:(hf + 1) * 1024],
                            obuf[:, :])
                        assert tk == st.ln_otok[(k, hf)]


def p8_ffn(st):
    nc = st.nc
    st.p7_ctx.close()
    p8c = ExitStack()
    wselb = st.sbufr(p8c, "wselbP8", [128, KD * NE], BF16)
    lg = st.sbufr(p8c, "lgP8", [128, 64], F32)
    selb = st.sbufr(p8c, "selbP8", [128, 64], BF16)
    t8 = st.sbufr(p8c, "t8P8", [128, 8], F32)
    thr = st.sbufr(p8c, "thrP8", [128, 1], F32)
    msk = st.sbufr(p8c, "mskP8", [128, 64], F32)
    gtile = st.sbufr(p8c, "gtileP8", [128, 64], F32)
    gatesT = st.sbufr(p8c, "gatesTP8", [64, T], BF16)
    wkb = st.sbufr(p8c, "wkbP8", [128, 2 * KD * 2048], BF16)
    wvb = st.sbufr(p8c, "wvbP8", [128, 16 * 1024], BF16)
    scob = st.sbufr(p8c, "scobP8", [128, 16 * 512], BF16)
    gp = st.sbufr(p8c, "gpP8", [128, 6 * 512], BF16)
    v2acc = st.sbufr(p8c, "v2accP8", [128, KD * T], BF16)
    obuf = st.sbufr(p8c, "obufP8", [128, 1024], F32)
    strw = st.sbufr(p8c, "strwP8", [1, T], F32)
    invb = st.sbufr(p8c, "invbP8", [1, T], BF16)
    ps = st.ps

    xf_ready = st.vC.v
    v8 = st.vC.v
    s8 = st.sC.v
    st.p8_lg_ev = {i: v8 + 6 * i + 1 for i in range(16)}
    st.p8_gt_ev = {i: v8 + 6 * i + 5 for i in range(16)}
    with nc.Block() as blk:
        seltok = []

        @blk.gpsimd
        def _(e):
            for k in range(KD):
                seltok.append(st.chW.issue(
                    e, wselb[:, k * NE:(k + 1) * NE],
                    st.wsel[k * 128:(k + 1) * 128, :]))

        gmm = {}
        gtr = {}

        @blk.tensor
        def _(e):
            wait_tokens(e, seltok)
            e.wait_ge(st.vC.h, xf_ready)
            for i in range(16):
                if i >= 1:
                    e.wait_ge(st.vC.h, st.p8_lg_ev[i - 1])
                for k in range(KD):
                    inst = e.matmul(ps[0][:, 0:NE],
                                    st.xfbuf[:, k * T + i * 128:k * T + i * 128 + 128],
                                    wselb[:, k * NE:(k + 1) * NE],
                                    start=(k == 0), stop=(k == KD - 1))
                inst.then_inc(st.tMM.h, 1)
                gmm[i] = st.tMM.inc()
                e.wait_ge(st.vC.h, st.p8_gt_ev[i])
                inst = e.transpose(ps[1][0:64, 0:128], gtile[:, :], st.io1[:, :])
                inst.then_inc(st.tMM.h, 1)
                gtr[i] = st.tMM.inc()

        @blk.vector
        def _(e):
            dep = mkdep(e, st.vC)
            for i in range(16):
                e.wait_ge(st.tMM.h, gmm[i])
                dep(e.tensor_copy(lg[:, :], ps[0][:, 0:NE]))
                assert st.vC.v == st.p8_lg_ev[i]
                dep(e.max(out=t8[:, :], in_=lg[:, :]))
                dep(e.tensor_copy(thr[:, :], t8[:, 3:4]))
                dep(e.tensor_scalar(msk[:, :], lg[:, :], thr[:, 0:1], None, OP.is_ge))
                e.wait_ge(st.sC.h, s8 + i + 1)
                dep(e.tensor_mul(gtile[:, :], msk[:, :], selb[:, :]))
                assert st.vC.v == st.p8_gt_ev[i]
                e.wait_ge(st.tMM.h, gtr[i])
                dep(e.tensor_copy(gatesT[:, i * 128:(i + 1) * 128],
                                  ps[1][0:64, 0:128]))

        @blk.scalar
        def _(e):
            for i in range(16):
                e.wait_ge(st.vC.h, st.p8_lg_ev[i])
                e.activation(out=selb[:, :], in_=lg[:, :],
                             func=AF.Sigmoid).then_inc(st.sC.h, 1)
                st.sC.inc()
    gates_done = st.vC.v
    st.p8_gtcp = {i: v8 + 6 * i + 6 for i in range(16)}

    t8_base = st.tMM.v
    PER_TILE = 16 + 8

    def hmm_val(mblk, tile, mi):
        return t8_base + (mblk * 4 + tile) * PER_TILE + mi + 1

    def omm_val(mblk, tile, v):
        return t8_base + (mblk * 4 + tile) * PER_TILE + 16 + v + 1
    st.p8_blk_mm = {mblk: t8_base + (mblk * 4 + 4) * PER_TILE
                    for mblk in range(4)}
    v8b = st.vC.v

    def relu_ev(mblk, tile, mi):
        p = mi + 1 if mi < 6 else 8 + 2 * (mi - 6)
        return v8b + (mblk * 4 + tile) * 48 + p

    def sco_ev(mblk, tile, mi):
        p = 7 + 2 * mi if mi <= 9 else 27 + (mi - 10)
        return v8b + (mblk * 4 + tile) * 48 + p

    def acc_ev(mblk, tile, v):
        return v8b + (mblk * 4 + tile) * 48 + 32 + 2 * v + 2
    gre_base = st.chRe.i
    gg_base = st.chG.i

    def gp_in(mblk, tile, mi):
        m = mblk * 16 + mi
        return bass.AP(gatesT, m * T + tile * 512, [[T, 1], [0, 128], [1, 512]])

    def gp_tok(mblk, tile, mi):
        i = (mblk * 4 + tile) * 16 + mi
        if i % 2 == 0:
            return st.chRe.token(gre_base + i // 2)
        return st.chG.token(gg_base + i // 2)

    with nc.Block() as blk:
        wktok = {}
        wvtok = {}

        @blk.sync
        def _(e):
            for mblk in range(4):
                for tile in range(4):
                    for mi in range(16):
                        i = (mblk * 4 + tile) * 16 + mi
                        if i % 2 != 0:
                            continue
                        if mi == 0:
                            e.wait_ge(st.vC.h, st.p8_gtcp[tile * 4 + 3])
                        if i >= 6:
                            pi = i - 6
                            e.wait_ge(st.vC.h,
                                      sco_ev(pi // 64, (pi // 16) % 4, pi % 16))
                        tk = st.chRe.issue(
                            e, gp[:, (i % 6) * 512:(i % 6 + 1) * 512],
                            gp_in(mblk, tile, mi))
                        assert tk == gp_tok(mblk, tile, mi)

        @blk.gpsimd
        def _(e):
            for mblk in range(4):
                if mblk >= 2:
                    e.wait_ge(st.tMM.h, hmm_val(mblk - 2, 3, 15))
                h = (mblk % 2) * KD * 2048
                toks = []
                for k in range(KD):
                    toks.append(st.chW.issue(
                        e, wkb[:, h + k * 2048:h + (k + 1) * 2048],
                        st.wkeys[k * 128:(k + 1) * 128,
                                 mblk * 2048:(mblk + 1) * 2048]))
                wktok[mblk] = toks
                if mblk >= 1:
                    e.wait_ge(st.tMM.h, st.p8_blk_mm[mblk - 1])
                toks = []
                for mi in range(16):
                    m = mblk * 16 + mi
                    toks.append(st.chW.issue(
                        e, wvb[:, mi * 1024:(mi + 1) * 1024],
                        st.wvals[m * 128:(m + 1) * 128, :]))
                wvtok[mblk] = toks

        @blk.tensor
        def _(e):
            for mblk in range(4):
                h = (mblk % 2) * KD * 2048
                h2 = 0
                wait_tokens(e, wktok[mblk])
                for tile in range(4):
                    ti = mblk * 4 + tile
                    if ti > 0:
                        e.wait_ge(st.vC.h,
                                  acc_ev((ti - 1) // 4, (ti - 1) % 4, 7))
                    for mi in range(16):
                        i = (mblk * 4 + tile) * 16 + mi
                        if i >= 6:
                            pi = i - 6
                            e.wait_ge(st.vC.h,
                                      relu_ev(pi // 64, (pi // 16) % 4, pi % 16))
                        for k in range(KD):
                            inst = e.matmul(
                                ps[i % 6][:, :],
                                wkb[:, h + k * 2048 + mi * 128:h + k * 2048 + (mi + 1) * 128],
                                st.xfbuf[:, k * T + tile * 512:k * T + (tile + 1) * 512],
                                start=(k == 0), stop=(k == KD - 1))
                        inst.then_inc(st.tMM.h, 1)
                        st.tMM.inc()
                    if tile == 0:
                        wait_tokens(e, wvtok[mblk])
                    e.wait_ge(st.vC.h, sco_ev(mblk, tile, 15))
                    for v in range(8):
                        bA, bB = (6, 7) if v % 2 == 0 else (0, 1)
                        j = (mblk * 4 + tile) * 8 + v
                        if j >= 2:
                            pj = j - 2
                            e.wait_ge(st.vC.h,
                                      acc_ev(pj // 32, (pj // 8) % 4, pj % 8))
                        for mi in range(16):
                            bnk = bA if mi < 8 else bB
                            inst = e.matmul(
                                ps[bnk][:, :],
                                wvb[:, h2 + mi * 1024 + v * 128:h2 + mi * 1024 + (v + 1) * 128],
                                scob[:, mi * 512:(mi + 1) * 512],
                                start=(mi % 8 == 0), stop=(mi % 8 == 7))
                        inst.then_inc(st.tMM.h, 1)
                        st.tMM.inc()

        @blk.scalar
        def _(e):
            for i in range(1, 256, 2):
                mb, tl, m_ = i // 64, (i // 16) % 4, i % 16
                if m_ == 1:
                    e.wait_ge(st.vC.h, st.p8_gtcp[tl * 4 + 3])
                if i >= 6:
                    pi = i - 6
                    e.wait_ge(st.vC.h,
                              sco_ev(pi // 64, (pi // 16) % 4, pi % 16))
                tk = st.chG.issue(e, gp[:, (i % 6) * 512:(i % 6 + 1) * 512],
                                  gp_in(mb, tl, m_))
                assert tk == gp_tok(mb, tl, m_)

        @blk.vector
        def _(e):
            hw = [st.vC.v]

            def op(mk, deps=()):
                dv = max(deps) if deps else 0
                if dv > hw[0]:
                    e.wait_ge(st.vC.h, dv)
                    hw[0] = dv
                mk().then_inc(st.vC.h, 1)
                st.vC.inc()
                return st.vC.v

            dum = [0]
            for mblk in range(4):
                for tile in range(4):
                    def relu(mi):
                        i3 = (mblk * 4 + tile) * 16 + mi
                        e.wait_ge(st.tMM.h, hmm_val(mblk, tile, mi))
                        op(lambda: e.tensor_scalar(
                            scob[:, mi * 512:(mi + 1) * 512],
                            ps[i3 % 6][:, :], 0.0, None, OP.max))
                        assert st.vC.v == relu_ev(mblk, tile, mi)

                    def sco(mi):
                        i3 = (mblk * 4 + tile) * 16 + mi
                        wait_tokens(e, [gp_tok(mblk, tile, mi)])
                        op(lambda: e.tensor_tensor(
                            scob[:, mi * 512:(mi + 1) * 512],
                            scob[:, mi * 512:(mi + 1) * 512],
                            gp[:, (i3 % 6) * 512:(i3 % 6 + 1) * 512],
                            OP.mult), [relu_ev(mblk, tile, mi)])
                        assert st.vC.v == sco_ev(mblk, tile, mi)

                    for mi in range(6):
                        relu(mi)
                    for j in range(10):
                        sco(j)
                        relu(j + 6)
                    for j in range(10, 16):
                        sco(j)
                    for v in range(8):
                        bA, bB = (6, 7) if v % 2 == 0 else (0, 1)
                        e.wait_ge(st.tMM.h, omm_val(mblk, tile, v))
                        dst = v2acc[:, v * T + tile * 512:v * T + (tile + 1) * 512]
                        if mblk == 0:
                            a1 = op(lambda dst=dst, bA=bA: e.tensor_copy(
                                dst, ps[bA][:, :]))
                            op(lambda dst=dst, bB=bB: e.tensor_tensor(
                                dst, dst, ps[bB][:, :], OP.add), [a1])
                        else:
                            a1 = op(lambda dst=dst, bA=bA: e.tensor_tensor(
                                dst, dst, ps[bA][:, :], OP.add),
                                [acc_ev(mblk - 1, tile, v)])
                            op(lambda dst=dst, bB=bB: e.tensor_tensor(
                                dst, dst, ps[bB][:, :], OP.add), [a1])
                        assert st.vC.v == acc_ev(mblk, tile, v)

    with nc.Block() as blk:
        @blk.vector
        def _(e):
            dep = mkdep(e, st.vC)
            for k in range(KD):
                for tt in range(4):
                    sl = v2acc[:, k * T + tt * 512:k * T + (tt + 1) * 512]
                    dep(e.tensor_add(sl, sl,
                                     st.xfbuf[:, k * T + tt * 512:k * T + (tt + 1) * 512]))
            e.sem_inc(st.vC.h, 1)
            st.vC.inc()
    _ln_norm(st, v2acc, st.xfbuf, strw, invb, out_dram=st.out_p, obuf=obuf)
    st.p8_ctx = p8c


def final_waits(st):
    nc = st.nc
    with nc.Block() as blk:
        @blk.sync
        def _(e):
            for ch in [st.chOut, st.chSp, st.chRe, st.chIn, st.chW]:
                ch.wait_all(e)


def _prep_inputs(inp):
    f = lambda k: np.ascontiguousarray(np.asarray(inp[k], np.float32))
    W = f("L0_attn_in_proj_W")
    bW = f("L0_attn_in_proj_b")
    rows = np.concatenate([PERM, DI + 2 * PERM, DI + 2 * PERM + 1, 3 * DI + PERM])
    wproj = np.ascontiguousarray(W[rows].T)
    bproj = np.ascontiguousarray(bW[rows].reshape(64, 128).T)
    bproj = np.ascontiguousarray(bW[rows].reshape(64, 128).T)

    Qw = f("L0_attn_Q_W")
    qrows = np.concatenate([2 * PERM, 2 * PERM + 1])
    wq = np.ascontiguousarray(Qw[np.ix_(qrows, PERM)].T)

    dyn = f("L0_attn_dyn_W")
    sm_rows = [dyn[0:32], dyn[32:64], dyn[64:96], f("L0_attn_seldt_W"),
               f("L0_attn_beta_W"), f("L0_attn_rg_W"), f("L0_attn_ug_W"),
               f("L0_attn_selB_W")[0::2], f("L0_attn_selB_W")[1::2],
               f("L0_attn_selC_W")[0::2], f("L0_attn_selC_W")[1::2],
               np.zeros((32, DI), np.float32)]
    wsm = np.concatenate(sm_rows, 0)[:, PERM]
    wsm = np.ascontiguousarray(wsm.T)

    wout = np.ascontiguousarray(f("L0_attn_out_W")[:, PERM].T)
    wsel = np.ascontiguousarray(f("L0_ffn_expert_sel").T)
    wkeys = np.ascontiguousarray(f("L0_ffn_keys").transpose(1, 0, 2).reshape(DM, NE * ES))
    wvals = np.ascontiguousarray(f("L0_ffn_values").reshape(NE * ES, DM))

    def tile_di(v):
        return np.ascontiguousarray(v[PERM].reshape(KI, 128).T)

    convw = f("L0_attn_conv_w")[PERM]
    convw_dev = np.ascontiguousarray(
        convw.reshape(KI, 128, 4).transpose(1, 0, 2).reshape(128, KI * 4))
    convb_dev = tile_di(f("L0_attn_conv_b"))
    d_dev = tile_di(f("L0_attn_D"))
    gnw_dev = tile_di(f("L0_attn_gn_w"))
    gnb_dev = tile_di(f("L0_attn_gn_b"))
    rmsw_dev = np.ascontiguousarray(f("L0_attn_rms_w").reshape(KD, 128).T)
    dynb = f("L0_attn_dyn_b")
    bsm = np.stack([dynb[0:32], dynb[32:64], dynb[64:96],
                    f("L0_attn_beta_b"), f("L0_attn_rg_b"), f("L0_attn_ug_b"),
                    np.zeros(32, np.float32), np.zeros(32, np.float32)], 1)
    bsm_dev = np.ascontiguousarray(np.tile(bsm, (4, 1)))
    dtc_dev = np.ascontiguousarray(np.tile(f("L0_attn_dt_c"), 4)[:, None])

    return dict(wproj=wproj, bproj=bproj, wq=wq, wsm=wsm, wout=wout,
                wsel=wsel, wkeys=wkeys, wvals=wvals, convw=convw_dev,
                convb=convb_dev, d_skip=d_dev, gnw=gnw_dev, gnb=gnb_dev,
                rmsw=rmsw_dev, bsm=bsm_dev, dtc=dtc_dev)


def _core_inputs(inp, shared, cid):
    x = np.asarray(inp["L0_attn_x"], np.float32)
    b, q = cid // 4, cid % 4
    t0 = q * T
    xs = x[b, t0 + TOKPERM, :]
    x_fm = np.ascontiguousarray(xs.T)
    halo = np.zeros((DM, 4), np.float32)
    if q > 0:
        halo[:, 1:4] = x[b, t0 - 3:t0, :].T
    meta = np.zeros((128, 12), np.float32)
    meta[:, 0] = float(t0)
    meta[:, 1] = 1.0 if q > 0 else 0.0
    for j in range(4):
        meta[:, 2 + j] = 1.0 if j < q else 0.0
        meta[:, 6 + j] = 0.0 if j < q else 1.0
    d = dict(shared)
    d["x_fm"] = x_fm
    d["x_halo"] = np.ascontiguousarray(halo)
    d["meta"] = np.ascontiguousarray(meta)
    return d


def kernel(**inputs):
    nc, st = build_program()
    shared = _prep_inputs(inputs)
    in_maps = [_core_inputs(inputs, shared, cid) for cid in range(8)]
    res = run_bass_kernel_spmd(nc, in_maps, core_ids=list(range(8)))
    out = np.zeros((B, S, DM), np.float32)
    for cid in range(8):
        b, q = cid // 4, cid % 4
        out[b, q * T + TOKPERM, :] = res.results[cid]["out_p"].T
    return out
